# version 02
# baseline (speedup 1.0000x reference)
import contextlib
import numpy as np
import ml_dtypes
import concourse.bass as bass
import concourse.mybir as mybir
from concourse.bass_utils import run_bass_kernel_spmd

F32 = mybir.dt.float32
BF16 = mybir.dt.bfloat16
AF = mybir.ActivationFunctionType
ALU = mybir.AluOpType
AXX = mybir.AxisListType.X
BF = ml_dtypes.bfloat16

L = 4096
D = 1024
KD = 8
DIN = 2048
KDI = 16
NST = 16
DTR = 64
HM = 4096
KHM = 32
NH = 8
DH = 64
INNER = 512
KIN = 4
WIN = 512
HID = 2730
HIDP = 2816
KHID = 22
NL = 2

TCM = 256
NCM = L // TCM
TCF = 512
NCF = L // TCF

NEG = -1.0e38
EPS = 1e-5

_DBG = None
_PHASES = None
_FA_STAGE = 9
_FA_SOFT = 9
_FA_SIM = 9
_NCF_LIM = None
_FA_NT = None
_FA_DUMMY = 0

ENGS = ("sync", "gpsimd", "tensor", "vector", "scalar")


class Prog:
    def __init__(self, nc):
        self.nc = nc
        self.sems = {}
        self.cnt = {}
        self.seen = {e: {} for e in ENGS}
        self.ops = {e: [] for e in ENGS}

    def sem(self, name):
        self.sems[name] = self.nc.alloc_semaphore(name)
        self.cnt[name] = 0

    def cur(self, name):
        return self.cnt[name]

    def emit(self, eng, fn, wait=(), inc=()):
        w = tuple((s, self.cnt[s] if v is None else v) for (s, v) in wait)
        self.ops[eng].append((w, fn, tuple(inc)))
        for s, a in inc:
            self.cnt[s] += a

    def dma(self, q, sem, out, in_, wait=()):
        self.emit(q, (lambda e, o=out, i=in_: e.dma_start(out=o, in_=i)),
                  wait, ((sem, 16),))

    def fence(self):
        w = tuple((s, self.cnt[s]) for s in self.cnt)
        for eng in ENGS:
            self.emit(eng, lambda e: e.nop(), wait=w)

    def flush_phase(self):
        ops = self.ops
        self.ops = {e: [] for e in ENGS}
        sems = self.sems
        seen = self.seen
        with self.nc.Block() as blk:
            for eng in ENGS:
                lst = ops[eng]
                if not lst:
                    continue

                def runner(e, lst=lst, eng=eng):
                    sv = seen[eng]
                    for w, fn, incs in lst:
                        for s, v in w:
                            if v > 0 and sv.get(s, 0) < v:
                                e.wait_ge(sems[s], v)
                                sv[s] = v
                        r = fn(e)
                        for s, a in incs:
                            r.then_inc(sems[s], a)
                getattr(blk, eng)(runner)


_CONST_CACHE = {}


def _consts():
    if _CONST_CACHE:
        return _CONST_CACHE
    c = {}
    c["ident"] = np.eye(128).astype(BF)
    c["ones1"] = np.ones((128, 1), dtype=BF)
    c["onesf"] = np.ones((1, 128), dtype=np.float32)
    g8 = np.zeros((4, 8, 128), np.float32)
    for j in range(4):
        g8[j, 2 * j, 0:64] = 1.0
        g8[j, 2 * j + 1, 64:128] = 1.0
    c["g8"] = g8
    e8 = np.zeros((4, 128, 8), np.float32)
    for k in range(4):
        e8[k, 0:64, 2 * k] = 1.0
        e8[k, 64:128, 2 * k + 1] = 1.0
    c["e8"] = e8.astype(BF)
    i_ = np.arange(128)[:, None]
    j_ = np.arange(128)[None, :]
    m_own = np.where(j_ > i_, NEG, 0.0).astype(np.float32)
    m_prev = np.where(j_ < i_, NEG, 0.0).astype(np.float32)
    c["maskownT"] = np.ascontiguousarray(m_own.T).astype(BF)
    c["maskprevT"] = np.ascontiguousarray(m_prev.T).astype(BF)
    R = np.zeros((128, 128), np.float32)
    for d in range(128):
        dl = d % 64
        base = d - dl
        if dl < 32:
            R[d, base + dl + 32] = -1.0
        else:
            R[d, base + dl - 32] = 1.0
    c["rotmT"] = np.ascontiguousarray(R.T).astype(BF)
    half = np.arange(0, DH, 2, dtype=np.float64)
    invf = 1.0 / (10000.0 ** (half / DH))
    t = np.arange(L, dtype=np.float64)
    fr = t[:, None] * invf[None, :]
    freqs = np.concatenate([fr, fr], axis=1)
    base = (half + 0.4 * DH) / (1.4 * DH)
    power = (t - L // 2) / float(WIN // 2)
    scale = base[None, :] ** power[:, None]
    scale = np.concatenate([scale, scale], axis=1)
    cos = np.cos(freqs)
    sin = np.sin(freqs)

    def two(x):
        xt = x.T
        return np.ascontiguousarray(
            np.concatenate([xt, xt], axis=0)).astype(np.float32)
    c["csq0"] = two(cos * scale)
    c["ssq0"] = two(sin * scale)
    c["csk0"] = two(cos / scale)
    c["ssk0"] = two(sin / scale)
    _CONST_CACHE.update(c)
    return c


def _stage_inputs(inputs):
    A = np.ascontiguousarray
    g = lambda k: np.asarray(inputs[k])
    s = {}
    s["inw"] = A(g("L0_attn_in_proj_w").T)
    s["convw"] = A(g("L0_attn_conv_w"))
    s["convb"] = A(g("L0_attn_conv_b"))
    s["xpw"] = A(g("L0_attn_x_proj_w").T)
    s["dtw"] = A(g("L0_attn_dt_proj_w").T)
    s["dtb"] = A(g("L0_attn_dt_proj_b"))
    s["alog"] = A(g("L0_attn_A_log"))
    s["dp"] = A(g("L0_attn_Dp"))
    s["outw"] = A(g("L0_attn_out_proj_w").T)
    s["w1m"] = A(g("L0_attn_w1").T)
    s["b1m"] = A(g("L0_attn_b1"))
    s["w2m"] = A(g("L0_attn_w2").T)
    s["b2m"] = A(g("L0_attn_b2"))
    s["wqkv"] = A(g("L0_ffn_wqkv"))
    s["qs"] = A(g("L0_ffn_q_scale"))
    s["ks"] = A(g("L0_ffn_k_scale"))
    s["wo"] = A(g("L0_ffn_wo"))
    s["lng"] = A(g("L0_ffn_ln_g"))
    s["lnb"] = A(g("L0_ffn_ln_b"))
    w1f = g("L0_ffn_w1")
    pad = HIDP - HID
    s["w1fa"] = A(np.pad(w1f[:, :, :HID], ((0, 0), (0, 0), (0, pad))))
    s["w1fg"] = A(np.pad(w1f[:, :, HID:], ((0, 0), (0, 0), (0, pad))))
    b1f = g("L0_ffn_b1")
    s["b1fa"] = A(np.pad(b1f[:, :HID], ((0, 0), (0, pad))))
    s["b1fg"] = A(np.pad(b1f[:, HID:], ((0, 0), (0, pad))))
    s["w2f"] = A(np.pad(g("L0_ffn_w2"), ((0, 0), (0, pad), (0, 0))))
    s["b2f"] = A(g("L0_ffn_b2"))
    s.update(_consts())
    return s


def _rk(ap, p=128):
    return ap.rearrange("(k p) t -> p k t", p=p)


def _on(name):
    return _PHASES is None or name in _PHASES


def build_nc():
    nc = bass.Bass("TRN2", target_bir_lowering=False, debug=False,
                   detect_race_conditions=False)
    nc.ctx.enter_context(
        nc.allow_non_contiguous_dma(reason="small vector/stat loads"))
    dt_ = nc.dram_tensor

    xT = dt_("xT", [D, L], F32, kind="ExternalInput").ap()
    inw = dt_("inw", [D, HM], F32, kind="ExternalInput").ap()
    convw = dt_("convw", [DIN, 3], F32, kind="ExternalInput").ap()
    convb = dt_("convb", [DIN], F32, kind="ExternalInput").ap()
    xpw = dt_("xpw", [DIN, 96], F32, kind="ExternalInput").ap()
    dtw = dt_("dtw", [DTR, DIN], F32, kind="ExternalInput").ap()
    dtb = dt_("dtb", [DIN], F32, kind="ExternalInput").ap()
    alog = dt_("alog", [DIN, NST], F32, kind="ExternalInput").ap()
    dpv = dt_("dp", [DIN], F32, kind="ExternalInput").ap()
    outw = dt_("outw", [DIN, D], F32, kind="ExternalInput").ap()
    w1m = dt_("w1m", [D, HM], F32, kind="ExternalInput").ap()
    b1m = dt_("b1m", [HM], F32, kind="ExternalInput").ap()
    w2m = dt_("w2m", [HM, D], F32, kind="ExternalInput").ap()
    b2m = dt_("b2m", [D], F32, kind="ExternalInput").ap()
    wqkv = dt_("wqkv", [NL, D, 3 * INNER], F32, kind="ExternalInput").ap()
    qs = dt_("qs", [NL, DH], F32, kind="ExternalInput").ap()
    ks = dt_("ks", [NL, DH], F32, kind="ExternalInput").ap()
    wo = dt_("wo", [NL, INNER, D], F32, kind="ExternalInput").ap()
    lng = dt_("lng", [NL, D], F32, kind="ExternalInput").ap()
    lnbv = dt_("lnb", [NL, D], F32, kind="ExternalInput").ap()
    w1fa = dt_("w1fa", [NL, D, HIDP], F32, kind="ExternalInput").ap()
    w1fg = dt_("w1fg", [NL, D, HIDP], F32, kind="ExternalInput").ap()
    b1fa = dt_("b1fa", [NL, HIDP], F32, kind="ExternalInput").ap()
    b1fg = dt_("b1fg", [NL, HIDP], F32, kind="ExternalInput").ap()
    w2f = dt_("w2f", [NL, HIDP, D], F32, kind="ExternalInput").ap()
    b2f = dt_("b2f", [NL, D], F32, kind="ExternalInput").ap()
    ident = dt_("ident", [128, 128], BF16, kind="ExternalInput").ap()
    ones1 = dt_("ones1", [128, 1], BF16, kind="ExternalInput").ap()
    onesf = dt_("onesf", [1, 128], F32, kind="ExternalInput").ap()
    g8c = dt_("g8", [4, 8, 128], F32, kind="ExternalInput").ap()
    e8 = dt_("e8", [4, 128, 8], BF16, kind="ExternalInput").ap()
    maskownT = dt_("maskownT", [128, 128], BF16, kind="ExternalInput").ap()
    maskprevT = dt_("maskprevT", [128, 128], BF16, kind="ExternalInput").ap()
    rotmT = dt_("rotmT", [128, 128], BF16, kind="ExternalInput").ap()
    csq0 = dt_("csq0", [128, L], F32, kind="ExternalInput").ap()
    ssq0 = dt_("ssq0", [128, L], F32, kind="ExternalInput").ap()
    csk0 = dt_("csk0", [128, L], F32, kind="ExternalInput").ap()
    ssk0 = dt_("ssk0", [128, L], F32, kind="ExternalInput").ap()

    outT = dt_("outT", [D, L], F32, kind="ExternalOutput").ap()

    s_inwb = dt_("s_inwb", [D, HM], BF16).ap()
    s_outwb = dt_("s_outwb", [DIN, D], BF16).ap()
    s_w1mb = dt_("s_w1mb", [D, HM], BF16).ap()
    s_w2mb = dt_("s_w2mb", [HM, D], BF16).ap()
    s_w1fab = dt_("s_w1fab", [NL, D, HIDP], BF16).ap()
    s_w1fgb = dt_("s_w1fgb", [NL, D, HIDP], BF16).ap()
    s_w2fb = dt_("s_w2fb", [NL, HIDP, D], BF16).ap()
    s_ln1 = dt_("s_ln1", [D, L], F32).ap()
    s_ygb = dt_("s_ygb", [DIN, L], BF16).ap()
    s_x2 = dt_("s_x2", [D, L], F32).ap()
    s_xf0 = dt_("s_xf0", [D, L], F32).ap()
    s_xm = [dt_("s_xm0", [D, L], F32).ap(), dt_("s_xm1", [D, L], F32).ap()]
    s_xl = [dt_("s_xl1", [D, L], F32).ap(), dt_("s_xl2", [D, L], F32).ap()]
    s_dbc = dt_("s_dbc", [32, TCM], F32).ap()

    dbg = None
    if _DBG is not None:
        dbg = dt_("dbg", [DIN, L], F32, kind="ExternalOutput").ap()

    p = Prog(nc)
    for s in ("mm", "vv", "ss", "out", "ax", "sp",
              "swiA", "swiB", "swoA", "swoB", "sw1A", "sw1B", "sw2A", "sw2B",
              "swaA", "swaB", "swgA", "swgB", "swfA", "swfB",
              "sbA", "sbB", "gpc", "gbA", "gbB", "gbc", "gxf", "gw"):
        p.sem(s)

    sb = nc.sbuf_tensor
    pst = nc.psum_tensor
    perstack = contextlib.ExitStack()
    E = perstack.enter_context
    identt = E(sb("identt", [128, 128], BF16))
    onest = E(sb("onest", [128, 1], BF16))
    onesft = E(sb("onesft", [1, 128], F32))
    g8t = E(sb("g8t", [8, 4, 128], F32))
    e8t = E(sb("e8t", [128, 4, 8], BF16))
    maskot = E(sb("maskot", [128, 128], BF16))
    maskpt = E(sb("maskpt", [128, 128], BF16))
    rotmt = E(sb("rotmt", [128, 128], BF16))
    cwt = E(sb("cwt", [128, KDI, 3], F32))
    cbt = E(sb("cbt", [128, KDI], F32))
    dtbt = E(sb("dtbt", [128, KDI], F32))
    dpt = E(sb("dpt", [128, KDI], F32))
    negP = E(sb("negP", [128, KDI, NST], F32))
    carry = E(sb("carry", [128, KDI, NST], F32))
    carry2 = E(sb("carry2", [128, KDI, 2], F32))
    b1mv = E(sb("b1mv", [128, KHM], F32))
    b2mv = E(sb("b2mv", [128, KD], F32))
    lngv = E(sb("lngv", [128, NL, KD], F32))
    lnbt = E(sb("lnbt", [128, NL, KD], F32))
    b1fav = E(sb("b1fav", [128, NL, KHID], F32))
    b1fgv = E(sb("b1fgv", [128, NL, KHID], F32))
    b2fv = E(sb("b2fv", [128, NL, KD], F32))
    qs2 = E(sb("qs2", [128, NL], F32))
    qs2r = E(sb("qs2r", [128, NL], F32))
    ks2 = E(sb("ks2", [128, NL], F32))
    ks2r = E(sb("ks2r", [128, NL], F32))
    epsc = E(sb("epsc", [128, 1], F32))
    onec = E(sb("onec", [128, 1], F32))
    xpwb = E(sb("xpwb", [128, KDI, 96], BF16))
    dtwb = E(sb("dtwb", [64, KDI, 128], BF16))

    def ln_stats(src, psrA, psrB, rows, cast16, T, psr_guard,
                 wait_extra=(), shift=None, state=None):
        castg = state.get("cast") if state is not None else None
        rowsg = state.get("rows") if state is not None else None
        p.emit("scalar", lambda e: e.copy(out=cast16[:], in_=src[:]),
               wait=wait_extra + ((castg,) if castg else ()), inc=(("ss", 1),))
        scast = p.cur("ss")

        def rhs(buf, ct):
            if shift is None:
                return buf[:, ct, 0:T]
            return buf[:, ct, 1:T + 1] if ct < KD // 2 else buf[:, ct, 0:T]

        for ct in range(KD):
            p.emit("tensor",
                   lambda e, ct=ct: e.matmul(
                       psrA[0:1, 0:T], onest[:], rhs(cast16, ct),
                       start=(ct == 0), stop=(ct == KD - 1)),
                   wait=((psr_guard[0], ("ss", scast)) if psr_guard[0]
                         else (("ss", scast),)) if ct == 0 else (),
                   inc=((("mm", 1),) if ct == KD - 1 else ()))
        amm = p.cur("mm")
        p.emit("scalar", lambda e: e.activation(cast16[:], src[:], AF.Square),
               wait=(("mm", amm),), inc=(("ss", 1),))
        scast = p.cur("ss")
        for ct in range(KD):
            p.emit("tensor",
                   lambda e, ct=ct: e.matmul(
                       psrB[0:1, 0:T], onest[:], rhs(cast16, ct),
                       start=(ct == 0), stop=(ct == KD - 1)),
                   wait=(("ss", scast),) if ct == 0 else (),
                   inc=((("mm", 1),) if ct == KD - 1 else ()))
        smm = p.cur("mm")
        p.emit("vector", lambda e: e.tensor_scalar(
            rows[:, 0, :], psrA[0:1, 0:T], 1.0 / D, None, ALU.mult),
            wait=(("mm", smm),) + ((rowsg,) if rowsg else ()), inc=(("vv", 1),))
        p.emit("vector", lambda e: e.tensor_scalar(
            rows[:, 1, :], psrB[0:1, 0:T], 1.0 / D, None, ALU.mult),
            inc=(("vv", 1),))
        p.emit("vector", lambda e: e.tensor_tensor(
            rows[:, 2, :], rows[:, 0, :], rows[:, 0, :], ALU.mult),
            inc=(("vv", 1),))
        p.emit("vector", lambda e: e.tensor_tensor(
            rows[:, 3, :], rows[:, 1, :], rows[:, 2, :], ALU.subtract),
            inc=(("vv", 1),))
        svv = p.cur("vv")
        p.emit("scalar", lambda e: e.activation(
            rows[:, 4, :], rows[:, 3, :], AF.Sqrt, bias=epsc[0:1, :]),
            wait=(("vv", svv),), inc=(("ss", 1),))
        p.emit("vector", lambda e: e.reciprocal(rows[:, 5, :], rows[:, 4, :]),
               wait=(("ss", p.cur("ss")), ("vv", None)), inc=(("vv", 1),))
        rvv = p.cur("vv")
        p.emit("tensor", lambda e: e.matmul(
            psrA[:, 0:T], onesft[:], rows[:, 0, :], start=True, stop=True),
            wait=(("vv", rvv),))
        p.emit("tensor", lambda e: e.matmul(
            psrB[:, 0:T], onesft[:], rows[:, 5, :], start=True, stop=True),
            inc=(("mm", 1),))
        repmm = p.cur("mm")
        if state is not None:
            state["cast"] = ("mm", smm)
            state["rows"] = ("mm", repmm)
            state["scast"] = ("ss", scast)
        return repmm

    with contextlib.ExitStack() as st:
        Es = st.enter_context
        alogt = Es(sb("alogt", [128, KDI, NST], F32))
        bb = [Es(sb(f"bb{i}", [128, 4096], BF16)) for i in range(2)]
        def d(q, out, in_):
            p.dma(q, "ax" if q == "sync" else "gpc", out, in_)
        d("sync", identt[:], ident)
        d("sync", onest[:], ones1)
        d("sync", onesft[:], onesf)
        d("sync", g8t[:], g8c.transpose([1, 0, 2]))
        d("sync", e8t[:], e8.transpose([1, 0, 2]))
        d("sync", maskot[:], maskownT)
        d("sync", maskpt[:], maskprevT)
        d("sync", rotmt[:], rotmT)
        d("sync", cwt[:], convw.rearrange("(k p) c -> p k c", p=128))
        d("sync", cbt[:], convb.rearrange("(k p) -> p k", p=128))
        d("sync", dtbt[:], dtb.rearrange("(k p) -> p k", p=128))
        d("sync", dpt[:], dpv.rearrange("(k p) -> p k", p=128))
        d("sync", b1mv[:], b1m.rearrange("(k p) -> p k", p=128))
        d("sync", b2mv[:], b2m.rearrange("(k p) -> p k", p=128))
        d("sync", lngv[:], lng.rearrange("l (k p) -> p l k", p=128))
        d("sync", lnbt[:], lnbv.rearrange("l (k p) -> p l k", p=128))
        d("sync", b1fav[:], b1fa.rearrange("l (k p) -> p l k", p=128))
        d("sync", b1fgv[:], b1fg.rearrange("l (k p) -> p l k", p=128))
        d("sync", b2fv[:], b2f.rearrange("l (k p) -> p l k", p=128))
        for l in range(NL):
            for h in range(2):
                d("sync", qs2[64 * h:64 * h + 64, l:l + 1], qs[l, :].unsqueeze(1))
                d("sync", ks2[64 * h:64 * h + 64, l:l + 1], ks[l, :].unsqueeze(1))
                d("sync", qs2r[64 * h:64 * h + 32, l:l + 1], qs[l, 32:64].unsqueeze(1))
                d("sync", qs2r[64 * h + 32:64 * h + 64, l:l + 1], qs[l, 0:32].unsqueeze(1))
                d("sync", ks2r[64 * h:64 * h + 32, l:l + 1], ks[l, 32:64].unsqueeze(1))
                d("sync", ks2r[64 * h + 32:64 * h + 64, l:l + 1], ks[l, 0:32].unsqueeze(1))
        d("gpsimd", xpwb[:], xpw.rearrange("(k p) o -> p k o", p=128))
        d("gpsimd", dtwb[:], dtw.rearrange("c (k p) -> c k p", p=128))
        d("sync", alogt[:], alog.rearrange("(k p) n -> p k n", p=128))
        p.emit("scalar", lambda e: e.activation(negP[:], alogt[:], AF.Exp),
               wait=(("ax", None), ("gpc", None)), inc=(("ss", 1),))
        p.emit("vector", lambda e: e.tensor_scalar(
            negP[:], negP[:], -1.0, None, ALU.mult),
            wait=(("ss", None),), inc=(("vv", 1),))
        p.emit("vector", lambda e: e.memset(epsc[:], EPS), inc=(("vv", 1),))
        p.emit("vector", lambda e: e.memset(onec[:], 1.0), inc=(("vv", 1),))
        p.emit("vector", lambda e: e.memset(carry[:], 0.0), inc=(("vv", 1),))
        p.emit("vector", lambda e: e.memset(carry2[:], 0.0), inc=(("vv", 1),))

        jobs = []

        def add_rows(src, dst, rows_, cols):
            for r0 in range(0, rows_, 128):
                jobs.append((src[r0:r0 + 128, :], dst[r0:r0 + 128, :], cols))

        add_rows(inw, s_inwb, D, HM)
        add_rows(outw, s_outwb, DIN, D)
        add_rows(w1m, s_w1mb, D, HM)
        add_rows(w2m, s_w2mb, HM, D)
        for l in range(NL):
            add_rows(w1fa[l], s_w1fab[l], D, HIDP)
            add_rows(w1fg[l], s_w1fgb[l], D, HIDP)
            add_rows(w2f[l], s_w2fb[l], HIDP, D)
        for i, (srcap, dstap, cols) in enumerate(jobs):
            b = "AB"[i % 2]
            buf = bb[i % 2]
            p.dma("gpsimd", "gb" + b, buf[:, 0:cols], srcap,
                  wait=(("sb" + b, None),))
            p.dma("sync", "sb" + b, dstap, buf[:, 0:cols],
                  wait=(("gb" + b, None),))
        p.fence()
        p.flush_phase()

    if not _on("m1"):
        ncm_eff = 0
    else:
        ncm_eff = NCM
    with contextlib.ExitStack() as st:
        Es = st.enter_context
        T = TCM
        x0c = Es(sb("x0c", [128, KD, T], F32))
        cast16 = Es(sb("cast16", [128, KD, T], BF16))
        rows = Es(sb("rows", [1, 6, T], F32))
        ln1c = Es(sb("ln1c", [128, KD, T], F32))
        lnb16 = Es(sb("lnb16", [128, KD, T], BF16))
        wbuf = [Es(sb(f"m1wb{i}", [128, KD, 256], BF16)) for i in range(2)]
        xpr = Es(sb("xpr", [128, KDI, T + 2], F32))
        szt = Es(sb("szt", [128, KDI, T], F32))
        dtt = Es(sb("dtt", [128, KDI, T], F32))
        xp2 = Es(sb("xp2", [128, KDI, T], F32))
        xp2b = Es(sb("xp2b", [128, KDI, T], BF16))
        dbc = Es(sb("dbc", [96, T], F32))
        dtinb = Es(sb("dtinb", [64, T], BF16))
        bcrep = Es(sb("bcrep", [128, 32, T], BF16))
        at = Es(sb("at", [128, NST, T], F32))
        btl = Es(sb("btl", [128, NST, T], F32))
        ht = Es(sb("ht", [128, NST, T], F32))
        tmp16w = Es(sb("tmp16w", [128, NST], F32))
        dtx = Es(sb("dtx", [128, T], F32))
        ybuf = Es(sb("ybuf", [128, T], F32))
        y2buf = Es(sb("y2buf", [128, T], F32))
        ct1 = Es(sb("ct1", [128, T], F32))
        ct2 = Es(sb("ct2", [128, T], F32))
        etmp = Es(sb("etmp", [128, T], F32))
        ygb = Es(sb("ygb", [128, KDI, T], BF16))
        ps = [Es(pst(f"m1ps{i}", [128, 512], F32)) for i in range(4)]
        psrA = Es(pst("m1psrA", [128, 512], F32))
        psrB = Es(pst("m1psrB", [128, 512], F32))
        bankg = [None] * 4
        wguard = [None, None]
        psrg = [None]

        def bw(g):
            return (g,) if g else ()

        brep = bcrep[:, 0:16, :]
        crep = bcrep[:, 16:32, :]
        lnst = {}
        gx0 = ()
        glnb = None
        ghalo = None
        gsd = None
        for c in range(ncm_eff):
            t0 = c * T
            p.dma("sync", "ax", x0c[:], _rk(xT)[:, :, t0:t0 + T], wait=gx0)
            repmm = ln_stats(x0c, psrA, psrB, rows, cast16, T, psrg,
                             wait_extra=(("ax", None),), state=lnst)
            p.emit("vector", lambda e: e.tensor_tensor(
                ln1c[:], x0c[:],
                psrA[:, 0:T].unsqueeze(1).to_broadcast([128, KD, T]),
                ALU.subtract),
                wait=(("mm", repmm), ("sp", None)), inc=(("vv", 1),))
            p.emit("vector", lambda e: e.tensor_tensor(
                ln1c[:], ln1c[:],
                psrB[:, 0:T].unsqueeze(1).to_broadcast([128, KD, T]),
                ALU.mult),
                inc=(("vv", 1),))
            lnvv = p.cur("vv")
            psrg[0] = ("vv", lnvv)
            gx0 = (lnst["scast"], ("vv", lnvv))
            p.emit("scalar", lambda e: e.copy(out=lnb16[:], in_=ln1c[:]),
                   wait=(("vv", lnvv),) + ((glnb,) if glnb else ()),
                   inc=(("ss", 1),))
            lnss = p.cur("ss")
            p.dma("sync", "sp", _rk(s_ln1)[:, :, t0:t0 + T], ln1c[:],
                  wait=(("vv", lnvv),))
            p.emit("scalar", lambda e: e.copy(out=xpr[:, :, 0:2], in_=carry2[:]),
                   wait=((ghalo,) if ghalo else ()),
                   inc=(("ss", 1),))
            for g in range(16):
                wb = wbuf[g % 2]
                wsem = "swi" + "AB"[g % 2]
                p.dma("sync", wsem, wb[:], _rk(s_inwb)[:, :, g * 256:(g + 1) * 256],
                      wait=bw(wguard[g % 2]))
                ldw = (wsem, p.cur(wsem))
                for otl in range(2):
                    ot = 2 * g + otl
                    bk = ot % 4
                    for ct in range(KD):
                        p.emit("tensor",
                               lambda e, ct=ct, wb=wb, otl=otl, bk=bk: e.matmul(
                                   ps[bk][:, 0:T],
                                   wb[:, ct, 128 * otl:128 * otl + 128],
                                   lnb16[:, ct, :],
                                   start=(ct == 0), stop=(ct == KD - 1)),
                               wait=(bw(bankg[bk]) + (ldw, ("ss", lnss))) if ct == 0 else (),
                               inc=((("mm", 1),) if ct == KD - 1 else ()))
                    mmc = p.cur("mm")
                    kk = ot if ot < KDI else ot - KDI
                    sdw = ((("vv", gsd[kk]),) if gsd is not None else ())
                    if ot < KDI:
                        p.emit("scalar",
                               lambda e, ot=ot, bk=bk: e.copy(
                                   out=xpr[:, ot, 2:2 + T], in_=ps[bk][:, 0:T]),
                               wait=(("mm", mmc),) + sdw, inc=(("ss", 1),))
                    else:
                        p.emit("scalar",
                               lambda e, ot=ot, bk=bk: e.activation(
                                   szt[:, ot - KDI, :], ps[bk][:, 0:T], AF.Silu),
                               wait=(("mm", mmc),) + sdw, inc=(("ss", 1),))
                    bankg[bk] = ("ss", p.cur("ss"))
                wguard[g % 2] = ("mm", p.cur("mm"))
            glnb = ("mm", p.cur("mm"))
            p.emit("scalar", lambda e: e.copy(out=carry2[:], in_=xpr[:, :, T:T + 2]),
                   inc=(("ss", 1),))
            for k in range(KDI):
                p.emit("scalar",
                       lambda e, k=k: e.activation(
                           ct1[:], xpr[:, k, 0:T], AF.Identity,
                           scale=cwt[:, k, 0:1]),
                       inc=(("ss", 1),))
                s1 = p.cur("ss")
                p.emit("vector",
                       lambda e, k=k: e.scalar_tensor_tensor(
                           ct2[:], xpr[:, k, 1:1 + T], cwt[:, k, 1:2], ct1[:],
                           ALU.mult, ALU.add),
                       wait=(("ss", s1),), inc=(("vv", 1),))
                p.emit("vector",
                       lambda e, k=k: e.scalar_tensor_tensor(
                           ct1[:], xpr[:, k, 2:2 + T], cwt[:, k, 2:3], ct2[:],
                           ALU.mult, ALU.add),
                       inc=(("vv", 1),))
                cvv = p.cur("vv")
                p.emit("scalar",
                       lambda e, k=k: e.activation(
                           xp2[:, k, :], ct1[:], AF.Silu, bias=cbt[:, k:k + 1]),
                       wait=(("vv", cvv),), inc=(("ss", 1),))
                p.emit("scalar",
                       lambda e, k=k: e.copy(out=xp2b[:, k, :], in_=xp2[:, k, :]),
                       inc=(("ss", 1),))
            cvss = p.cur("ss")
            ghalo = ("vv", p.cur("vv"))
            for ct in range(KDI):
                p.emit("tensor",
                       lambda e, ct=ct: e.matmul(
                           psrA[0:96, 0:T], xpwb[:, ct, :], xp2b[:, ct, :],
                           start=(ct == 0), stop=(ct == KDI - 1)),
                       wait=(bw(psrg[0]) + (("ss", cvss),)) if ct == 0 else (),
                       inc=((("mm", 1),) if ct == KDI - 1 else ()))
            xpmm = p.cur("mm")
            p.emit("scalar", lambda e: e.copy(out=dbc[:], in_=psrA[0:96, 0:T]),
                   wait=(("mm", xpmm), ("sp", None)), inc=(("ss", 1),))
            p.emit("scalar", lambda e: e.copy(out=dtinb[:], in_=psrA[0:64, 0:T]),
                   inc=(("ss", 1),))
            psrg[0] = ("ss", p.cur("ss"))
            dbss = p.cur("ss")
            p.dma("sync", "sp", s_dbc, dbc[64:96, :], wait=(("ss", dbss),))
            p.dma("gpsimd", "gbc", bcrep[:],
                  s_dbc.unsqueeze(0).partition_broadcast(128),
                  wait=(("sp", None),) + ((("vv", gsd[KDI - 1]),) if gsd is not None else ()))
            bcdg = p.cur("gbc")
            for ot in range(KDI):
                bk = ot % 4
                p.emit("tensor",
                       lambda e, ot=ot, bk=bk: e.matmul(
                           ps[bk][:, 0:T], dtwb[:, ot, :], dtinb[:],
                           start=True, stop=True),
                       wait=bw(bankg[bk]) + ((("ss", dbss),) if ot == 0 else ()),
                       inc=(("mm", 1),))
                mmc = p.cur("mm")
                p.emit("scalar",
                       lambda e, ot=ot, bk=bk: e.activation(
                           etmp[:], ps[bk][:, 0:T], AF.Exp,
                           bias=dtbt[:, ot:ot + 1]),
                       wait=(("mm", mmc),), inc=(("ss", 1),))
                p.emit("scalar",
                       lambda e, ot=ot: e.activation(
                           dtt[:, ot, :], etmp[:], AF.Ln, bias=onec[:]),
                       inc=(("ss", 1),))
                bankg[bk] = ("ss", p.cur("ss"))
            sd = [0] * KDI
            for k in range(KDI):
                avv_guard = p.cur("vv")
                for n in range(NST):
                    p.emit("scalar",
                           lambda e, k=k, n=n: e.activation(
                               at[:, n, :], dtt[:, k, :], AF.Exp,
                               scale=negP[:, k, n:n + 1]),
                           wait=((("vv", avv_guard),) if n == 0 else ()),
                           inc=((("ss", 1),) if n == NST - 1 else ()))
                ass = p.cur("ss")
                p.emit("vector",
                       lambda e, k=k: e.tensor_tensor(
                           dtx[:], dtt[:, k, :], xp2[:, k, :], ALU.mult),
                       wait=(("ss", ass),), inc=(("vv", 1),))
                p.emit("vector",
                       lambda e: e.tensor_tensor(
                           btl[:], dtx[:].unsqueeze(1).to_broadcast([128, NST, T]),
                           brep, ALU.mult),
                       wait=(("gbc", bcdg),), inc=(("vv", 1),))
                p.emit("vector",
                       lambda e, k=k: e.tensor_tensor(
                           tmp16w[:], at[:, :, 0], carry[:, k, :], ALU.mult),
                       inc=(("vv", 1),))
                p.emit("vector",
                       lambda e: e.tensor_tensor(
                           btl[:, :, 0], btl[:, :, 0], tmp16w[:], ALU.add),
                       inc=(("vv", 1),))
                p.emit("vector", lambda e: e.memset(at[:, :, 0], 0.0),
                       inc=(("vv", 1),))
                p.emit("vector",
                       lambda e: e.tensor_tensor_scan(
                           ht[:].rearrange("p n t -> p (n t)"),
                           at[:].rearrange("p n t -> p (n t)"),
                           btl[:].rearrange("p n t -> p (n t)"),
                           0.0, ALU.mult, ALU.add),
                       inc=(("vv", 1),))
                p.emit("vector",
                       lambda e, k=k: e.tensor_copy(carry[:, k, :], ht[:, :, T - 1]),
                       inc=(("vv", 1),))
                p.emit("vector",
                       lambda e: e.tensor_tensor(at[:], ht[:], crep, ALU.mult),
                       inc=(("vv", 1),))
                p.emit("vector",
                       lambda e: e.tensor_reduce(
                           ybuf[:].unsqueeze(2), at[:].transpose([0, 2, 1]),
                           AXX, ALU.add),
                       inc=(("vv", 1),))
                p.emit("vector",
                       lambda e, k=k: e.scalar_tensor_tensor(
                           y2buf[:], xp2[:, k, :], dpt[:, k:k + 1], ybuf[:],
                           ALU.mult, ALU.add),
                       inc=(("vv", 1),))
                p.emit("vector",
                       lambda e, k=k: e.tensor_tensor(
                           ygb[:, k, :], y2buf[:], szt[:, k, :], ALU.mult),
                       wait=((("sp", None),) if k == 0 else ()),
                       inc=(("vv", 1),))
                sd[k] = p.cur("vv")
            gsd = sd
            ygvv = p.cur("vv")
            p.dma("sync", "sp", _rk(s_ygb)[:, :, t0:t0 + T], ygb[:],
                  wait=(("vv", ygvv),))
            if _DBG == "m1xp2":
                p.dma("sync", "sp", _rk(dbg)[:, 0:KDI, t0:t0 + T], xp2[:],
                      wait=(("ss", cvss),))
            if _DBG == "m1dt":
                p.dma("sync", "sp", _rk(dbg)[:, 0:KDI, t0:t0 + T], dtt[:],
                      wait=(("vv", ygvv),))
        p.fence()
        p.flush_phase()

    ncm2_eff = NCM if _on("m2") else 0
    with contextlib.ExitStack() as st:
        Es = st.enter_context
        T = TCM
        ygc = Es(sb("ygc", [128, KDI, T], BF16))
        ln1c = Es(sb("m2ln1", [128, KD, T], F32))
        x0c = Es(sb("m2x0", [128, KD, T], F32))
        x1c = Es(sb("x1c", [128, KD, T], F32))
        x1b = Es(sb("x1b", [128, KD, T], BF16))
        hbt = Es(sb("hbt", [128, KHM, T], BF16))
        aoc = Es(sb("aoc", [128, KD, T], F32))
        tc_ = Es(sb("tc_", [128, KD, T], F32))
        x2c = Es(sb("x2c", [128, KD, T], F32))
        cast16 = Es(sb("m2c16", [128, KD, T], BF16))
        rows = Es(sb("m2rows", [1, 6, T], F32))
        owb = [Es(sb(f"owb{i}", [128, KDI, 128], BF16)) for i in range(2)]
        w1b = [Es(sb(f"m2w1b{i}", [128, KD, 512], BF16)) for i in range(2)]
        w2b = [Es(sb(f"m2w2b{i}", [128, KHM, 128], BF16)) for i in range(2)]
        ps = [Es(pst(f"m2ps{i}", [128, 512], F32)) for i in range(4)]
        psrA = Es(pst("m2psrA", [128, 512], F32))
        psrB = Es(pst("m2psrB", [128, 512], F32))
        bankg = [None] * 4
        wguard = {}
        psrg = [None]

        def bw(g):
            return (g,) if g else ()

        for c in range(ncm2_eff):
            t0 = c * T
            p.dma("sync", "ax", ygc[:], _rk(s_ygb)[:, :, t0:t0 + T])
            p.dma("sync", "ax", ln1c[:], _rk(s_ln1)[:, :, t0:t0 + T])
            p.dma("sync", "ax", x0c[:], _rk(xT)[:, :, t0:t0 + T])
            ldx = ("ax", p.cur("ax"))
            for ot in range(KD):
                wb = owb[ot % 2]
                wsem = "swo" + "AB"[ot % 2]
                p.dma("sync", wsem, wb[:], _rk(s_outwb)[:, :, ot * 128:(ot + 1) * 128],
                      wait=bw(wguard.get(("ow", ot % 2))))
                ldw = (wsem, p.cur(wsem))
                bk = ot % 4
                for ct in range(KDI):
                    p.emit("tensor",
                           lambda e, ct=ct, wb=wb, bk=bk: e.matmul(
                               ps[bk][:, 0:T], wb[:, ct, :], ygc[:, ct, :],
                               start=(ct == 0), stop=(ct == KDI - 1)),
                           wait=(bw(bankg[bk]) + (ldw, ("ax", None))) if ct == 0 else (),
                           inc=((("mm", 1),) if ct == KDI - 1 else ()))
                mmc = p.cur("mm")
                wguard[("ow", ot % 2)] = ("mm", mmc)
                p.emit("vector",
                       lambda e, ot=ot, bk=bk: e.tensor_tensor(
                           x1c[:, ot, :], ps[bk][:, 0:T], ln1c[:, ot, :], ALU.add),
                       wait=(("mm", mmc),), inc=(("vv", 1),))
                bankg[bk] = ("vv", p.cur("vv"))
            x1vv = p.cur("vv")
            p.emit("scalar", lambda e: e.copy(out=x1b[:], in_=x1c[:]),
                   wait=(("vv", x1vv),), inc=(("ss", 1),))
            x1ss = p.cur("ss")
            for g in range(KHM // 4):
                wb = w1b[g % 2]
                wsem = "sw1" + "AB"[g % 2]
                p.dma("sync", wsem, wb[:], _rk(s_w1mb)[:, :, g * 512:(g + 1) * 512],
                      wait=bw(wguard.get(("w1", g % 2))))
                ldw = (wsem, p.cur(wsem))
                for otl in range(4):
                    ot = 4 * g + otl
                    for ct in range(KD):
                        p.emit("tensor",
                               lambda e, ct=ct, wb=wb, otl=otl: e.matmul(
                                   ps[otl][:, 0:T],
                                   wb[:, ct, 128 * otl:128 * otl + 128],
                                   x1b[:, ct, :],
                                   start=(ct == 0), stop=(ct == KD - 1)),
                               wait=(bw(bankg[otl]) + (ldw, ("ss", x1ss))) if ct == 0 else (),
                               inc=((("mm", 1),) if ct == KD - 1 else ()))
                    mmc = p.cur("mm")
                    p.emit("scalar",
                           lambda e, ot=ot, otl=otl: e.activation(
                               hbt[:, ot, :], ps[otl][:, 0:T], AF.Gelu,
                               bias=b1mv[:, ot:ot + 1]),
                           wait=(("mm", mmc),), inc=(("ss", 1),))
                    bankg[otl] = ("ss", p.cur("ss"))
                wguard[("w1", g % 2)] = ("mm", p.cur("mm"))
            hbss = p.cur("ss")
            for ot in range(KD):
                wb = w2b[ot % 2]
                wsem = "sw2" + "AB"[ot % 2]
                p.dma("sync", wsem, wb[:], _rk(s_w2mb)[:, :, ot * 128:(ot + 1) * 128],
                      wait=bw(wguard.get(("w2", ot % 2))))
                ldw = (wsem, p.cur(wsem))
                bk = ot % 4
                for ct in range(KHM):
                    p.emit("tensor",
                           lambda e, ct=ct, wb=wb, bk=bk: e.matmul(
                               ps[bk][:, 0:T], wb[:, ct, :], hbt[:, ct, :],
                               start=(ct == 0), stop=(ct == KHM - 1)),
                           wait=(bw(bankg[bk]) + (ldw, ("ss", hbss))) if ct == 0 else (),
                           inc=((("mm", 1),) if ct == KHM - 1 else ()))
                mmc = p.cur("mm")
                wguard[("w2", ot % 2)] = ("mm", mmc)
                p.emit("vector",
                       lambda e, ot=ot, bk=bk: e.scalar_tensor_tensor(
                           aoc[:, ot, :], ps[bk][:, 0:T], b2mv[:, ot:ot + 1],
                           x1c[:, ot, :], ALU.add, ALU.add),
                       wait=(("mm", mmc),), inc=(("vv", 1),))
                bankg[bk] = ("vv", p.cur("vv"))
            aovv = p.cur("vv")
            repmm = ln_stats(aoc, psrA, psrB, rows, cast16, T, psrg,
                             wait_extra=(("vv", aovv),))
            p.emit("vector", lambda e: e.tensor_tensor(
                tc_[:], aoc[:],
                psrA[:, 0:T].unsqueeze(1).to_broadcast([128, KD, T]),
                ALU.subtract),
                wait=(("mm", repmm),), inc=(("vv", 1),))
            p.emit("vector", lambda e: e.tensor_tensor(
                tc_[:], tc_[:],
                psrB[:, 0:T].unsqueeze(1).to_broadcast([128, KD, T]),
                ALU.mult),
                inc=(("vv", 1),))
            p.emit("vector", lambda e: e.tensor_tensor(
                x2c[:], tc_[:], x0c[:], ALU.add),
                wait=(ldx,), inc=(("vv", 1),))
            x2vv = p.cur("vv")
            psrg[0] = ("vv", x2vv)
            p.dma("sync", "sp", _rk(s_x2)[:, :, t0:t0 + T], x2c[:],
                  wait=(("vv", x2vv),))
            repmm = ln_stats(x2c, psrA, psrB, rows, cast16, T, psrg,
                             wait_extra=(("vv", x2vv),))
            p.emit("vector", lambda e: e.tensor_tensor(
                tc_[:], x2c[:],
                psrA[:, 0:T].unsqueeze(1).to_broadcast([128, KD, T]),
                ALU.subtract),
                wait=(("mm", repmm),), inc=(("vv", 1),))
            p.emit("vector", lambda e: e.tensor_tensor(
                tc_[:], tc_[:],
                psrB[:, 0:T].unsqueeze(1).to_broadcast([128, KD, T]),
                ALU.mult),
                inc=(("vv", 1),))
            xfvv = p.cur("vv")
            psrg[0] = ("vv", xfvv)
            p.dma("sync", "sp", _rk(s_xf0)[:, :, t0:t0 + T], tc_[:],
                  wait=(("vv", xfvv),))
            p.fence()
        p.flush_phase()

    def fa_phase(l, src, dst):
        T = TCF
        with contextlib.ExitStack() as st:
            Es = st.enter_context
            wqkvb = Es(sb(f"L{l}wqkvb", [128, KD, 3 * INNER], BF16))
            wob = Es(sb(f"L{l}wob", [128, KIN, D], BF16))
            xfb = Es(sb(f"L{l}xfb", [128, KD, T + 1], BF16))
            xff = Es(sb(f"L{l}xff", [128, KD, T], F32))
            qf = Es(sb(f"L{l}qf", [128, KIN, T], F32))
            kf = Es(sb(f"L{l}kf", [128, KIN, T], F32))
            sqb = Es(sb(f"L{l}sqb", [128, T], BF16))
            qrb = Es(sb(f"L{l}qrb", [128, KIN, T], BF16))
            kfull = Es(sb(f"L{l}kfull", [128, KIN, T], BF16))
            kz = [Es(sb(f"L{l}kz{i}", [128, 2, KIN, T], BF16)) for i in range(2)]
            vb = Es(sb(f"L{l}vb", [128, KIN, T], BF16))
            vT = [Es(sb(f"L{l}vT{i}", [128, 4, KIN, 128], BF16)) for i in range(2)]
            rowq = Es(sb(f"L{l}rowq", [8, 4, T], F32))
            csqw = Es(sb(f"L{l}csqw", [128, T], F32))
            ssqw = Es(sb(f"L{l}ssqw", [128, T], F32))
            cskw = Es(sb(f"L{l}cskw", [128, T], F32))
            sskw = Es(sb(f"L{l}sskw", [128, T], F32))
            base4 = Es(sb(f"L{l}base4", [128, 4, T], F32))
            m1t = Es(sb(f"L{l}m1t", [128, T], F32))
            m2t = Es(sb(f"L{l}m2t", [128, T], F32))
            ut = Es(sb(f"L{l}ut", [128, T], F32))
            qb16 = Es(sb(f"L{l}qb16", [128, T], BF16))
            nums = Es(sb(f"L{l}nums", [128, 640], BF16))
            attnT = Es(sb(f"L{l}attnT", [128, 8, T], BF16))
            mxt = Es(sb(f"L{l}mxt", [128, 8], F32))
            o_sb = Es(sb(f"L{l}o_sb", [128, KIN, T], BF16))
            xmo = Es(sb(f"L{l}xmo", [128, KD, T], F32))
            simA = [Es(pst(f"L{l}simA{i}", [128, 512], F32)) for i in range(2)]
            simB = [Es(pst(f"L{l}simB{i}", [128, 512], F32)) for i in range(2)]
            psT = Es(pst(f"L{l}psT", [128, 1024], BF16))
            avp = [Es(pst(f"L{l}avp{i}", [128, 512], F32)) for i in range(2)]
            spare = Es(pst(f"L{l}spare", [128, 512], F32))
            gsim = [None, None]
            gsimB = [None, None]
            gpsT = [None]
            gsp = [None]
            gav = [None, None]
            gnums = [None]
            gattnT = [None]

            def bw(g):
                return (g,) if g else ()

            p.emit("vector", lambda e: e.memset(attnT[:], 0.0), inc=(("vv", 1),))
            p.emit("vector", lambda e: e.memset(kz[0][:], 0.0), inc=(("vv", 1),))
            p.emit("vector", lambda e: e.memset(kz[1][:], 0.0), inc=(("vv", 1),))
            p.dma("gpsimd", "gw", wqkvb[:], wqkv[l].rearrange("(k p) o -> p k o", p=128))
            p.dma("gpsimd", "gw", wob[:], wo[l].rearrange("(k p) o -> p k o", p=128))
            wdg = p.cur("gw")

            for w in range(NCF if _on(f"fa{l}") else 0):
                t0 = w * T
                cw = w % 2
                pw = 1 - cw
                if w == 0:
                    p.emit("vector", lambda e: e.memset(xfb[:, :, 0:1], 0.0),
                           inc=(("vv", 1),))
                    p.dma("gpsimd", "gxf", xfb[:, :, 1:T + 1], _rk(src)[:, :, 0:T])
                else:
                    p.dma("gpsimd", "gxf", xfb[:], _rk(src)[:, :, t0 - 1:t0 + T])
                p.dma("sync", "ax", xff[:], _rk(src)[:, :, t0:t0 + T])
                p.dma("sync", "ax", base4[:, 0, :], csq0[:, t0:t0 + T])
                p.dma("sync", "ax", base4[:, 1, :], ssq0[:, t0:t0 + T])
                p.dma("sync", "ax", base4[:, 2, :], csk0[:, t0:t0 + T])
                p.dma("sync", "ax", base4[:, 3, :], ssk0[:, t0:t0 + T])
                ldx, ldg = p.cur("ax"), p.cur("gxf")
                p.emit("vector", lambda e: e.tensor_scalar(
                    csqw[:], base4[:, 0, :], qs2[:, l:l + 1], 8.0,
                    ALU.mult, ALU.mult),
                    wait=(("ax", ldx),), inc=(("vv", 1),))
                p.emit("vector", lambda e: e.tensor_scalar(
                    ssqw[:], base4[:, 1, :], qs2r[:, l:l + 1], 8.0,
                    ALU.mult, ALU.mult),
                    inc=(("vv", 1),))
                p.emit("vector", lambda e: e.tensor_scalar(
                    cskw[:], base4[:, 2, :], ks2[:, l:l + 1], None, ALU.mult),
                    inc=(("vv", 1),))
                p.emit("vector", lambda e: e.tensor_scalar(
                    sskw[:], base4[:, 3, :], ks2r[:, l:l + 1], None, ALU.mult),
                    inc=(("vv", 1),))
                tabvv = p.cur("vv")

                qkbanks = [simA[0], simA[1], avp[0], avp[1]]
                qkg = [gsim[0], gsim[1], gav[0], gav[1]]
                for part in ("q", "k", "v"):
                    base_ot = {"q": 0, "k": KIN, "v": 2 * KIN}[part]
                    for j in range(KIN):
                        ot = base_ot + j
                        bank = qkbanks[j]
                        for ct in range(KD):
                            ap_rhs = (xfb[:, ct, 1:T + 1] if ct < KD // 2
                                      else xfb[:, ct, 0:T])
                            p.emit("tensor",
                                   lambda e, ct=ct, ot=ot, bank=bank, ap=ap_rhs: e.matmul(
                                       bank[:, 0:T],
                                       wqkvb[:, ct, 128 * ot:128 * ot + 128],
                                       ap, start=(ct == 0), stop=(ct == KD - 1)),
                                   wait=(bw(qkg[j]) +
                                         ((("gxf", ldg), ("gw", wdg),
                                           ("vv", None)) if ct == 0 else ())),
                                   inc=((("mm", 1),) if ct == KD - 1 else ()))
                        mmc = p.cur("mm")
                        if part in ("q", "k"):
                            dest = qf if part == "q" else kf
                            p.emit("scalar",
                                   lambda e, j=j, bank=bank, dest=dest: e.copy(
                                       out=dest[:, j, :], in_=bank[:, 0:T]),
                                   wait=(("mm", mmc),), inc=(("ss", 1),))
                            p.emit("scalar",
                                   lambda e, bank=bank: e.activation(
                                       sqb[:], bank[:, 0:T], AF.Square),
                                   wait=(("mm", None),), inc=(("ss", 1),))
                            sqs = p.cur("ss")
                            p.emit("tensor",
                                   lambda e, j=j: e.matmul(
                                       spare[0:8, 0:T], e8t[:, j, :], sqb[:],
                                       start=(j == 0), stop=(j == KIN - 1)),
                                   wait=(bw(gsp[0]) if j == 0 else ()) + (("ss", sqs),),
                                   inc=(("mm", 1),))
                            qkg[j] = ("ss", sqs)
                            if j == KIN - 1:
                                smm = p.cur("mm")
                                row = 0 if part == "q" else 1
                                p.emit("scalar",
                                       lambda e, row=row: e.activation(
                                           rowq[:, row, :], spare[0:8, 0:T],
                                           AF.Sqrt),
                                       wait=(("mm", smm),), inc=(("ss", 1),))
                                p.emit("vector",
                                       lambda e, row=row: e.tensor_scalar(
                                           rowq[:, row, :], rowq[:, row, :],
                                           1e-12, None, ALU.max),
                                       wait=(("ss", p.cur("ss")),),
                                       inc=(("vv", 1),))
                                p.emit("vector",
                                       lambda e, row=row: e.reciprocal(
                                           rowq[:, 2 + row, :], rowq[:, row, :]),
                                       wait=(("vv", None),), inc=(("vv", 1),))
                                gsp[0] = ("vv", p.cur("vv"))
                        else:
                            p.emit("scalar",
                                   lambda e, j=j, bank=bank: e.copy(
                                       out=vb[:, j, :], in_=bank[:, 0:T]),
                                   wait=(("mm", mmc),), inc=(("ss", 1),))
                            qkg[j] = ("ss", p.cur("ss"))
                gsim[0], gsim[1], gav[0], gav[1] = qkg
                rowvv = p.cur("vv")
                for part in (("q", "k") if _FA_STAGE >= 2 else ()):
                    src_f = qf if part == "q" else kf
                    dst_b = qrb if part == "q" else kfull
                    cst = csqw if part == "q" else cskw
                    sst = ssqw if part == "q" else sskw
                    row = 0 if part == "q" else 1
                    for j in range(KIN):
                        p.emit("tensor",
                               lambda e, j=j, row=row: e.matmul(
                                   spare[:, 0:T], g8t[:, j, :],
                                   rowq[:, 2 + row, :],
                                   start=True, stop=True),
                               wait=bw(gsp[0]) + (("vv", rowvv),),
                               inc=(("mm", 1),))
                        srmm = p.cur("mm")
                        p.emit("vector",
                               lambda e, j=j, src_f=src_f, cst=cst: e.tensor_tensor(
                                   m1t[:], src_f[:, j, :], cst[:], ALU.mult),
                               wait=(("vv", tabvv),), inc=(("vv", 1),))
                        p.emit("scalar",
                               lambda e, j=j, src_f=src_f: e.copy(
                                   out=qb16[:], in_=src_f[:, j, :]),
                               wait=(("mm", None),), inc=(("ss", 1),))
                        qbs = p.cur("ss")
                        p.emit("tensor",
                               lambda e: e.matmul(
                                   simB[0][:, 0:T], rotmt[:], qb16[:],
                                   start=True, stop=True),
                               wait=bw(gsimB[0]) + (("ss", qbs),),
                               inc=(("mm", 1),))
                        rmm = p.cur("mm")
                        p.emit("vector",
                               lambda e, sst=sst: e.tensor_tensor(
                                   m2t[:], simB[0][:, 0:T], sst[:], ALU.mult),
                               wait=(("mm", rmm),), inc=(("vv", 1),))
                        gsimB[0] = ("vv", p.cur("vv"))
                        p.emit("vector",
                               lambda e: e.tensor_tensor(ut[:], m1t[:], m2t[:],
                                                         ALU.add),
                               inc=(("vv", 1),))
                        p.emit("vector",
                               lambda e, j=j, dst_b=dst_b: e.tensor_tensor(
                                   dst_b[:, j, :], ut[:], spare[:, 0:T],
                                   ALU.mult),
                               wait=(("mm", srmm),), inc=(("vv", 1),))
                        gsp[0] = ("vv", p.cur("vv"))
                        if part == "k":
                            kvv = p.cur("vv")
                            p.emit("scalar",
                                   lambda e, j=j, cw=cw: e.copy(
                                       out=kz[cw][0:64, 0, j, :],
                                       in_=kfull[0:64, j, :]),
                                   wait=(("vv", kvv),), inc=(("ss", 1),))
                            p.emit("scalar",
                                   lambda e, j=j, cw=cw: e.copy(
                                       out=kz[cw][64:128, 1, j, :],
                                       in_=kfull[64:128, j, :]),
                                   inc=(("ss", 1),))
                qkvv = p.cur("vv")
                kzss = p.cur("ss")
                vbss = p.cur("ss")
                for half in (range(2) if _FA_STAGE >= 3 else ()):
                    for idx in range(8):
                        gi = half * 8 + idx
                        tb = gi // 4
                        pr = gi % 4
                        p.emit("tensor",
                               lambda e, pr=pr, tb=tb, idx=idx: e.transpose(
                                   psT[:, idx * 128:(idx + 1) * 128],
                                   vb[:, pr, 128 * tb:128 * tb + 128],
                                   identt[:]),
                               wait=(bw(gpsT[0]) + (("ss", vbss),)) if idx == 0 else (),
                               inc=((("mm", 1),) if idx == 7 else ()))
                    tmm = p.cur("mm")
                    p.emit("scalar",
                           lambda e, half=half, cw=cw: e.copy(
                               out=vT[cw][:].rearrange("p a h d -> p (a h d)")
                               [:, half * 1024:(half + 1) * 1024],
                               in_=psT[:]),
                           wait=(("mm", tmm),), inc=(("ss", 1),))
                    gpsT[0] = ("ss", p.cur("ss"))

                def slots(ib):
                    s = []
                    if w > 0:
                        for jt in range(ib, 4):
                            s.append(("prev", jt))
                    for jt in range(0, ib + 1):
                        s.append(("own", jt))
                    return s

                tiles = [(h, ib) for h in range(NH) for ib in range(4)]
                if _FA_NT is not None:
                    tiles = tiles[:_FA_NT]
                simdone = {}

                def emit_sim(h, ib, par):
                    sl = slots(ib)
                    nsl_ = len(sl)
                    for si, (which, jt) in enumerate(sl):
                        ksrc = kz[pw] if which == "prev" else kz[cw]
                        inB = si >= 4
                        tgt = (simB[par][:, 0:128] if inB
                               else simA[par][:, 128 * si:128 * si + 128])
                        mk = None
                        if which == "own" and jt == ib:
                            mk = maskot
                        if which == "prev" and jt == ib:
                            mk = maskpt
                        wt = ()
                        if si == 0:
                            wt = (bw(gsim[par]) + bw(gsimB[par])
                                  + (("vv", qkvv), ("ss", kzss)))
                        last = (si == nsl_ - 1)
                        p.emit("tensor",
                               lambda e, h=h, ib=ib, jt=jt, ksrc=ksrc, tgt=tgt, mk=mk: e.matmul(
                                   tgt,
                                   qrb[:, h // 2, 128 * ib:128 * ib + 128],
                                   ksrc[:, h % 2, h // 2,
                                        128 * jt:128 * jt + 128],
                                   start=True, stop=(mk is None)),
                               wait=wt,
                               inc=((("mm", 1),) if (last and mk is None) else ()))
                        if mk is not None:
                            p.emit("tensor",
                                   lambda e, mk=mk, tgt=tgt: e.matmul(
                                       tgt, mk[:], identt[:],
                                       start=False, stop=True),
                                   inc=((("mm", 1),) if last else ()))

                def soft_tile(h, ib, par):
                    sl = slots(ib)
                    nsl = len(sl)
                    nA = min(nsl, 4)
                    mmsim = simdone[(h, ib)]
                    if _FA_SOFT < 1:
                        gsim[par] = ("mm", mmsim)
                        gsimB[par] = ("mm", mmsim)
                        return
                    p.emit("vector",
                           lambda e, nA=nA, par=par: e.tensor_reduce(
                               mxt[:, 0:1], simA[par][:, 0:128 * nA],
                               AXX, ALU.max),
                           wait=(("mm", mmsim), ("ss", None)),
                           inc=(("vv", 1),))
                    if nsl > 4:
                        p.emit("vector",
                               lambda e, par=par: e.tensor_reduce(
                                   mxt[:, 1:2], simB[par][:, 0:128],
                                   AXX, ALU.max),
                               inc=(("vv", 1),))
                        p.emit("vector",
                               lambda e: e.tensor_tensor(
                                   mxt[:, 0:1], mxt[:, 0:1], mxt[:, 1:2],
                                   ALU.max),
                               inc=(("vv", 1),))
                    p.emit("vector",
                           lambda e: e.tensor_scalar(
                               mxt[:, 2:3], mxt[:, 0:1], -1.0, None, ALU.mult),
                           inc=(("vv", 1),))
                    mxvv = p.cur("vv")
                    if _FA_SOFT < 2:
                        gsim[par] = ("vv", mxvv)
                        gsimB[par] = ("vv", mxvv)
                        return
                    p.emit("scalar",
                           lambda e, nA=nA, par=par: e.activation(
                               nums[:, 0:128 * nA], simA[par][:, 0:128 * nA],
                               AF.Exp, bias=mxt[:, 2:3],
                               accum_out=mxt[:, 3:4]),
                           wait=(("vv", mxvv),) + bw(gnums[0]),
                           inc=(("ss", 1),))
                    if nsl > 4:
                        p.emit("scalar",
                               lambda e, par=par: e.activation(
                                   nums[:, 512:640], simB[par][:, 0:128],
                                   AF.Exp, bias=mxt[:, 2:3],
                                   accum_out=mxt[:, 4:5]),
                               inc=(("ss", 1),))
                    e1ss = p.cur("ss")
                    if _FA_SOFT < 3:
                        gsim[par] = ("ss", e1ss)
                        gsimB[par] = ("ss", e1ss)
                        return
                    if nsl > 4:
                        p.emit("vector",
                               lambda e: e.tensor_tensor(
                                   mxt[:, 5:6], mxt[:, 3:4], mxt[:, 4:5],
                                   ALU.add),
                               wait=(("ss", e1ss),), inc=(("vv", 1),))
                    else:
                        p.emit("vector",
                               lambda e: e.tensor_copy(mxt[:, 5:6], mxt[:, 3:4]),
                               wait=(("ss", e1ss),), inc=(("vv", 1),))
                    p.emit("scalar",
                           lambda e: e.activation(mxt[:, 6:7], mxt[:, 5:6],
                                                  AF.Ln),
                           wait=(("vv", p.cur("vv")),), inc=(("ss", 1),))
                    p.emit("vector",
                           lambda e: e.tensor_tensor(
                               mxt[:, 7:8], mxt[:, 2:3], mxt[:, 6:7],
                               ALU.subtract),
                           wait=(("ss", p.cur("ss")),), inc=(("vv", 1),))
                    b2vv = p.cur("vv")
                    if _FA_SOFT < 4:
                        gsim[par] = ("vv", b2vv)
                        gsimB[par] = ("vv", b2vv)
                        return
                    p.emit("scalar",
                           lambda e, nA=nA, par=par: e.activation(
                               nums[:, 0:128 * nA], simA[par][:, 0:128 * nA],
                               AF.Exp, bias=mxt[:, 7:8]),
                           wait=(("vv", b2vv),), inc=(("ss", 1),))
                    if nsl > 4:
                        p.emit("scalar",
                               lambda e, par=par: e.activation(
                                   nums[:, 512:640], simB[par][:, 0:128],
                                   AF.Exp, bias=mxt[:, 7:8]),
                               inc=(("ss", 1),))
                    ess = p.cur("ss")
                    gsim[par] = ("ss", ess)
                    gsimB[par] = ("ss", ess)
                    if _FA_SOFT < 5:
                        return
                    for si in range(nsl):
                        p.emit("tensor",
                               lambda e, si=si: e.transpose(
                                   psT[:, 128 * si:128 * si + 128],
                                   nums[:, 128 * si:128 * si + 128]
                                   if si < 4 else nums[:, 512:640],
                                   identt[:]),
                               wait=(bw(gpsT[0]) + (("ss", ess),)) if si == 0 else (),
                               inc=((("mm", 1),) if si == nsl - 1 else ()))
                    tmm = p.cur("mm")
                    gnums[0] = ("mm", tmm)
                    if _FA_SOFT < 6:
                        return
                    for si, (which, jt) in enumerate(sl):
                        idx = jt if which == "prev" else 4 + jt
                        p.emit("scalar",
                               lambda e, si=si, idx=idx, ib=ib: e.copy(
                                   out=attnT[:, idx, 128 * ib:128 * ib + 128],
                                   in_=psT[:, 128 * si:128 * si + 128]),
                               wait=((("mm", tmm),) + bw(gattnT[0])) if si == 0 else (),
                               inc=((("ss", 1),) if si == nsl - 1 else ()))
                    gpsT[0] = ("ss", p.cur("ss"))
                    atss = p.cur("ss")
                    if _FA_SOFT < 7:
                        return
                    bank = avp[h % 2]
                    first_av = (ib == 0)
                    last_av = (ib == 3)
                    for si, (which, jt) in enumerate(sl):
                        idx = jt if which == "prev" else 4 + jt
                        vsrc = vT[pw] if which == "prev" else vT[cw]
                        p.emit("tensor",
                               lambda e, h=h, ib=ib, jt=jt, idx=idx, vsrc=vsrc, bank=bank, si=si, nsl=nsl: e.matmul(
                                   bank[0:64, 128 * ib:128 * ib + 128],
                                   vsrc[:, jt, h // 2,
                                        64 * (h % 2):64 * (h % 2) + 64],
                                   attnT[:, idx, 128 * ib:128 * ib + 128],
                                   start=(si == 0),
                                   stop=(si == nsl - 1)),
                               wait=((bw(gav[h % 2]) + (("ss", atss),)) if (first_av and si == 0)
                                     else ((("ss", atss),) if si == 0 else ())),
                               inc=((("mm", 1),) if (last_av and si == nsl - 1) else ()))
                    gattnT[0] = ("mm", p.cur("mm"))
                    if last_av:
                        avmm = p.cur("mm")
                        p.emit("scalar",
                               lambda e, h=h, bank=bank: e.copy(
                                   out=o_sb[64 * (h % 2):64 * (h % 2) + 64,
                                            h // 2, :],
                                   in_=bank[0:64, 0:T]),
                               wait=(("mm", avmm),), inc=(("ss", 1),))
                        gav[h % 2] = ("ss", p.cur("ss"))

                if _FA_STAGE < 4:
                    tiles = []
                if tiles:
                    emit_sim(*tiles[0], 0)
                    simdone[tiles[0]] = p.cur("mm")
                for i, (h, ib) in enumerate(tiles):
                    if i + 1 < len(tiles):
                        nh_, nib = tiles[i + 1]
                        emit_sim(nh_, nib, (i + 1) % 2)
                        simdone[(nh_, nib)] = p.cur("mm")
                    soft_tile(h, ib, i % 2)
                osbss = p.cur("ss")
                for ot in (range(KD) if _FA_STAGE >= 5 else ()):
                    bank = simA[ot % 2]
                    for ct in range(KIN):
                        p.emit("tensor",
                               lambda e, ct=ct, ot=ot, bank=bank: e.matmul(
                                   bank[:, 0:T],
                                   wob[:, ct, 128 * ot:128 * ot + 128],
                                   o_sb[:, ct, :],
                                   start=(ct == 0), stop=(ct == KIN - 1)),
                               wait=(bw(gsim[ot % 2]) + (("ss", osbss),)) if ct == 0 else (),
                               inc=((("mm", 1),) if ct == KIN - 1 else ()))
                    mmc = p.cur("mm")
                    p.emit("vector",
                           lambda e, ot=ot, bank=bank: e.tensor_tensor(
                               xmo[:, ot, :], bank[:, 0:T], xff[:, ot, :],
                               ALU.add),
                           wait=(("mm", mmc), ("ax", ldx)), inc=(("vv", 1),))
                    gsim[ot % 2] = ("vv", p.cur("vv"))
                xmvv = p.cur("vv")
                p.dma("sync", "sp", _rk(dst)[:, :, t0:t0 + T], xmo[:],
                      wait=(("vv", xmvv),))
                p.fence()
            p.flush_phase()

    def fb_phase(l, src, dst):
        T = TCF
        with contextlib.ExitStack() as st:
            Es = st.enter_context
            xmc = Es(sb(f"L{l}xmc", [128, KD, T + 1], F32))
            xmb = Es(sb(f"L{l}xmb", [128, KD, T + 1], BF16))
            rows = Es(sb(f"L{l}fbrows", [1, 6, T], F32))
            tc_ = Es(sb(f"L{l}fbtc", [128, KD, T], F32))
            xrb = Es(sb(f"L{l}xrb", [128, KD, T], BF16))
            h4b = Es(sb(f"L{l}h4b", [128, KHID, T], BF16))
            glt = Es(sb(f"L{l}glt", [128, T], F32))
            xnc = Es(sb(f"L{l}xnc", [128, KD, T], F32))
            w1ab = [Es(sb(f"L{l}w1ab{i}", [128, KD, 256], BF16)) for i in range(2)]
            w1gb = [Es(sb(f"L{l}w1gb{i}", [128, KD, 256], BF16)) for i in range(2)]
            w2b = [Es(sb(f"L{l}fbw2b{i}", [128, KHID, 128], BF16)) for i in range(2)]
            ps = [Es(pst(f"L{l}fbps{i}", [128, 512], F32)) for i in range(4)]
            psrA = Es(pst(f"L{l}fbpsrA", [128, 512], F32))
            psrB = Es(pst(f"L{l}fbpsrB", [128, 512], F32))
            bankg = [None] * 4
            wguard = {}
            psrg = [None]

            def bw(g):
                return (g,) if g else ()

            for c in range((_NCF_LIM or NCF) if _on(f"fb{l}") else 0):
                t0 = c * T
                if c == 0:
                    p.emit("vector", lambda e: e.memset(xmc[:, :, 0:1], 0.0),
                           inc=(("vv", 1),))
                    p.dma("sync", "ax", xmc[:, :, 1:T + 1], _rk(src)[:, :, 0:T])
                else:
                    p.dma("sync", "ax", xmc[:], _rk(src)[:, :, t0 - 1:t0 + T])
                repmm = ln_stats(xmc, psrA, psrB, rows, xmb, T, psrg,
                                 wait_extra=(("ax", None),), shift=True)
                for ct in range(KD):
                    src_ap = (xmc[:, ct, 1:T + 1] if ct < KD // 2
                              else xmc[:, ct, 0:T])
                    p.emit("vector",
                           lambda e, ct=ct, src_ap=src_ap: e.tensor_tensor(
                               tc_[:, ct, :], src_ap, psrA[:, 0:T],
                               ALU.subtract),
                           wait=(("mm", repmm),) if ct == 0 else (),
                           inc=(("vv", 1),))
                p.emit("vector", lambda e: e.tensor_tensor(
                    tc_[:], tc_[:],
                    psrB[:, 0:T].unsqueeze(1).to_broadcast([128, KD, T]),
                    ALU.mult),
                    inc=(("vv", 1),))
                lnvv = p.cur("vv")
                psrg[0] = ("vv", lnvv)
                for ct in range(KD):
                    p.emit("scalar",
                           lambda e, ct=ct: e.activation(
                               xrb[:, ct, :], tc_[:, ct, :], AF.Identity,
                               scale=lngv[:, l, ct:ct + 1],
                               bias=lnbt[:, l, ct:ct + 1]),
                           wait=(("vv", lnvv),) if ct == 0 else (),
                           inc=(("ss", 1),))
                xrss = p.cur("ss")
                for g in range(KHID // 2):
                    pts = [2 * g, 2 * g + 1]
                    wa = w1ab[g % 2]
                    wg = w1gb[g % 2]
                    wsa = "swa" + "AB"[g % 2]
                    wsg = "swg" + "AB"[g % 2]
                    p.dma("sync", wsa, wa[:],
                          _rk(s_w1fab[l])[:, :, 256 * g:256 * (g + 1)],
                          wait=bw(wguard.get(("wa", g % 2))))
                    p.dma("sync", wsg, wg[:],
                          _rk(s_w1fgb[l])[:, :, 256 * g:256 * (g + 1)],
                          wait=bw(wguard.get(("wg", g % 2))))
                    ldw = ((wsa, p.cur(wsa)), (wsg, p.cur(wsg)))
                    for ii, pt in enumerate(pts):
                        bka = (2 * ii) % 4
                        bkg_ = (2 * ii + 1) % 4
                        for ct in range(KD):
                            p.emit("tensor",
                                   lambda e, ct=ct, wa=wa, ii=ii, bka=bka: e.matmul(
                                       ps[bka][:, 0:T],
                                       wa[:, ct, 128 * ii:128 * ii + 128],
                                       xrb[:, ct, :],
                                       start=(ct == 0), stop=(ct == KD - 1)),
                                   wait=(bw(bankg[bka]) +
                                         (ldw + (("ss", xrss),) if ct == 0 else ())),
                                   inc=((("mm", 1),) if ct == KD - 1 else ()))
                        for ct in range(KD):
                            p.emit("tensor",
                                   lambda e, ct=ct, wg=wg, ii=ii, bkg_=bkg_: e.matmul(
                                       ps[bkg_][:, 0:T],
                                       wg[:, ct, 128 * ii:128 * ii + 128],
                                       xrb[:, ct, :],
                                       start=(ct == 0), stop=(ct == KD - 1)),
                                   wait=(bw(bankg[bkg_]) if ct == 0 else ()),
                                   inc=((("mm", 1),) if ct == KD - 1 else ()))
                        mmc = p.cur("mm")
                        p.emit("scalar",
                               lambda e, pt=pt, bkg_=bkg_: e.activation(
                                   glt[:], ps[bkg_][:, 0:T], AF.Gelu,
                                   bias=b1fgv[:, l, pt:pt + 1]),
                               wait=(("mm", mmc), ("vv", None)), inc=(("ss", 1),))
                        gss = p.cur("ss")
                        p.emit("vector",
                               lambda e, pt=pt, bka=bka: e.scalar_tensor_tensor(
                                   h4b[:, pt, :], ps[bka][:, 0:T],
                                   b1fav[:, l, pt:pt + 1], glt[:],
                                   ALU.add, ALU.mult),
                               wait=(("ss", gss),), inc=(("vv", 1),))
                        bankg[bka] = ("vv", p.cur("vv"))
                        bankg[bkg_] = ("vv", p.cur("vv"))
                    wguard[("wa", g % 2)] = ("mm", p.cur("mm"))
                    wguard[("wg", g % 2)] = ("mm", p.cur("mm"))
                h4vv = p.cur("vv")
                for ot in range(KD):
                    wb = w2b[ot % 2]
                    wsem = "swf" + "AB"[ot % 2]
                    p.dma("sync", wsem, wb[:],
                          _rk(s_w2fb[l])[:, :, 128 * ot:128 * (ot + 1)],
                          wait=bw(wguard.get(("w2", ot % 2))))
                    ldw = (wsem, p.cur(wsem))
                    bk = ot % 4
                    for ct in range(KHID):
                        p.emit("tensor",
                               lambda e, ct=ct, wb=wb, bk=bk: e.matmul(
                                   ps[bk][:, 0:T], wb[:, ct, :], h4b[:, ct, :],
                                   start=(ct == 0), stop=(ct == KHID - 1)),
                               wait=(bw(bankg[bk]) +
                                     ((ldw, ("vv", h4vv)) if ct == 0 else ())),
                               inc=((("mm", 1),) if ct == KHID - 1 else ()))
                    mmc = p.cur("mm")
                    wguard[("w2", ot % 2)] = ("mm", mmc)
                    p.emit("vector",
                           lambda e, ot=ot, bk=bk: e.scalar_tensor_tensor(
                               xnc[:, ot, :], ps[bk][:, 0:T],
                               b2fv[:, l, ot:ot + 1], xmc[:, ot, 1:T + 1],
                               ALU.add, ALU.add),
                           wait=(("mm", mmc),), inc=(("vv", 1),))
                    bankg[bk] = ("vv", p.cur("vv"))
                xnvv = p.cur("vv")
                p.dma("sync", "sp", _rk(dst)[:, :, t0:t0 + T], xnc[:],
                      wait=(("vv", xnvv),))
                p.fence()
            p.flush_phase()

    fa_phase(0, s_xf0, s_xm[0])
    fb_phase(0, s_xm[0], s_xl[0])
    fa_phase(1, s_xl[0], s_xm[1])
    fb_phase(1, s_xm[1], s_xl[1])

    with contextlib.ExitStack() as st:
        Es = st.enter_context
        T = TCF
        fc = Es(sb("fc", [128, KD, T], F32))
        x2c = Es(sb("finx2", [128, KD, T], F32))
        cast16 = Es(sb("fic16", [128, KD, T], BF16))
        rows = Es(sb("firows", [1, 6, T], F32))
        oc = Es(sb("oc", [128, KD, T], F32))
        psrA = Es(pst("fipsrA", [128, 512], F32))
        psrB = Es(pst("fipsrB", [128, 512], F32))
        psrg = [None]
        for c in range(NCF if _on("fin") else 0):
            t0 = c * T
            p.dma("sync", "ax", fc[:], _rk(s_xl[1])[:, :, t0:t0 + T])
            p.dma("sync", "ax", x2c[:], _rk(s_x2)[:, :, t0:t0 + T])
            repmm = ln_stats(fc, psrA, psrB, rows, cast16, T, psrg,
                             wait_extra=(("ax", None),))
            p.emit("vector", lambda e: e.tensor_tensor(
                oc[:], fc[:],
                psrA[:, 0:T].unsqueeze(1).to_broadcast([128, KD, T]),
                ALU.subtract),
                wait=(("mm", repmm),), inc=(("vv", 1),))
            p.emit("vector", lambda e: e.tensor_tensor(
                oc[:], oc[:],
                psrB[:, 0:T].unsqueeze(1).to_broadcast([128, KD, T]),
                ALU.mult),
                inc=(("vv", 1),))
            p.emit("vector", lambda e: e.tensor_tensor(
                oc[:], oc[:], x2c[:], ALU.add),
                wait=(("ax", None),), inc=(("vv", 1),))
            ovv = p.cur("vv")
            psrg[0] = ("vv", ovv)
            p.emit("sync", lambda e, t0=t0: e.dma_start(
                out=_rk(outT)[:, :, t0:t0 + T], in_=oc[:]),
                wait=(("vv", ovv),), inc=(("out", 16),))
            p.fence()
        if _DBG is not None:
            srcmap = {"ln1": s_ln1, "x2": s_x2, "xf0": s_xf0,
                      "xm0": s_xm[0], "xl1": s_xl[0], "xm1": s_xm[1],
                      "xl2": s_xl[1]}
            if _DBG in srcmap:
                p.emit("sync", lambda e: e.dma_start(
                    out=_rk(dbg)[:, 0:KD, :], in_=_rk(srcmap[_DBG])[:]),
                    wait=(("sp", None),), inc=(("out", 16),))
            elif _DBG == "ygb":
                p.emit("gpsimd", lambda e: e.dma_start(
                    out=_rk(dbg)[:, 0:KDI, :], in_=_rk(s_ygb)[:]),
                    wait=(("sp", None),), inc=(("out", 16),))
        p.emit("sync", lambda e: e.nop(),
               wait=(("out", p.cur("out")),))
        p.flush_phase()

    perstack.close()
    return nc


_NC_CACHE = {}


def kernel(**inputs):
    staged = _stage_inputs(inputs)
    x = np.asarray(inputs["L0_attn_x"])
    in_maps = []
    for ccore in range(8):
        m = dict(staged)
        m["xT"] = np.ascontiguousarray(x[ccore].T)
        in_maps.append(m)
    if "nc" not in _NC_CACHE:
        _NC_CACHE["nc"] = build_nc()
    nc = _NC_CACHE["nc"]
    res = run_bass_kernel_spmd(nc, in_maps, core_ids=list(range(8)))
    out = np.stack([np.asarray(r["outT"]).T for r in res.results], axis=0)
    if _DBG is not None:
        kernel._dbg = [np.asarray(r.get("dbg")) for r in res.results]
    return out


# version 06
# speedup vs baseline: 1.1687x; 1.1687x over previous
import contextlib
import numpy as np
import ml_dtypes
import concourse.bass as bass
import concourse.mybir as mybir
from concourse.bass_utils import run_bass_kernel_spmd

F32 = mybir.dt.float32
BF16 = mybir.dt.bfloat16
AF = mybir.ActivationFunctionType
ALU = mybir.AluOpType
AXX = mybir.AxisListType.X
BF = ml_dtypes.bfloat16

L = 4096
D = 1024
KD = 8
DIN = 2048
KDI = 16
NST = 16
DTR = 64
HM = 4096
KHM = 32
NH = 8
DH = 64
INNER = 512
KIN = 4
WIN = 512
HID = 2730
HIDP = 2816
KHID = 22
NL = 2

TCM = 256
NCM = L // TCM
TCF = 512
NCF = L // TCF

NEG = -1.0e38
EPS = 1e-5

_DBG = None
_PHASES = None
_FA_STAGE = 9
_FA_SOFT = 9
_FA_SIM = 9
_NCF_LIM = None
_FA_NT = None
_FA_DUMMY = 0

ENGS = ("sync", "gpsimd", "tensor", "vector", "scalar")


class Prog:
    def __init__(self, nc):
        self.nc = nc
        self.sems = {}
        self.cnt = {}
        self.seen = {e: {} for e in ENGS}
        self.ops = {e: [] for e in ENGS}

    def sem(self, name):
        self.sems[name] = self.nc.alloc_semaphore(name)
        self.cnt[name] = 0

    def cur(self, name):
        return self.cnt[name]

    def emit(self, eng, fn, wait=(), inc=()):
        w = tuple((s, self.cnt[s] if v is None else v) for (s, v) in wait)
        self.ops[eng].append((w, fn, tuple(inc)))
        for s, a in inc:
            self.cnt[s] += a

    def dma(self, q, sem, out, in_, wait=()):
        self.emit(q, (lambda e, o=out, i=in_: e.dma_start(out=o, in_=i)),
                  wait, ((sem, 16),))

    def fence(self):
        w = tuple((s, self.cnt[s]) for s in self.cnt)
        for eng in ENGS:
            self.emit(eng, lambda e: e.nop(), wait=w)

    def flush_phase(self):
        ops = self.ops
        self.ops = {e: [] for e in ENGS}
        sems = self.sems
        seen = self.seen
        with self.nc.Block() as blk:
            for eng in ENGS:
                lst = ops[eng]
                if not lst:
                    continue

                def runner(e, lst=lst, eng=eng):
                    sv = seen[eng]
                    for w, fn, incs in lst:
                        for s, v in w:
                            if v > 0 and sv.get(s, 0) < v:
                                e.wait_ge(sems[s], v)
                                sv[s] = v
                        r = fn(e)
                        for s, a in incs:
                            r.then_inc(sems[s], a)
                getattr(blk, eng)(runner)


_CONST_CACHE = {}


def _consts():
    if _CONST_CACHE:
        return _CONST_CACHE
    c = {}
    c["ident"] = np.eye(128).astype(BF)
    c["ones1"] = np.ones((128, 1), dtype=BF)
    c["onesf"] = np.ones((1, 128), dtype=np.float32)
    g8 = np.zeros((4, 8, 128), np.float32)
    for j in range(4):
        g8[j, 2 * j, 0:64] = 1.0
        g8[j, 2 * j + 1, 64:128] = 1.0
    c["g8"] = g8
    e8 = np.zeros((4, 128, 8), np.float32)
    for k in range(4):
        e8[k, 0:64, 2 * k] = 1.0
        e8[k, 64:128, 2 * k + 1] = 1.0
    c["e8"] = e8.astype(BF)
    i_ = np.arange(128)[:, None]
    j_ = np.arange(128)[None, :]
    m_own = np.where(j_ > i_, NEG, 0.0).astype(np.float32)
    m_prev = np.where(j_ < i_, NEG, 0.0).astype(np.float32)
    c["maskownT"] = np.ascontiguousarray(m_own.T).astype(BF)
    c["maskprevT"] = np.ascontiguousarray(m_prev.T).astype(BF)
    R = np.zeros((128, 128), np.float32)
    for d in range(128):
        dl = d % 64
        base = d - dl
        if dl < 32:
            R[d, base + dl + 32] = -1.0
        else:
            R[d, base + dl - 32] = 1.0
    c["rotmT"] = np.ascontiguousarray(R.T).astype(BF)
    half = np.arange(0, DH, 2, dtype=np.float64)
    invf = 1.0 / (10000.0 ** (half / DH))
    t = np.arange(L, dtype=np.float64)
    fr = t[:, None] * invf[None, :]
    freqs = np.concatenate([fr, fr], axis=1)
    base = (half + 0.4 * DH) / (1.4 * DH)
    power = (t - L // 2) / float(WIN // 2)
    scale = base[None, :] ** power[:, None]
    scale = np.concatenate([scale, scale], axis=1)
    cos = np.cos(freqs)
    sin = np.sin(freqs)

    def two(x):
        xt = x.T
        return np.ascontiguousarray(
            np.concatenate([xt, xt], axis=0)).astype(np.float32)
    c["csq0"] = two(cos * scale)
    c["ssq0"] = two(sin * scale)
    c["csk0"] = two(cos / scale)
    c["ssk0"] = two(sin / scale)
    _CONST_CACHE.update(c)
    return c


def _stage_inputs(inputs):
    A = np.ascontiguousarray
    g = lambda k: np.asarray(inputs[k])
    s = {}
    s["inw"] = A(g("L0_attn_in_proj_w").T)
    s["convw"] = A(g("L0_attn_conv_w"))
    s["convb"] = A(g("L0_attn_conv_b"))
    s["xpw"] = A(g("L0_attn_x_proj_w").T)
    s["dtw"] = A(g("L0_attn_dt_proj_w").T)
    s["dtb"] = A(g("L0_attn_dt_proj_b"))
    s["alog"] = A(g("L0_attn_A_log"))
    s["dp"] = A(g("L0_attn_Dp"))
    s["outw"] = A(g("L0_attn_out_proj_w").T)
    s["w1m"] = A(g("L0_attn_w1").T)
    s["b1m"] = A(g("L0_attn_b1"))
    s["w2m"] = A(g("L0_attn_w2").T)
    s["b2m"] = A(g("L0_attn_b2"))
    s["wqkv"] = A(g("L0_ffn_wqkv"))
    s["qs"] = A(g("L0_ffn_q_scale"))
    s["ks"] = A(g("L0_ffn_k_scale"))
    s["wo"] = A(g("L0_ffn_wo"))
    s["lng"] = A(g("L0_ffn_ln_g"))
    s["lnb"] = A(g("L0_ffn_ln_b"))
    w1f = g("L0_ffn_w1")
    pad = HIDP - HID
    s["w1fa"] = A(np.pad(w1f[:, :, :HID], ((0, 0), (0, 0), (0, pad))))
    s["w1fg"] = A(np.pad(w1f[:, :, HID:], ((0, 0), (0, 0), (0, pad))))
    b1f = g("L0_ffn_b1")
    s["b1fa"] = A(np.pad(b1f[:, :HID], ((0, 0), (0, pad))))
    s["b1fg"] = A(np.pad(b1f[:, HID:], ((0, 0), (0, pad))))
    s["w2f"] = A(np.pad(g("L0_ffn_w2"), ((0, 0), (0, pad), (0, 0))))
    s["b2f"] = A(g("L0_ffn_b2"))
    s.update(_consts())
    return s


def _rk(ap, p=128):
    return ap.rearrange("(k p) t -> p k t", p=p)


def _on(name):
    return _PHASES is None or name in _PHASES


def build_nc():
    nc = bass.Bass("TRN2", target_bir_lowering=False, debug=False,
                   detect_race_conditions=False)
    nc.ctx.enter_context(
        nc.allow_non_contiguous_dma(reason="small vector/stat loads"))
    dt_ = nc.dram_tensor

    xT = dt_("xT", [D, L], F32, kind="ExternalInput").ap()
    inw = dt_("inw", [D, HM], F32, kind="ExternalInput").ap()
    convw = dt_("convw", [DIN, 3], F32, kind="ExternalInput").ap()
    convb = dt_("convb", [DIN], F32, kind="ExternalInput").ap()
    xpw = dt_("xpw", [DIN, 96], F32, kind="ExternalInput").ap()
    dtw = dt_("dtw", [DTR, DIN], F32, kind="ExternalInput").ap()
    dtb = dt_("dtb", [DIN], F32, kind="ExternalInput").ap()
    alog = dt_("alog", [DIN, NST], F32, kind="ExternalInput").ap()
    dpv = dt_("dp", [DIN], F32, kind="ExternalInput").ap()
    outw = dt_("outw", [DIN, D], F32, kind="ExternalInput").ap()
    w1m = dt_("w1m", [D, HM], F32, kind="ExternalInput").ap()
    b1m = dt_("b1m", [HM], F32, kind="ExternalInput").ap()
    w2m = dt_("w2m", [HM, D], F32, kind="ExternalInput").ap()
    b2m = dt_("b2m", [D], F32, kind="ExternalInput").ap()
    wqkv = dt_("wqkv", [NL, D, 3 * INNER], F32, kind="ExternalInput").ap()
    qs = dt_("qs", [NL, DH], F32, kind="ExternalInput").ap()
    ks = dt_("ks", [NL, DH], F32, kind="ExternalInput").ap()
    wo = dt_("wo", [NL, INNER, D], F32, kind="ExternalInput").ap()
    lng = dt_("lng", [NL, D], F32, kind="ExternalInput").ap()
    lnbv = dt_("lnb", [NL, D], F32, kind="ExternalInput").ap()
    w1fa = dt_("w1fa", [NL, D, HIDP], F32, kind="ExternalInput").ap()
    w1fg = dt_("w1fg", [NL, D, HIDP], F32, kind="ExternalInput").ap()
    b1fa = dt_("b1fa", [NL, HIDP], F32, kind="ExternalInput").ap()
    b1fg = dt_("b1fg", [NL, HIDP], F32, kind="ExternalInput").ap()
    w2f = dt_("w2f", [NL, HIDP, D], F32, kind="ExternalInput").ap()
    b2f = dt_("b2f", [NL, D], F32, kind="ExternalInput").ap()
    ident = dt_("ident", [128, 128], BF16, kind="ExternalInput").ap()
    ones1 = dt_("ones1", [128, 1], BF16, kind="ExternalInput").ap()
    onesf = dt_("onesf", [1, 128], F32, kind="ExternalInput").ap()
    g8c = dt_("g8", [4, 8, 128], F32, kind="ExternalInput").ap()
    e8 = dt_("e8", [4, 128, 8], BF16, kind="ExternalInput").ap()
    maskownT = dt_("maskownT", [128, 128], BF16, kind="ExternalInput").ap()
    maskprevT = dt_("maskprevT", [128, 128], BF16, kind="ExternalInput").ap()
    rotmT = dt_("rotmT", [128, 128], BF16, kind="ExternalInput").ap()
    csq0 = dt_("csq0", [128, L], F32, kind="ExternalInput").ap()
    ssq0 = dt_("ssq0", [128, L], F32, kind="ExternalInput").ap()
    csk0 = dt_("csk0", [128, L], F32, kind="ExternalInput").ap()
    ssk0 = dt_("ssk0", [128, L], F32, kind="ExternalInput").ap()

    outT = dt_("outT", [D, L], F32, kind="ExternalOutput").ap()

    s_inwb = dt_("s_inwb", [D, HM], BF16).ap()
    s_outwb = dt_("s_outwb", [DIN, D], BF16).ap()
    s_w1mb = dt_("s_w1mb", [D, HM], BF16).ap()
    s_w2mb = dt_("s_w2mb", [HM, D], BF16).ap()
    s_w1fab = dt_("s_w1fab", [NL, D, HIDP], BF16).ap()
    s_w1fgb = dt_("s_w1fgb", [NL, D, HIDP], BF16).ap()
    s_w2fb = dt_("s_w2fb", [NL, HIDP, D], BF16).ap()
    s_ln1 = dt_("s_ln1", [D, L], F32).ap()
    s_ygb = dt_("s_ygb", [DIN, L], BF16).ap()
    s_x2 = dt_("s_x2", [D, L], F32).ap()
    s_xf0 = dt_("s_xf0", [D, L], F32).ap()
    s_xm = [dt_("s_xm0", [D, L], F32).ap(), dt_("s_xm1", [D, L], F32).ap()]
    s_xl = [dt_("s_xl1", [D, L], F32).ap(), dt_("s_xl2", [D, L], F32).ap()]
    s_dbc = dt_("s_dbc", [32, TCM], F32).ap()

    dbg = None
    if _DBG is not None:
        dbg = dt_("dbg", [DIN, L], F32, kind="ExternalOutput").ap()

    p = Prog(nc)
    for s in ("mm", "vv", "ss", "out", "ax", "sp",
              "swiA", "swiB", "swoA", "swoB", "sw1A", "sw1B", "sw2A", "sw2B",
              "swaA", "swaB", "swgA", "swgB", "swfA", "swfB",
              "sbA", "sbB", "gpc", "gbA", "gbB", "gbc", "gxf", "gw"):
        p.sem(s)

    sb = nc.sbuf_tensor
    pst = nc.psum_tensor
    perstack = contextlib.ExitStack()
    E = perstack.enter_context
    identt = E(sb("identt", [128, 128], BF16))
    onest = E(sb("onest", [128, 1], BF16))
    onesft = E(sb("onesft", [1, 128], F32))
    g8t = E(sb("g8t", [8, 4, 128], F32))
    e8t = E(sb("e8t", [128, 4, 8], BF16))
    maskot = E(sb("maskot", [128, 128], BF16))
    maskpt = E(sb("maskpt", [128, 128], BF16))
    rotmt = E(sb("rotmt", [128, 128], BF16))
    cwt = E(sb("cwt", [128, KDI, 3], F32))
    cbt = E(sb("cbt", [128, KDI], F32))
    dtbt = E(sb("dtbt", [128, KDI], F32))
    dpt = E(sb("dpt", [128, KDI], F32))
    negP = E(sb("negP", [128, KDI, NST], F32))
    carry = E(sb("carry", [128, KDI, NST], F32))
    carry2 = E(sb("carry2", [128, KDI, 2], F32))
    b1mv = E(sb("b1mv", [128, KHM], F32))
    b2mv = E(sb("b2mv", [128, KD], F32))
    lngv = E(sb("lngv", [128, NL, KD], F32))
    lnbt = E(sb("lnbt", [128, NL, KD], F32))
    b1fav = E(sb("b1fav", [128, NL, KHID], F32))
    b1fgv = E(sb("b1fgv", [128, NL, KHID], F32))
    b2fv = E(sb("b2fv", [128, NL, KD], F32))
    qs2 = E(sb("qs2", [128, NL], F32))
    qs2r = E(sb("qs2r", [128, NL], F32))
    ks2 = E(sb("ks2", [128, NL], F32))
    ks2r = E(sb("ks2r", [128, NL], F32))
    epsc = E(sb("epsc", [128, 1], F32))
    onec = E(sb("onec", [128, 1], F32))
    xpwb = E(sb("xpwb", [128, KDI, 96], BF16))
    dtwb = E(sb("dtwb", [64, KDI, 128], BF16))

    def ln_stats(src, psrA, psrB, rows, cast16, T, psr_guard,
                 wait_extra=(), shift=None, state=None):
        castg = state.get("cast") if state is not None else None
        rowsg = state.get("rows") if state is not None else None
        p.emit("scalar", lambda e: e.copy(out=cast16[:], in_=src[:]),
               wait=wait_extra + ((castg,) if castg else ()), inc=(("ss", 1),))
        scast = p.cur("ss")

        def rhs(buf, ct):
            if shift is None:
                return buf[:, ct, 0:T]
            return buf[:, ct, 1:T + 1] if ct < KD // 2 else buf[:, ct, 0:T]

        for ct in range(KD):
            p.emit("tensor",
                   lambda e, ct=ct: e.matmul(
                       psrA[0:1, 0:T], onest[:], rhs(cast16, ct),
                       start=(ct == 0), stop=(ct == KD - 1)),
                   wait=((psr_guard[0], ("ss", scast)) if psr_guard[0]
                         else (("ss", scast),)) if ct == 0 else (),
                   inc=((("mm", 1),) if ct == KD - 1 else ()))
        amm = p.cur("mm")
        p.emit("scalar", lambda e: e.activation(cast16[:], src[:], AF.Square),
               wait=(("mm", amm),), inc=(("ss", 1),))
        scast = p.cur("ss")
        for ct in range(KD):
            p.emit("tensor",
                   lambda e, ct=ct: e.matmul(
                       psrB[0:1, 0:T], onest[:], rhs(cast16, ct),
                       start=(ct == 0), stop=(ct == KD - 1)),
                   wait=(("ss", scast),) if ct == 0 else (),
                   inc=((("mm", 1),) if ct == KD - 1 else ()))
        smm = p.cur("mm")
        p.emit("vector", lambda e: e.tensor_scalar(
            rows[:, 0, :], psrA[0:1, 0:T], 1.0 / D, None, ALU.mult),
            wait=(("mm", smm),) + ((rowsg,) if rowsg else ()), inc=(("vv", 1),))
        p.emit("vector", lambda e: e.tensor_scalar(
            rows[:, 1, :], psrB[0:1, 0:T], 1.0 / D, None, ALU.mult),
            inc=(("vv", 1),))
        p.emit("vector", lambda e: e.tensor_tensor(
            rows[:, 2, :], rows[:, 0, :], rows[:, 0, :], ALU.mult),
            inc=(("vv", 1),))
        p.emit("vector", lambda e: e.tensor_tensor(
            rows[:, 1, :], rows[:, 1, :], rows[:, 2, :], ALU.subtract),
            inc=(("vv", 1),))
        svv = p.cur("vv")
        p.emit("scalar", lambda e: e.activation(
            rows[:, 2, :], rows[:, 1, :], AF.Sqrt, bias=epsc[0:1, :]),
            wait=(("vv", svv),), inc=(("ss", 1),))
        p.emit("vector", lambda e: e.reciprocal(rows[:, 3, :], rows[:, 2, :]),
               wait=(("ss", p.cur("ss")), ("vv", None)), inc=(("vv", 1),))
        rvv = p.cur("vv")
        p.emit("tensor", lambda e: e.matmul(
            psrA[:, 0:T], onesft[:], rows[:, 0, :], start=True, stop=True),
            wait=(("vv", rvv),))
        p.emit("tensor", lambda e: e.matmul(
            psrB[:, 0:T], onesft[:], rows[:, 3, :], start=True, stop=True),
            inc=(("mm", 1),))
        repmm = p.cur("mm")
        if state is not None:
            state["cast"] = ("mm", smm)
            state["rows"] = ("mm", repmm)
            state["scast"] = ("ss", scast)
        return repmm

    with contextlib.ExitStack() as st:
        Es = st.enter_context
        alogt = Es(sb("alogt", [128, KDI, NST], F32))
        bb = [Es(sb(f"bb{i}", [128, 4096], BF16)) for i in range(2)]
        def d(q, out, in_):
            p.dma(q, "ax" if q == "sync" else "gpc", out, in_)
        d("sync", identt[:], ident)
        d("sync", onest[:], ones1)
        d("sync", onesft[:], onesf)
        d("sync", g8t[:], g8c.transpose([1, 0, 2]))
        d("sync", e8t[:], e8.transpose([1, 0, 2]))
        d("sync", maskot[:], maskownT)
        d("sync", maskpt[:], maskprevT)
        d("sync", rotmt[:], rotmT)
        d("sync", cwt[:], convw.rearrange("(k p) c -> p k c", p=128))
        d("sync", cbt[:], convb.rearrange("(k p) -> p k", p=128))
        d("sync", dtbt[:], dtb.rearrange("(k p) -> p k", p=128))
        d("sync", dpt[:], dpv.rearrange("(k p) -> p k", p=128))
        d("sync", b1mv[:], b1m.rearrange("(k p) -> p k", p=128))
        d("sync", b2mv[:], b2m.rearrange("(k p) -> p k", p=128))
        d("sync", lngv[:], lng.rearrange("l (k p) -> p l k", p=128))
        d("sync", lnbt[:], lnbv.rearrange("l (k p) -> p l k", p=128))
        d("sync", b1fav[:], b1fa.rearrange("l (k p) -> p l k", p=128))
        d("sync", b1fgv[:], b1fg.rearrange("l (k p) -> p l k", p=128))
        d("sync", b2fv[:], b2f.rearrange("l (k p) -> p l k", p=128))
        for l in range(NL):
            for h in range(2):
                d("sync", qs2[64 * h:64 * h + 64, l:l + 1], qs[l, :].unsqueeze(1))
                d("sync", ks2[64 * h:64 * h + 64, l:l + 1], ks[l, :].unsqueeze(1))
                d("sync", qs2r[64 * h:64 * h + 32, l:l + 1], qs[l, 32:64].unsqueeze(1))
                d("sync", qs2r[64 * h + 32:64 * h + 64, l:l + 1], qs[l, 0:32].unsqueeze(1))
                d("sync", ks2r[64 * h:64 * h + 32, l:l + 1], ks[l, 32:64].unsqueeze(1))
                d("sync", ks2r[64 * h + 32:64 * h + 64, l:l + 1], ks[l, 0:32].unsqueeze(1))
        d("gpsimd", xpwb[:], xpw.rearrange("(k p) o -> p k o", p=128))
        d("gpsimd", dtwb[:], dtw.rearrange("c (k p) -> c k p", p=128))
        d("sync", alogt[:], alog.rearrange("(k p) n -> p k n", p=128))
        p.emit("scalar", lambda e: e.activation(negP[:], alogt[:], AF.Exp),
               wait=(("ax", None), ("gpc", None)), inc=(("ss", 1),))
        p.emit("vector", lambda e: e.tensor_scalar(
            negP[:], negP[:], -1.0, None, ALU.mult),
            wait=(("ss", None),), inc=(("vv", 1),))
        p.emit("vector", lambda e: e.memset(epsc[:], EPS), inc=(("vv", 1),))
        p.emit("vector", lambda e: e.memset(onec[:], 1.0), inc=(("vv", 1),))
        p.emit("vector", lambda e: e.memset(carry[:], 0.0), inc=(("vv", 1),))
        p.emit("vector", lambda e: e.memset(carry2[:], 0.0), inc=(("vv", 1),))

        jobs = []

        def add_rows(src, dst, rows_, cols):
            for r0 in range(0, rows_, 128):
                jobs.append((src[r0:r0 + 128, :], dst[r0:r0 + 128, :], cols))

        add_rows(inw, s_inwb, D, HM)
        add_rows(outw, s_outwb, DIN, D)
        add_rows(w1m, s_w1mb, D, HM)
        add_rows(w2m, s_w2mb, HM, D)
        for l in range(NL):
            add_rows(w1fa[l], s_w1fab[l], D, HIDP)
            add_rows(w1fg[l], s_w1fgb[l], D, HIDP)
            add_rows(w2f[l], s_w2fb[l], HIDP, D)
        for i, (srcap, dstap, cols) in enumerate(jobs):
            b = "AB"[i % 2]
            buf = bb[i % 2]
            p.dma("gpsimd", "gb" + b, buf[:, 0:cols], srcap,
                  wait=(("sb" + b, None),))
            p.dma("sync", "sb" + b, dstap, buf[:, 0:cols],
                  wait=(("gb" + b, None),))
        p.fence()
        p.flush_phase()

    if not _on("m1"):
        ncm_eff = 0
    else:
        ncm_eff = NCM
    with contextlib.ExitStack() as st:
        Es = st.enter_context
        T = TCM
        x0c = Es(sb("x0c", [128, KD, T], F32))
        cast16 = Es(sb("cast16", [128, KD, T], BF16))
        rows = Es(sb("rows", [1, 4, T], F32))
        ln1c = Es(sb("ln1c", [128, KD, T], F32))
        lnb16 = Es(sb("lnb16", [128, KD, T], BF16))
        wbuf = [Es(sb(f"m1wb{i}", [128, KD, 128], BF16)) for i in range(2)]
        xpr = Es(sb("xpr", [128, KDI, T + 2], F32))
        szt = Es(sb("szt", [128, KDI, T], F32))
        dtt = Es(sb("dtt", [128, KDI, T], F32))
        xp2 = Es(sb("xp2", [128, KDI, T], F32))
        xp2b = Es(sb("xp2b", [128, KDI, T], BF16))
        dbc = Es(sb("dbc", [96, T], F32))
        dtinb = Es(sb("dtinb", [64, T], BF16))
        bcrep = Es(sb("bcrep", [128, 32, T], BF16))
        at2 = [Es(sb(f"at{i}", [128, NST, T], F32)) for i in range(2)]
        btl = Es(sb("btl", [128, NST, T], BF16))
        ht = Es(sb("ht", [128, NST, T], F32))
        tmp16w = Es(sb("tmp16w", [128, NST], F32))
        dtx = Es(sb("dtx", [128, T], F32))
        ybuf = Es(sb("ybuf", [128, T], F32))
        ct1 = Es(sb("ct1", [128, T], F32))
        ct2 = Es(sb("ct2", [128, T], F32))
        etmp = Es(sb("etmp", [128, T], F32))
        ygb = Es(sb("ygb", [128, KDI, T], BF16))
        ps = [Es(pst(f"m1ps{i}", [128, 512], F32)) for i in range(4)]
        psrA = Es(pst("m1psrA", [128, 512], F32))
        psrB = Es(pst("m1psrB", [128, 512], F32))
        bankg = [None] * 4
        wguard = [None, None]
        psrg = [None]

        def bw(g):
            return (g,) if g else ()

        brep = bcrep[:, 0:16, :]
        crep = bcrep[:, 16:32, :]
        lnst = {}
        atg = [None, None]
        gx0 = ()
        glnb = None
        ghalo = None
        gsd = None
        for c in range(ncm_eff):
            t0 = c * T
            p.dma("sync", "ax", x0c[:], _rk(xT)[:, :, t0:t0 + T], wait=gx0)
            repmm = ln_stats(x0c, psrA, psrB, rows, cast16, T, psrg,
                             wait_extra=(("ax", None),), state=lnst)
            p.emit("vector", lambda e: e.tensor_tensor(
                ln1c[:], x0c[:],
                psrA[:, 0:T].unsqueeze(1).to_broadcast([128, KD, T]),
                ALU.subtract),
                wait=(("mm", repmm), ("sp", None)), inc=(("vv", 1),))
            p.emit("vector", lambda e: e.tensor_tensor(
                ln1c[:], ln1c[:],
                psrB[:, 0:T].unsqueeze(1).to_broadcast([128, KD, T]),
                ALU.mult),
                inc=(("vv", 1),))
            lnvv = p.cur("vv")
            psrg[0] = ("vv", lnvv)
            gx0 = (lnst["scast"], ("vv", lnvv))
            p.emit("scalar", lambda e: e.copy(out=lnb16[:], in_=ln1c[:]),
                   wait=(("vv", lnvv),) + ((glnb,) if glnb else ()),
                   inc=(("ss", 1),))
            lnss = p.cur("ss")
            p.dma("sync", "sp", _rk(s_ln1)[:, :, t0:t0 + T], ln1c[:],
                  wait=(("vv", lnvv),))
            p.emit("scalar", lambda e: e.copy(out=xpr[:, :, 0:2], in_=carry2[:]),
                   wait=((ghalo,) if ghalo else ()),
                   inc=(("ss", 1),))
            for ot in range(32):
                wb = wbuf[ot % 2]
                wsem = "swi" + "AB"[ot % 2]
                p.dma("sync", wsem, wb[:], _rk(s_inwb)[:, :, ot * 128:(ot + 1) * 128],
                      wait=bw(wguard[ot % 2]))
                ldw = (wsem, p.cur(wsem))
                bk = ot % 4
                for ct in range(KD):
                    p.emit("tensor",
                           lambda e, ct=ct, wb=wb, bk=bk: e.matmul(
                               ps[bk][:, 0:T], wb[:, ct, :],
                               lnb16[:, ct, :],
                               start=(ct == 0), stop=(ct == KD - 1)),
                           wait=(bw(bankg[bk]) + (ldw, ("ss", lnss))) if ct == 0 else (),
                           inc=((("mm", 1),) if ct == KD - 1 else ()))
                mmc = p.cur("mm")
                kk = ot if ot < KDI else ot - KDI
                sdw = ((("vv", gsd[kk]),) if gsd is not None else ())
                if ot < KDI:
                    p.emit("scalar",
                           lambda e, ot=ot, bk=bk: e.copy(
                               out=xpr[:, ot, 2:2 + T], in_=ps[bk][:, 0:T]),
                           wait=(("mm", mmc),) + sdw, inc=(("ss", 1),))
                else:
                    p.emit("scalar",
                           lambda e, ot=ot, bk=bk: e.activation(
                               szt[:, ot - KDI, :], ps[bk][:, 0:T], AF.Silu),
                           wait=(("mm", mmc),) + sdw, inc=(("ss", 1),))
                bankg[bk] = ("ss", p.cur("ss"))
                wguard[ot % 2] = ("mm", p.cur("mm"))
            glnb = ("mm", p.cur("mm"))
            p.emit("scalar", lambda e: e.copy(out=carry2[:], in_=xpr[:, :, T:T + 2]),
                   inc=(("ss", 1),))
            for k in range(KDI):
                p.emit("scalar",
                       lambda e, k=k: e.activation(
                           ct1[:], xpr[:, k, 0:T], AF.Identity,
                           scale=cwt[:, k, 0:1]),
                       inc=(("ss", 1),))
                s1 = p.cur("ss")
                p.emit("vector",
                       lambda e, k=k: e.scalar_tensor_tensor(
                           ct2[:], xpr[:, k, 1:1 + T], cwt[:, k, 1:2], ct1[:],
                           ALU.mult, ALU.add),
                       wait=(("ss", s1),), inc=(("vv", 1),))
                p.emit("vector",
                       lambda e, k=k: e.scalar_tensor_tensor(
                           ct1[:], xpr[:, k, 2:2 + T], cwt[:, k, 2:3], ct2[:],
                           ALU.mult, ALU.add),
                       inc=(("vv", 1),))
                cvv = p.cur("vv")
                p.emit("scalar",
                       lambda e, k=k: e.activation(
                           xp2[:, k, :], ct1[:], AF.Silu, bias=cbt[:, k:k + 1]),
                       wait=(("vv", cvv),), inc=(("ss", 1),))
                p.emit("scalar",
                       lambda e, k=k: e.copy(out=xp2b[:, k, :], in_=xp2[:, k, :]),
                       inc=(("ss", 1),))
            cvss = p.cur("ss")
            ghalo = ("vv", p.cur("vv"))
            for ct in range(KDI):
                p.emit("tensor",
                       lambda e, ct=ct: e.matmul(
                           psrA[0:96, 0:T], xpwb[:, ct, :], xp2b[:, ct, :],
                           start=(ct == 0), stop=(ct == KDI - 1)),
                       wait=(bw(psrg[0]) + (("ss", cvss),)) if ct == 0 else (),
                       inc=((("mm", 1),) if ct == KDI - 1 else ()))
            xpmm = p.cur("mm")
            p.emit("scalar", lambda e: e.copy(out=dbc[:], in_=psrA[0:96, 0:T]),
                   wait=(("mm", xpmm), ("sp", None)), inc=(("ss", 1),))
            p.emit("scalar", lambda e: e.copy(out=dtinb[:], in_=psrA[0:64, 0:T]),
                   inc=(("ss", 1),))
            psrg[0] = ("ss", p.cur("ss"))
            dbss = p.cur("ss")
            p.dma("sync", "sp", s_dbc, dbc[64:96, :], wait=(("ss", dbss),))
            p.dma("gpsimd", "gbc", bcrep[:],
                  s_dbc.unsqueeze(0).partition_broadcast(128),
                  wait=(("sp", None),) + ((("vv", gsd[KDI - 1]),) if gsd is not None else ()))
            bcdg = p.cur("gbc")
            for ot in range(KDI):
                bk = ot % 4
                p.emit("tensor",
                       lambda e, ot=ot, bk=bk: e.matmul(
                           ps[bk][:, 0:T], dtwb[:, ot, :], dtinb[:],
                           start=True, stop=True),
                       wait=bw(bankg[bk]) + ((("ss", dbss),) if ot == 0 else ()),
                       inc=(("mm", 1),))
                mmc = p.cur("mm")
                p.emit("scalar",
                       lambda e, ot=ot, bk=bk: e.activation(
                           etmp[:], ps[bk][:, 0:T], AF.Exp,
                           bias=dtbt[:, ot:ot + 1]),
                       wait=(("mm", mmc),), inc=(("ss", 1),))
                p.emit("scalar",
                       lambda e, ot=ot: e.activation(
                           dtt[:, ot, :], etmp[:], AF.Ln, bias=onec[:]),
                       inc=(("ss", 1),))
                bankg[bk] = ("ss", p.cur("ss"))
            sd = [0] * KDI
            for k in range(KDI):
                at = at2[k % 2]
                aw = atg[k % 2]
                for n in range(NST):
                    p.emit("scalar",
                           lambda e, k=k, n=n, at=at: e.activation(
                               at[:, n, :], dtt[:, k, :], AF.Exp,
                               scale=negP[:, k, n:n + 1]),
                           wait=(bw(aw) if n == 0 else ()),
                           inc=((("ss", 1),) if n == NST - 1 else ()))
                ass = p.cur("ss")
                p.emit("vector",
                       lambda e, k=k: e.tensor_tensor(
                           dtx[:], dtt[:, k, :], xp2[:, k, :], ALU.mult),
                       wait=(("ss", ass),), inc=(("vv", 1),))
                p.emit("vector",
                       lambda e: e.tensor_tensor(
                           btl[:], dtx[:].unsqueeze(1).to_broadcast([128, NST, T]),
                           brep, ALU.mult),
                       wait=(("gbc", bcdg),), inc=(("vv", 1),))
                p.emit("vector",
                       lambda e, k=k, at=at: e.tensor_tensor(
                           tmp16w[:], at[:, :, 0], carry[:, k, :], ALU.mult),
                       inc=(("vv", 1),))
                p.emit("vector",
                       lambda e: e.tensor_tensor(
                           btl[:, :, 0], btl[:, :, 0], tmp16w[:], ALU.add),
                       inc=(("vv", 1),))
                p.emit("vector", lambda e, at=at: e.memset(at[:, :, 0], 0.0),
                       inc=(("vv", 1),))
                p.emit("vector",
                       lambda e, at=at: e.tensor_tensor_scan(
                           ht[:].rearrange("p n t -> p (n t)"),
                           at[:].rearrange("p n t -> p (n t)"),
                           btl[:].rearrange("p n t -> p (n t)"),
                           0.0, ALU.mult, ALU.add),
                       inc=(("vv", 1),))
                p.emit("vector",
                       lambda e, k=k: e.tensor_copy(carry[:, k, :], ht[:, :, T - 1]),
                       inc=(("vv", 1),))
                p.emit("vector",
                       lambda e: e.tensor_tensor(btl[:], ht[:], crep, ALU.mult),
                       inc=(("vv", 1),))
                atg[k % 2] = ("vv", p.cur("vv"))
                p.emit("vector",
                       lambda e: e.tensor_tensor(
                           btl[:, 0:8, :], btl[:, 0:8, :], btl[:, 8:16, :],
                           ALU.add),
                       inc=(("vv", 1),))
                p.emit("vector",
                       lambda e: e.tensor_tensor(
                           btl[:, 0:4, :], btl[:, 0:4, :], btl[:, 4:8, :],
                           ALU.add),
                       inc=(("vv", 1),))
                p.emit("vector",
                       lambda e: e.tensor_tensor(
                           btl[:, 0:2, :], btl[:, 0:2, :], btl[:, 2:4, :],
                           ALU.add),
                       inc=(("vv", 1),))
                p.emit("vector",
                       lambda e: e.tensor_tensor(
                           ybuf[:], btl[:, 0, :], btl[:, 1, :], ALU.add),
                       inc=(("vv", 1),))
                p.emit("vector",
                       lambda e, k=k: e.scalar_tensor_tensor(
                           ybuf[:], xp2[:, k, :], dpt[:, k:k + 1], ybuf[:],
                           ALU.mult, ALU.add),
                       inc=(("vv", 1),))
                p.emit("vector",
                       lambda e, k=k: e.tensor_tensor(
                           ygb[:, k, :], ybuf[:], szt[:, k, :], ALU.mult),
                       wait=((("sp", None),) if k == 0 else ()),
                       inc=(("vv", 1),))
                sd[k] = p.cur("vv")
            gsd = sd
            ygvv = p.cur("vv")
            p.dma("sync", "sp", _rk(s_ygb)[:, :, t0:t0 + T], ygb[:],
                  wait=(("vv", ygvv),))
            if _DBG == "m1xp2":
                p.dma("sync", "sp", _rk(dbg)[:, 0:KDI, t0:t0 + T], xp2[:],
                      wait=(("ss", cvss),))
            if _DBG == "m1dt":
                p.dma("sync", "sp", _rk(dbg)[:, 0:KDI, t0:t0 + T], dtt[:],
                      wait=(("vv", ygvv),))
        p.fence()
        p.flush_phase()

    ncm2_eff = (L // 512) if _on("m2") else 0
    with contextlib.ExitStack() as st:
        Es = st.enter_context
        T = 512
        ygc = Es(sb("ygc", [128, KDI, T], BF16))
        ln1c = Es(sb("m2ln1", [128, KD, T], F32))
        x0c = Es(sb("m2x0", [128, KD, T], F32))
        x1c = Es(sb("x1c", [128, KD, T], F32))
        x1b = Es(sb("x1b", [128, KD, T], BF16))
        hbt = Es(sb("hbt", [128, KHM, T], BF16))
        aoc = Es(sb("aoc", [128, KD, T], F32))
        x2c = Es(sb("x2c", [128, KD, T], F32))
        cast16 = Es(sb("m2c16", [128, KD, T], BF16))
        rows = Es(sb("m2rows", [1, 4, T], F32))
        owb = [Es(sb(f"owb{i}", [128, KDI, 128], BF16)) for i in range(2)]
        w1b = [Es(sb(f"m2w1b{i}", [128, KD, 512], BF16)) for i in range(2)]
        w2b = [Es(sb(f"m2w2b{i}", [128, KHM, 128], BF16)) for i in range(2)]
        ps = [Es(pst(f"m2ps{i}", [128, 512], F32)) for i in range(4)]
        psrA = Es(pst("m2psrA", [128, 512], F32))
        psrB = Es(pst("m2psrB", [128, 512], F32))
        bankg = [None] * 4
        wguard = {}
        psrg = [None]

        def bw(g):
            return (g,) if g else ()

        lnst = {}
        gld = ()
        gx1c = None
        gx1b = None
        ghbt = None
        gaoc = None
        gx2w = ()
        for c in range(ncm2_eff):
            t0 = c * T
            p.dma("sync", "ax", ygc[:], _rk(s_ygb)[:, :, t0:t0 + T], wait=gld)
            p.dma("sync", "ax", ln1c[:], _rk(s_ln1)[:, :, t0:t0 + T])
            p.dma("sync", "ax", x0c[:], _rk(xT)[:, :, t0:t0 + T])
            ldx = ("ax", p.cur("ax"))
            for ot in range(KD):
                wb = owb[ot % 2]
                wsem = "swo" + "AB"[ot % 2]
                p.dma("sync", wsem, wb[:], _rk(s_outwb)[:, :, ot * 128:(ot + 1) * 128],
                      wait=bw(wguard.get(("ow", ot % 2))))
                ldw = (wsem, p.cur(wsem))
                bk = ot % 4
                for ct in range(KDI):
                    p.emit("tensor",
                           lambda e, ct=ct, wb=wb, bk=bk: e.matmul(
                               ps[bk][:, 0:T], wb[:, ct, :], ygc[:, ct, :],
                               start=(ct == 0), stop=(ct == KDI - 1)),
                           wait=(bw(bankg[bk]) + (ldw, ("ax", None))) if ct == 0 else (),
                           inc=((("mm", 1),) if ct == KDI - 1 else ()))
                mmc = p.cur("mm")
                wguard[("ow", ot % 2)] = ("mm", mmc)
                p.emit("vector",
                       lambda e, ot=ot, bk=bk: e.tensor_tensor(
                           x1c[:, ot, :], ps[bk][:, 0:T], ln1c[:, ot, :], ALU.add),
                       wait=(("mm", mmc),) + (bw(gx1c) if ot == 0 else ()),
                       inc=(("vv", 1),))
                bankg[bk] = ("vv", p.cur("vv"))
            outmm = p.cur("mm")
            x1vv = p.cur("vv")
            p.emit("scalar", lambda e: e.copy(out=x1b[:], in_=x1c[:]),
                   wait=(("vv", x1vv),) + bw(gx1b), inc=(("ss", 1),))
            x1ss = p.cur("ss")
            gx1c = ("ss", x1ss)
            for g in range(KHM // 4):
                wb = w1b[g % 2]
                wsem = "sw1" + "AB"[g % 2]
                p.dma("sync", wsem, wb[:], _rk(s_w1mb)[:, :, g * 512:(g + 1) * 512],
                      wait=bw(wguard.get(("w1", g % 2))))
                ldw = (wsem, p.cur(wsem))
                for otl in range(4):
                    ot = 4 * g + otl
                    for ct in range(KD):
                        p.emit("tensor",
                               lambda e, ct=ct, wb=wb, otl=otl: e.matmul(
                                   ps[otl][:, 0:T],
                                   wb[:, ct, 128 * otl:128 * otl + 128],
                                   x1b[:, ct, :],
                                   start=(ct == 0), stop=(ct == KD - 1)),
                               wait=(bw(bankg[otl]) + (ldw, ("ss", x1ss))) if ct == 0 else (),
                               inc=((("mm", 1),) if ct == KD - 1 else ()))
                    mmc = p.cur("mm")
                    p.emit("scalar",
                           lambda e, ot=ot, otl=otl: e.activation(
                               hbt[:, ot, :], ps[otl][:, 0:T], AF.Gelu,
                               bias=b1mv[:, ot:ot + 1]),
                           wait=(("mm", mmc),) + (bw(ghbt) if ot == 0 else ()),
                           inc=(("ss", 1),))
                    bankg[otl] = ("ss", p.cur("ss"))
                wguard[("w1", g % 2)] = ("mm", p.cur("mm"))
            gx1b = ("mm", p.cur("mm"))
            hbss = p.cur("ss")
            for ot in range(KD):
                wb = w2b[ot % 2]
                wsem = "sw2" + "AB"[ot % 2]
                p.dma("sync", wsem, wb[:], _rk(s_w2mb)[:, :, ot * 128:(ot + 1) * 128],
                      wait=bw(wguard.get(("w2", ot % 2))))
                ldw = (wsem, p.cur(wsem))
                bk = ot % 4
                for ct in range(KHM):
                    p.emit("tensor",
                           lambda e, ct=ct, wb=wb, bk=bk: e.matmul(
                               ps[bk][:, 0:T], wb[:, ct, :], hbt[:, ct, :],
                               start=(ct == 0), stop=(ct == KHM - 1)),
                           wait=(bw(bankg[bk]) + (ldw, ("ss", hbss))) if ct == 0 else (),
                           inc=((("mm", 1),) if ct == KHM - 1 else ()))
                mmc = p.cur("mm")
                wguard[("w2", ot % 2)] = ("mm", mmc)
                p.emit("vector",
                       lambda e, ot=ot, bk=bk: e.scalar_tensor_tensor(
                           aoc[:, ot, :], ps[bk][:, 0:T], b2mv[:, ot:ot + 1],
                           x1c[:, ot, :], ALU.add, ALU.add),
                       wait=(("mm", mmc),) + (bw(gaoc) if ot == 0 else ()),
                       inc=(("vv", 1),))
                bankg[bk] = ("vv", p.cur("vv"))
            ghbt = ("mm", p.cur("mm"))
            aovv = p.cur("vv")
            repmm = ln_stats(aoc, psrA, psrB, rows, cast16, T, psrg,
                             wait_extra=(("vv", aovv),), state=lnst)
            gaoc = lnst["scast"]
            p.emit("vector", lambda e: e.tensor_tensor(
                aoc[:], aoc[:],
                psrA[:, 0:T].unsqueeze(1).to_broadcast([128, KD, T]),
                ALU.subtract),
                wait=(("mm", repmm), ("sp", None)), inc=(("vv", 1),))
            p.emit("vector", lambda e: e.tensor_tensor(
                aoc[:], aoc[:],
                psrB[:, 0:T].unsqueeze(1).to_broadcast([128, KD, T]),
                ALU.mult),
                inc=(("vv", 1),))
            p.emit("vector", lambda e: e.tensor_tensor(
                x2c[:], aoc[:], x0c[:], ALU.add),
                wait=(ldx, ("sp", None)) + gx2w, inc=(("vv", 1),))
            x2vv = p.cur("vv")
            psrg[0] = ("vv", x2vv)
            p.dma("sync", "sp", _rk(s_x2)[:, :, t0:t0 + T], x2c[:],
                  wait=(("vv", x2vv),))
            repmm = ln_stats(x2c, psrA, psrB, rows, cast16, T, psrg,
                             wait_extra=(("vv", x2vv),), state=lnst)
            gx2w = (lnst["scast"],)
            p.emit("vector", lambda e: e.tensor_tensor(
                x2c[:], x2c[:],
                psrA[:, 0:T].unsqueeze(1).to_broadcast([128, KD, T]),
                ALU.subtract),
                wait=(("mm", repmm), ("sp", None)), inc=(("vv", 1),))
            p.emit("vector", lambda e: e.tensor_tensor(
                x2c[:], x2c[:],
                psrB[:, 0:T].unsqueeze(1).to_broadcast([128, KD, T]),
                ALU.mult),
                inc=(("vv", 1),))
            xfvv = p.cur("vv")
            psrg[0] = ("vv", xfvv)
            p.dma("sync", "sp", _rk(s_xf0)[:, :, t0:t0 + T], x2c[:],
                  wait=(("vv", xfvv),))
            gld = (("mm", outmm), ("vv", x2vv))
        p.fence()
        p.flush_phase()

    def fa_phase(l, src, dst):
        T = TCF
        with contextlib.ExitStack() as st:
            Es = st.enter_context
            wqkvb = Es(sb(f"L{l}wqkvb", [128, KD, 3 * INNER], BF16))
            wob = Es(sb(f"L{l}wob", [128, KIN, D], BF16))
            xfb = Es(sb(f"L{l}xfb", [128, KD, T + 1], BF16))
            xff = Es(sb(f"L{l}xff", [128, KD, T], F32))
            qf = Es(sb(f"L{l}qf", [128, KIN, T], F32))
            kf = Es(sb(f"L{l}kf", [128, KIN, T], F32))
            sqb = Es(sb(f"L{l}sqb", [128, T], BF16))
            qrb = Es(sb(f"L{l}qrb", [128, KIN, T], BF16))
            kfull = Es(sb(f"L{l}kfull", [128, KIN, T], BF16))
            kz = [Es(sb(f"L{l}kz{i}", [128, 2, KIN, T], BF16)) for i in range(2)]
            vb = Es(sb(f"L{l}vb", [128, KIN, T], BF16))
            vT = [Es(sb(f"L{l}vT{i}", [128, 4, KIN, 128], BF16)) for i in range(2)]
            rowq = Es(sb(f"L{l}rowq", [8, 4, T], F32))
            csqw = Es(sb(f"L{l}csqw", [128, T], F32))
            ssqw = Es(sb(f"L{l}ssqw", [128, T], F32))
            cskw = Es(sb(f"L{l}cskw", [128, T], F32))
            sskw = Es(sb(f"L{l}sskw", [128, T], F32))
            base4 = Es(sb(f"L{l}base4", [128, 4, T], F32))
            m1t = Es(sb(f"L{l}m1t", [128, T], F32))
            m2t = Es(sb(f"L{l}m2t", [128, T], F32))
            ut = Es(sb(f"L{l}ut", [128, T], F32))
            qb16 = Es(sb(f"L{l}qb16", [128, T], BF16))
            nums = Es(sb(f"L{l}nums", [128, 640], BF16))
            attnT = Es(sb(f"L{l}attnT", [128, 8, T], BF16))
            mxt = Es(sb(f"L{l}mxt", [128, 8], F32))
            o_sb = Es(sb(f"L{l}o_sb", [128, KIN, T], BF16))
            xmo = Es(sb(f"L{l}xmo", [128, KD, T], F32))
            simA = [Es(pst(f"L{l}simA{i}", [128, 512], F32)) for i in range(2)]
            simB = [Es(pst(f"L{l}simB{i}", [128, 512], F32)) for i in range(2)]
            psT = Es(pst(f"L{l}psT", [128, 1024], BF16))
            avp = [Es(pst(f"L{l}avp{i}", [128, 512], F32)) for i in range(2)]
            spare = Es(pst(f"L{l}spare", [128, 512], F32))
            gsim = [None, None]
            gsimB = [None, None]
            gpsT = [None]
            gsp = [None]
            gav = [None, None]
            gnums = [None]
            gattnT = [None]

            def bw(g):
                return (g,) if g else ()

            p.emit("vector", lambda e: e.memset(attnT[:], 0.0), inc=(("vv", 1),))
            p.emit("vector", lambda e: e.memset(kz[0][:], 0.0), inc=(("vv", 1),))
            p.emit("vector", lambda e: e.memset(kz[1][:], 0.0), inc=(("vv", 1),))
            p.dma("gpsimd", "gw", wqkvb[:], wqkv[l].rearrange("(k p) o -> p k o", p=128))
            p.dma("gpsimd", "gw", wob[:], wo[l].rearrange("(k p) o -> p k o", p=128))
            wdg = p.cur("gw")

            for w in range(NCF if _on(f"fa{l}") else 0):
                t0 = w * T
                cw = w % 2
                pw = 1 - cw
                if w == 0:
                    p.emit("vector", lambda e: e.memset(xfb[:, :, 0:1], 0.0),
                           inc=(("vv", 1),))
                    p.dma("gpsimd", "gxf", xfb[:, :, 1:T + 1], _rk(src)[:, :, 0:T])
                else:
                    p.dma("gpsimd", "gxf", xfb[:], _rk(src)[:, :, t0 - 1:t0 + T])
                p.dma("sync", "ax", xff[:], _rk(src)[:, :, t0:t0 + T])
                p.dma("sync", "ax", base4[:, 0, :], csq0[:, t0:t0 + T])
                p.dma("sync", "ax", base4[:, 1, :], ssq0[:, t0:t0 + T])
                p.dma("sync", "ax", base4[:, 2, :], csk0[:, t0:t0 + T])
                p.dma("sync", "ax", base4[:, 3, :], ssk0[:, t0:t0 + T])
                ldx, ldg = p.cur("ax"), p.cur("gxf")
                p.emit("vector", lambda e: e.tensor_scalar(
                    csqw[:], base4[:, 0, :], qs2[:, l:l + 1], 8.0,
                    ALU.mult, ALU.mult),
                    wait=(("ax", ldx),), inc=(("vv", 1),))
                p.emit("vector", lambda e: e.tensor_scalar(
                    ssqw[:], base4[:, 1, :], qs2r[:, l:l + 1], 8.0,
                    ALU.mult, ALU.mult),
                    inc=(("vv", 1),))
                p.emit("vector", lambda e: e.tensor_scalar(
                    cskw[:], base4[:, 2, :], ks2[:, l:l + 1], None, ALU.mult),
                    inc=(("vv", 1),))
                p.emit("vector", lambda e: e.tensor_scalar(
                    sskw[:], base4[:, 3, :], ks2r[:, l:l + 1], None, ALU.mult),
                    inc=(("vv", 1),))
                tabvv = p.cur("vv")

                qkbanks = [simA[0], simA[1], avp[0], avp[1]]
                qkg = [gsim[0], gsim[1], gav[0], gav[1]]
                for part in ("q", "k", "v"):
                    base_ot = {"q": 0, "k": KIN, "v": 2 * KIN}[part]
                    for j in range(KIN):
                        ot = base_ot + j
                        bank = qkbanks[j]
                        for ct in range(KD):
                            ap_rhs = (xfb[:, ct, 1:T + 1] if ct < KD // 2
                                      else xfb[:, ct, 0:T])
                            p.emit("tensor",
                                   lambda e, ct=ct, ot=ot, bank=bank, ap=ap_rhs: e.matmul(
                                       bank[:, 0:T],
                                       wqkvb[:, ct, 128 * ot:128 * ot + 128],
                                       ap, start=(ct == 0), stop=(ct == KD - 1)),
                                   wait=(bw(qkg[j]) +
                                         ((("gxf", ldg), ("gw", wdg),
                                           ("vv", None)) if ct == 0 else ())),
                                   inc=((("mm", 1),) if ct == KD - 1 else ()))
                        mmc = p.cur("mm")
                        if part in ("q", "k"):
                            dest = qf if part == "q" else kf
                            p.emit("scalar",
                                   lambda e, j=j, bank=bank, dest=dest: e.copy(
                                       out=dest[:, j, :], in_=bank[:, 0:T]),
                                   wait=(("mm", mmc),), inc=(("ss", 1),))
                            p.emit("scalar",
                                   lambda e, bank=bank: e.activation(
                                       sqb[:], bank[:, 0:T], AF.Square),
                                   wait=(("mm", None),), inc=(("ss", 1),))
                            sqs = p.cur("ss")
                            p.emit("tensor",
                                   lambda e, j=j: e.matmul(
                                       spare[0:8, 0:T], e8t[:, j, :], sqb[:],
                                       start=(j == 0), stop=(j == KIN - 1)),
                                   wait=(bw(gsp[0]) if j == 0 else ()) + (("ss", sqs),),
                                   inc=(("mm", 1),))
                            qkg[j] = ("ss", sqs)
                            if j == KIN - 1:
                                smm = p.cur("mm")
                                row = 0 if part == "q" else 1
                                p.emit("scalar",
                                       lambda e, row=row: e.activation(
                                           rowq[:, row, :], spare[0:8, 0:T],
                                           AF.Sqrt),
                                       wait=(("mm", smm),), inc=(("ss", 1),))
                                p.emit("vector",
                                       lambda e, row=row: e.tensor_scalar(
                                           rowq[:, row, :], rowq[:, row, :],
                                           1e-12, None, ALU.max),
                                       wait=(("ss", p.cur("ss")),),
                                       inc=(("vv", 1),))
                                p.emit("vector",
                                       lambda e, row=row: e.reciprocal(
                                           rowq[:, 2 + row, :], rowq[:, row, :]),
                                       wait=(("vv", None),), inc=(("vv", 1),))
                                gsp[0] = ("vv", p.cur("vv"))
                        else:
                            p.emit("scalar",
                                   lambda e, j=j, bank=bank: e.copy(
                                       out=vb[:, j, :], in_=bank[:, 0:T]),
                                   wait=(("mm", mmc),), inc=(("ss", 1),))
                            qkg[j] = ("ss", p.cur("ss"))
                gsim[0], gsim[1], gav[0], gav[1] = qkg
                rowvv = p.cur("vv")
                for part in (("q", "k") if _FA_STAGE >= 2 else ()):
                    src_f = qf if part == "q" else kf
                    dst_b = qrb if part == "q" else kfull
                    cst = csqw if part == "q" else cskw
                    sst = ssqw if part == "q" else sskw
                    row = 0 if part == "q" else 1
                    for j in range(KIN):
                        p.emit("tensor",
                               lambda e, j=j, row=row: e.matmul(
                                   spare[:, 0:T], g8t[:, j, :],
                                   rowq[:, 2 + row, :],
                                   start=True, stop=True),
                               wait=bw(gsp[0]) + (("vv", rowvv),),
                               inc=(("mm", 1),))
                        srmm = p.cur("mm")
                        p.emit("vector",
                               lambda e, j=j, src_f=src_f, cst=cst: e.tensor_tensor(
                                   m1t[:], src_f[:, j, :], cst[:], ALU.mult),
                               wait=(("vv", tabvv),), inc=(("vv", 1),))
                        p.emit("scalar",
                               lambda e, j=j, src_f=src_f: e.copy(
                                   out=qb16[:], in_=src_f[:, j, :]),
                               wait=(("mm", None),), inc=(("ss", 1),))
                        qbs = p.cur("ss")
                        p.emit("tensor",
                               lambda e: e.matmul(
                                   simB[0][:, 0:T], rotmt[:], qb16[:],
                                   start=True, stop=True),
                               wait=bw(gsimB[0]) + (("ss", qbs),),
                               inc=(("mm", 1),))
                        rmm = p.cur("mm")
                        p.emit("vector",
                               lambda e, sst=sst: e.tensor_tensor(
                                   m2t[:], simB[0][:, 0:T], sst[:], ALU.mult),
                               wait=(("mm", rmm),), inc=(("vv", 1),))
                        gsimB[0] = ("vv", p.cur("vv"))
                        p.emit("vector",
                               lambda e: e.tensor_tensor(ut[:], m1t[:], m2t[:],
                                                         ALU.add),
                               inc=(("vv", 1),))
                        p.emit("vector",
                               lambda e, j=j, dst_b=dst_b: e.tensor_tensor(
                                   dst_b[:, j, :], ut[:], spare[:, 0:T],
                                   ALU.mult),
                               wait=(("mm", srmm),), inc=(("vv", 1),))
                        gsp[0] = ("vv", p.cur("vv"))
                        if part == "k":
                            kvv = p.cur("vv")
                            p.emit("scalar",
                                   lambda e, j=j, cw=cw: e.copy(
                                       out=kz[cw][0:64, 0, j, :],
                                       in_=kfull[0:64, j, :]),
                                   wait=(("vv", kvv),), inc=(("ss", 1),))
                            p.emit("scalar",
                                   lambda e, j=j, cw=cw: e.copy(
                                       out=kz[cw][64:128, 1, j, :],
                                       in_=kfull[64:128, j, :]),
                                   inc=(("ss", 1),))
                qkvv = p.cur("vv")
                kzss = p.cur("ss")
                vbss = p.cur("ss")
                for half in (range(2) if _FA_STAGE >= 3 else ()):
                    for idx in range(8):
                        gi = half * 8 + idx
                        tb = gi // 4
                        pr = gi % 4
                        p.emit("tensor",
                               lambda e, pr=pr, tb=tb, idx=idx: e.transpose(
                                   psT[:, idx * 128:(idx + 1) * 128],
                                   vb[:, pr, 128 * tb:128 * tb + 128],
                                   identt[:]),
                               wait=(bw(gpsT[0]) + (("ss", vbss),)) if idx == 0 else (),
                               inc=((("mm", 1),) if idx == 7 else ()))
                    tmm = p.cur("mm")
                    p.emit("scalar",
                           lambda e, half=half, cw=cw: e.copy(
                               out=vT[cw][:].rearrange("p a h d -> p (a h d)")
                               [:, half * 1024:(half + 1) * 1024],
                               in_=psT[:]),
                           wait=(("mm", tmm),), inc=(("ss", 1),))
                    gpsT[0] = ("ss", p.cur("ss"))

                def slots(ib):
                    s = []
                    if w > 0:
                        for jt in range(ib, 4):
                            s.append(("prev", jt))
                    for jt in range(0, ib + 1):
                        s.append(("own", jt))
                    return s

                tiles = [(h, ib) for h in range(NH) for ib in range(4)]
                if _FA_NT is not None:
                    tiles = tiles[:_FA_NT]
                simdone = {}

                def emit_sim(h, ib, par):
                    sl = slots(ib)
                    nsl_ = len(sl)
                    for si, (which, jt) in enumerate(sl):
                        ksrc = kz[pw] if which == "prev" else kz[cw]
                        inB = si >= 4
                        tgt = (simB[par][:, 0:128] if inB
                               else simA[par][:, 128 * si:128 * si + 128])
                        mk = None
                        if which == "own" and jt == ib:
                            mk = maskot
                        if which == "prev" and jt == ib:
                            mk = maskpt
                        wt = ()
                        if si == 0:
                            wt = (bw(gsim[par]) + bw(gsimB[par])
                                  + (("vv", qkvv), ("ss", kzss)))
                        last = (si == nsl_ - 1)
                        p.emit("tensor",
                               lambda e, h=h, ib=ib, jt=jt, ksrc=ksrc, tgt=tgt, mk=mk: e.matmul(
                                   tgt,
                                   qrb[:, h // 2, 128 * ib:128 * ib + 128],
                                   ksrc[:, h % 2, h // 2,
                                        128 * jt:128 * jt + 128],
                                   start=True, stop=(mk is None)),
                               wait=wt,
                               inc=((("mm", 1),) if (last and mk is None) else ()))
                        if mk is not None:
                            p.emit("tensor",
                                   lambda e, mk=mk, tgt=tgt: e.matmul(
                                       tgt, mk[:], identt[:],
                                       start=False, stop=True),
                                   inc=((("mm", 1),) if last else ()))

                def soft_tile(h, ib, par):
                    sl = slots(ib)
                    nsl = len(sl)
                    nA = min(nsl, 4)
                    mmsim = simdone[(h, ib)]
                    if _FA_SOFT < 1:
                        gsim[par] = ("mm", mmsim)
                        gsimB[par] = ("mm", mmsim)
                        return
                    p.emit("vector",
                           lambda e, nA=nA, par=par: e.tensor_reduce(
                               mxt[:, 0:1], simA[par][:, 0:128 * nA],
                               AXX, ALU.max),
                           wait=(("mm", mmsim), ("ss", None)),
                           inc=(("vv", 1),))
                    if nsl > 4:
                        p.emit("vector",
                               lambda e, par=par: e.tensor_reduce(
                                   mxt[:, 1:2], simB[par][:, 0:128],
                                   AXX, ALU.max),
                               inc=(("vv", 1),))
                        p.emit("vector",
                               lambda e: e.tensor_tensor(
                                   mxt[:, 0:1], mxt[:, 0:1], mxt[:, 1:2],
                                   ALU.max),
                               inc=(("vv", 1),))
                    p.emit("vector",
                           lambda e: e.tensor_scalar(
                               mxt[:, 2:3], mxt[:, 0:1], -1.0, None, ALU.mult),
                           inc=(("vv", 1),))
                    mxvv = p.cur("vv")
                    if _FA_SOFT < 2:
                        gsim[par] = ("vv", mxvv)
                        gsimB[par] = ("vv", mxvv)
                        return
                    p.emit("scalar",
                           lambda e, nA=nA, par=par: e.activation(
                               nums[:, 0:128 * nA], simA[par][:, 0:128 * nA],
                               AF.Exp, bias=mxt[:, 2:3],
                               accum_out=mxt[:, 3:4]),
                           wait=(("vv", mxvv),) + bw(gnums[0]),
                           inc=(("ss", 1),))
                    if nsl > 4:
                        p.emit("scalar",
                               lambda e, par=par: e.activation(
                                   nums[:, 512:640], simB[par][:, 0:128],
                                   AF.Exp, bias=mxt[:, 2:3],
                                   accum_out=mxt[:, 4:5]),
                               inc=(("ss", 1),))
                    e1ss = p.cur("ss")
                    if _FA_SOFT < 3:
                        gsim[par] = ("ss", e1ss)
                        gsimB[par] = ("ss", e1ss)
                        return
                    if nsl > 4:
                        p.emit("vector",
                               lambda e: e.tensor_tensor(
                                   mxt[:, 5:6], mxt[:, 3:4], mxt[:, 4:5],
                                   ALU.add),
                               wait=(("ss", e1ss),), inc=(("vv", 1),))
                    else:
                        p.emit("vector",
                               lambda e: e.tensor_copy(mxt[:, 5:6], mxt[:, 3:4]),
                               wait=(("ss", e1ss),), inc=(("vv", 1),))
                    p.emit("scalar",
                           lambda e: e.activation(mxt[:, 6:7], mxt[:, 5:6],
                                                  AF.Ln),
                           wait=(("vv", p.cur("vv")),), inc=(("ss", 1),))
                    p.emit("vector",
                           lambda e: e.tensor_tensor(
                               mxt[:, 7:8], mxt[:, 2:3], mxt[:, 6:7],
                               ALU.subtract),
                           wait=(("ss", p.cur("ss")),), inc=(("vv", 1),))
                    b2vv = p.cur("vv")
                    if _FA_SOFT < 4:
                        gsim[par] = ("vv", b2vv)
                        gsimB[par] = ("vv", b2vv)
                        return
                    p.emit("scalar",
                           lambda e, nA=nA, par=par: e.activation(
                               nums[:, 0:128 * nA], simA[par][:, 0:128 * nA],
                               AF.Exp, bias=mxt[:, 7:8]),
                           wait=(("vv", b2vv),), inc=(("ss", 1),))
                    if nsl > 4:
                        p.emit("scalar",
                               lambda e, par=par: e.activation(
                                   nums[:, 512:640], simB[par][:, 0:128],
                                   AF.Exp, bias=mxt[:, 7:8]),
                               inc=(("ss", 1),))
                    ess = p.cur("ss")
                    gsim[par] = ("ss", ess)
                    gsimB[par] = ("ss", ess)
                    if _FA_SOFT < 5:
                        return
                    for si in range(nsl):
                        p.emit("tensor",
                               lambda e, si=si: e.transpose(
                                   psT[:, 128 * si:128 * si + 128],
                                   nums[:, 128 * si:128 * si + 128]
                                   if si < 4 else nums[:, 512:640],
                                   identt[:]),
                               wait=(bw(gpsT[0]) + (("ss", ess),)) if si == 0 else (),
                               inc=((("mm", 1),) if si == nsl - 1 else ()))
                    tmm = p.cur("mm")
                    gnums[0] = ("mm", tmm)
                    if _FA_SOFT < 6:
                        return
                    for si, (which, jt) in enumerate(sl):
                        idx = jt if which == "prev" else 4 + jt
                        p.emit("scalar",
                               lambda e, si=si, idx=idx, ib=ib: e.copy(
                                   out=attnT[:, idx, 128 * ib:128 * ib + 128],
                                   in_=psT[:, 128 * si:128 * si + 128]),
                               wait=((("mm", tmm),) + bw(gattnT[0])) if si == 0 else (),
                               inc=((("ss", 1),) if si == nsl - 1 else ()))
                    gpsT[0] = ("ss", p.cur("ss"))
                    atss = p.cur("ss")
                    if _FA_SOFT < 7:
                        return
                    bank = avp[h % 2]
                    first_av = (ib == 0)
                    last_av = (ib == 3)
                    for si, (which, jt) in enumerate(sl):
                        idx = jt if which == "prev" else 4 + jt
                        vsrc = vT[pw] if which == "prev" else vT[cw]
                        p.emit("tensor",
                               lambda e, h=h, ib=ib, jt=jt, idx=idx, vsrc=vsrc, bank=bank, si=si, nsl=nsl: e.matmul(
                                   bank[0:64, 128 * ib:128 * ib + 128],
                                   vsrc[:, jt, h // 2,
                                        64 * (h % 2):64 * (h % 2) + 64],
                                   attnT[:, idx, 128 * ib:128 * ib + 128],
                                   start=(si == 0),
                                   stop=(si == nsl - 1)),
                               wait=((bw(gav[h % 2]) + (("ss", atss),)) if (first_av and si == 0)
                                     else ((("ss", atss),) if si == 0 else ())),
                               inc=((("mm", 1),) if (last_av and si == nsl - 1) else ()))
                    gattnT[0] = ("mm", p.cur("mm"))
                    if last_av:
                        avmm = p.cur("mm")
                        p.emit("scalar",
                               lambda e, h=h, bank=bank: e.copy(
                                   out=o_sb[64 * (h % 2):64 * (h % 2) + 64,
                                            h // 2, :],
                                   in_=bank[0:64, 0:T]),
                               wait=(("mm", avmm),), inc=(("ss", 1),))
                        gav[h % 2] = ("ss", p.cur("ss"))

                if _FA_STAGE < 4:
                    tiles = []
                if tiles:
                    emit_sim(*tiles[0], 0)
                    simdone[tiles[0]] = p.cur("mm")
                for i, (h, ib) in enumerate(tiles):
                    if i + 1 < len(tiles):
                        nh_, nib = tiles[i + 1]
                        emit_sim(nh_, nib, (i + 1) % 2)
                        simdone[(nh_, nib)] = p.cur("mm")
                    soft_tile(h, ib, i % 2)
                osbss = p.cur("ss")
                for ot in (range(KD) if _FA_STAGE >= 5 else ()):
                    bank = simA[ot % 2]
                    for ct in range(KIN):
                        p.emit("tensor",
                               lambda e, ct=ct, ot=ot, bank=bank: e.matmul(
                                   bank[:, 0:T],
                                   wob[:, ct, 128 * ot:128 * ot + 128],
                                   o_sb[:, ct, :],
                                   start=(ct == 0), stop=(ct == KIN - 1)),
                               wait=(bw(gsim[ot % 2]) + (("ss", osbss),)) if ct == 0 else (),
                               inc=((("mm", 1),) if ct == KIN - 1 else ()))
                    mmc = p.cur("mm")
                    p.emit("vector",
                           lambda e, ot=ot, bank=bank: e.tensor_tensor(
                               xmo[:, ot, :], bank[:, 0:T], xff[:, ot, :],
                               ALU.add),
                           wait=(("mm", mmc), ("ax", ldx)), inc=(("vv", 1),))
                    gsim[ot % 2] = ("vv", p.cur("vv"))
                xmvv = p.cur("vv")
                p.dma("sync", "sp", _rk(dst)[:, :, t0:t0 + T], xmo[:],
                      wait=(("vv", xmvv),))
                p.fence()
            p.flush_phase()

    def fb_phase(l, src, dst):
        T = TCF
        with contextlib.ExitStack() as st:
            Es = st.enter_context
            xmc = Es(sb(f"L{l}xmc", [128, KD, T + 1], F32))
            xmb = Es(sb(f"L{l}xmb", [128, KD, T + 1], BF16))
            rows = Es(sb(f"L{l}fbrows", [1, 4, T], F32))
            tc_ = Es(sb(f"L{l}fbtc", [128, KD, T], F32))
            xrb = Es(sb(f"L{l}xrb", [128, KD, T], BF16))
            h4b = Es(sb(f"L{l}h4b", [128, KHID, T], BF16))
            glt = Es(sb(f"L{l}glt", [128, T], F32))
            xnc = Es(sb(f"L{l}xnc", [128, KD, T], F32))
            w1ab = [Es(sb(f"L{l}w1ab{i}", [128, KD, 256], BF16)) for i in range(2)]
            w1gb = [Es(sb(f"L{l}w1gb{i}", [128, KD, 256], BF16)) for i in range(2)]
            w2b = [Es(sb(f"L{l}fbw2b{i}", [128, KHID, 128], BF16)) for i in range(2)]
            ps = [Es(pst(f"L{l}fbps{i}", [128, 512], F32)) for i in range(4)]
            psrA = Es(pst(f"L{l}fbpsrA", [128, 512], F32))
            psrB = Es(pst(f"L{l}fbpsrB", [128, 512], F32))
            bankg = [None] * 4
            wguard = {}
            psrg = [None]

            def bw(g):
                return (g,) if g else ()

            lnst = {}
            gld = ()
            gxrb = None
            gh4b = None
            for c in range((_NCF_LIM or NCF) if _on(f"fb{l}") else 0):
                t0 = c * T
                if c == 0:
                    p.emit("vector", lambda e: e.memset(xmc[:, :, 0:1], 0.0),
                           inc=(("vv", 1),))
                    p.dma("sync", "ax", xmc[:, :, 1:T + 1], _rk(src)[:, :, 0:T])
                else:
                    p.dma("sync", "ax", xmc[:], _rk(src)[:, :, t0 - 1:t0 + T],
                          wait=gld)
                repmm = ln_stats(xmc, psrA, psrB, rows, xmb, T, psrg,
                                 wait_extra=(("ax", None),), shift=True,
                                 state=lnst)
                for ct in range(KD):
                    src_ap = (xmc[:, ct, 1:T + 1] if ct < KD // 2
                              else xmc[:, ct, 0:T])
                    p.emit("vector",
                           lambda e, ct=ct, src_ap=src_ap: e.tensor_tensor(
                               tc_[:, ct, :], src_ap, psrA[:, 0:T],
                               ALU.subtract),
                           wait=(("mm", repmm),) if ct == 0 else (),
                           inc=(("vv", 1),))
                p.emit("vector", lambda e: e.tensor_tensor(
                    tc_[:], tc_[:],
                    psrB[:, 0:T].unsqueeze(1).to_broadcast([128, KD, T]),
                    ALU.mult),
                    inc=(("vv", 1),))
                lnvv = p.cur("vv")
                psrg[0] = ("vv", lnvv)
                for ct in range(KD):
                    p.emit("scalar",
                           lambda e, ct=ct: e.activation(
                               xrb[:, ct, :], tc_[:, ct, :], AF.Identity,
                               scale=lngv[:, l, ct:ct + 1],
                               bias=lnbt[:, l, ct:ct + 1]),
                           wait=((("vv", lnvv),) + bw(gxrb)) if ct == 0 else (),
                           inc=(("ss", 1),))
                xrss = p.cur("ss")
                for g in range(KHID // 2):
                    pts = [2 * g, 2 * g + 1]
                    wa = w1ab[g % 2]
                    wg = w1gb[g % 2]
                    wsa = "swa" + "AB"[g % 2]
                    wsg = "swg" + "AB"[g % 2]
                    p.dma("sync", wsa, wa[:],
                          _rk(s_w1fab[l])[:, :, 256 * g:256 * (g + 1)],
                          wait=bw(wguard.get(("wa", g % 2))))
                    p.dma("sync", wsg, wg[:],
                          _rk(s_w1fgb[l])[:, :, 256 * g:256 * (g + 1)],
                          wait=bw(wguard.get(("wg", g % 2))))
                    ldw = ((wsa, p.cur(wsa)), (wsg, p.cur(wsg)))
                    for ii, pt in enumerate(pts):
                        bka = (2 * ii) % 4
                        bkg_ = (2 * ii + 1) % 4
                        for ct in range(KD):
                            p.emit("tensor",
                                   lambda e, ct=ct, wa=wa, ii=ii, bka=bka: e.matmul(
                                       ps[bka][:, 0:T],
                                       wa[:, ct, 128 * ii:128 * ii + 128],
                                       xrb[:, ct, :],
                                       start=(ct == 0), stop=(ct == KD - 1)),
                                   wait=(bw(bankg[bka]) +
                                         (ldw + (("ss", xrss),) if ct == 0 else ())),
                                   inc=((("mm", 1),) if ct == KD - 1 else ()))
                        for ct in range(KD):
                            p.emit("tensor",
                                   lambda e, ct=ct, wg=wg, ii=ii, bkg_=bkg_: e.matmul(
                                       ps[bkg_][:, 0:T],
                                       wg[:, ct, 128 * ii:128 * ii + 128],
                                       xrb[:, ct, :],
                                       start=(ct == 0), stop=(ct == KD - 1)),
                                   wait=(bw(bankg[bkg_]) if ct == 0 else ()),
                                   inc=((("mm", 1),) if ct == KD - 1 else ()))
                        mmc = p.cur("mm")
                        p.emit("scalar",
                               lambda e, pt=pt, bkg_=bkg_: e.activation(
                                   glt[:], ps[bkg_][:, 0:T], AF.Gelu,
                                   bias=b1fgv[:, l, pt:pt + 1]),
                               wait=(("mm", mmc), ("vv", None)), inc=(("ss", 1),))
                        gss = p.cur("ss")
                        p.emit("vector",
                               lambda e, pt=pt, bka=bka: e.scalar_tensor_tensor(
                                   h4b[:, pt, :], ps[bka][:, 0:T],
                                   b1fav[:, l, pt:pt + 1], glt[:],
                                   ALU.add, ALU.mult),
                               wait=(("ss", gss),) + (bw(gh4b) if pt == 0 else ()),
                               inc=(("vv", 1),))
                        bankg[bka] = ("vv", p.cur("vv"))
                        bankg[bkg_] = ("vv", p.cur("vv"))
                    wguard[("wa", g % 2)] = ("mm", p.cur("mm"))
                    wguard[("wg", g % 2)] = ("mm", p.cur("mm"))
                gxrb = ("mm", p.cur("mm"))
                h4vv = p.cur("vv")
                for ot in range(KD):
                    wb = w2b[ot % 2]
                    wsem = "swf" + "AB"[ot % 2]
                    p.dma("sync", wsem, wb[:],
                          _rk(s_w2fb[l])[:, :, 128 * ot:128 * (ot + 1)],
                          wait=bw(wguard.get(("w2", ot % 2))))
                    ldw = (wsem, p.cur(wsem))
                    bk = ot % 4
                    for ct in range(KHID):
                        p.emit("tensor",
                               lambda e, ct=ct, wb=wb, bk=bk: e.matmul(
                                   ps[bk][:, 0:T], wb[:, ct, :], h4b[:, ct, :],
                                   start=(ct == 0), stop=(ct == KHID - 1)),
                               wait=(bw(bankg[bk]) +
                                     ((ldw, ("vv", h4vv)) if ct == 0 else ())),
                               inc=((("mm", 1),) if ct == KHID - 1 else ()))
                    mmc = p.cur("mm")
                    wguard[("w2", ot % 2)] = ("mm", mmc)
                    p.emit("vector",
                           lambda e, ot=ot, bk=bk: e.scalar_tensor_tensor(
                               xnc[:, ot, :], ps[bk][:, 0:T],
                               b2fv[:, l, ot:ot + 1], xmc[:, ot, 1:T + 1],
                               ALU.add, ALU.add),
                           wait=(("mm", mmc),) + ((("sp", None),) if ot == 0 else ()),
                           inc=(("vv", 1),))
                    bankg[bk] = ("vv", p.cur("vv"))
                gh4b = ("mm", p.cur("mm"))
                xnvv = p.cur("vv")
                p.dma("sync", "sp", _rk(dst)[:, :, t0:t0 + T], xnc[:],
                      wait=(("vv", xnvv),))
                gld = (("vv", xnvv), lnst["scast"])
            p.fence()
            p.flush_phase()

    fa_phase(0, s_xf0, s_xm[0])
    fb_phase(0, s_xm[0], s_xl[0])
    fa_phase(1, s_xl[0], s_xm[1])
    fb_phase(1, s_xm[1], s_xl[1])

    with contextlib.ExitStack() as st:
        Es = st.enter_context
        T = TCF
        fc = Es(sb("fc", [128, KD, T], F32))
        x2c = Es(sb("finx2", [128, KD, T], F32))
        cast16 = Es(sb("fic16", [128, KD, T], BF16))
        rows = Es(sb("firows", [1, 4, T], F32))
        oc = Es(sb("oc", [128, KD, T], F32))
        psrA = Es(pst("fipsrA", [128, 512], F32))
        psrB = Es(pst("fipsrB", [128, 512], F32))
        psrg = [None]
        for c in range(NCF if _on("fin") else 0):
            t0 = c * T
            p.dma("sync", "ax", fc[:], _rk(s_xl[1])[:, :, t0:t0 + T])
            p.dma("sync", "ax", x2c[:], _rk(s_x2)[:, :, t0:t0 + T])
            repmm = ln_stats(fc, psrA, psrB, rows, cast16, T, psrg,
                             wait_extra=(("ax", None),))
            p.emit("vector", lambda e: e.tensor_tensor(
                oc[:], fc[:],
                psrA[:, 0:T].unsqueeze(1).to_broadcast([128, KD, T]),
                ALU.subtract),
                wait=(("mm", repmm),), inc=(("vv", 1),))
            p.emit("vector", lambda e: e.tensor_tensor(
                oc[:], oc[:],
                psrB[:, 0:T].unsqueeze(1).to_broadcast([128, KD, T]),
                ALU.mult),
                inc=(("vv", 1),))
            p.emit("vector", lambda e: e.tensor_tensor(
                oc[:], oc[:], x2c[:], ALU.add),
                wait=(("ax", None),), inc=(("vv", 1),))
            ovv = p.cur("vv")
            psrg[0] = ("vv", ovv)
            p.emit("sync", lambda e, t0=t0: e.dma_start(
                out=_rk(outT)[:, :, t0:t0 + T], in_=oc[:]),
                wait=(("vv", ovv),), inc=(("out", 16),))
            p.fence()
        if _DBG is not None:
            srcmap = {"ln1": s_ln1, "x2": s_x2, "xf0": s_xf0,
                      "xm0": s_xm[0], "xl1": s_xl[0], "xm1": s_xm[1],
                      "xl2": s_xl[1]}
            if _DBG in srcmap:
                p.emit("sync", lambda e: e.dma_start(
                    out=_rk(dbg)[:, 0:KD, :], in_=_rk(srcmap[_DBG])[:]),
                    wait=(("sp", None),), inc=(("out", 16),))
            elif _DBG == "ygb":
                p.emit("gpsimd", lambda e: e.dma_start(
                    out=_rk(dbg)[:, 0:KDI, :], in_=_rk(s_ygb)[:]),
                    wait=(("sp", None),), inc=(("out", 16),))
        p.emit("sync", lambda e: e.nop(),
               wait=(("out", p.cur("out")),))
        p.flush_phase()

    perstack.close()
    return nc


_NC_CACHE = {}


def kernel(**inputs):
    staged = _stage_inputs(inputs)
    x = np.asarray(inputs["L0_attn_x"])
    in_maps = []
    for ccore in range(8):
        m = dict(staged)
        m["xT"] = np.ascontiguousarray(x[ccore].T)
        in_maps.append(m)
    if "nc" not in _NC_CACHE:
        _NC_CACHE["nc"] = build_nc()
    nc = _NC_CACHE["nc"]
    res = run_bass_kernel_spmd(nc, in_maps, core_ids=list(range(8)))
    out = np.stack([np.asarray(r["outT"]).T for r in res.results], axis=0)
    if _DBG is not None:
        kernel._dbg = [np.asarray(r.get("dbg")) for r in res.results]
    return out


# version 07 (final)
# speedup vs baseline: 1.1710x; 1.0020x over previous
import contextlib
import numpy as np
import ml_dtypes
import concourse.bass as bass
import concourse.mybir as mybir
from concourse.bass_utils import run_bass_kernel_spmd

F32 = mybir.dt.float32
BF16 = mybir.dt.bfloat16
AF = mybir.ActivationFunctionType
ALU = mybir.AluOpType
AXX = mybir.AxisListType.X
BF = ml_dtypes.bfloat16

L = 4096
D = 1024
KD = 8
DIN = 2048
KDI = 16
NST = 16
DTR = 64
HM = 4096
KHM = 32
NH = 8
DH = 64
INNER = 512
KIN = 4
WIN = 512
HID = 2730
HIDP = 2816
KHID = 22
NL = 2

TCM = 256
NCM = L // TCM
TCF = 512
NCF = L // TCF

NEG = -1.0e38
EPS = 1e-5

_DBG = None
_PHASES = None
_FA_STAGE = 9
_FA_SOFT = 9
_FA_SIM = 9
_NCF_LIM = None
_FA_NT = None
_FA_DUMMY = 0

ENGS = ("sync", "gpsimd", "tensor", "vector", "scalar")


class Prog:
    def __init__(self, nc):
        self.nc = nc
        self.sems = {}
        self.cnt = {}
        self.seen = {e: {} for e in ENGS}
        self.ops = {e: [] for e in ENGS}

    def sem(self, name):
        self.sems[name] = self.nc.alloc_semaphore(name)
        self.cnt[name] = 0

    def cur(self, name):
        return self.cnt[name]

    def emit(self, eng, fn, wait=(), inc=()):
        w = tuple((s, self.cnt[s] if v is None else v) for (s, v) in wait)
        self.ops[eng].append((w, fn, tuple(inc)))
        for s, a in inc:
            self.cnt[s] += a

    def dma(self, q, sem, out, in_, wait=()):
        self.emit(q, (lambda e, o=out, i=in_: e.dma_start(out=o, in_=i)),
                  wait, ((sem, 16),))

    def fence(self):
        w = tuple((s, self.cnt[s]) for s in self.cnt)
        for eng in ENGS:
            self.emit(eng, lambda e: e.nop(), wait=w)

    def flush_phase(self):
        ops = self.ops
        self.ops = {e: [] for e in ENGS}
        sems = self.sems
        seen = self.seen
        with self.nc.Block() as blk:
            for eng in ENGS:
                lst = ops[eng]
                if not lst:
                    continue

                def runner(e, lst=lst, eng=eng):
                    sv = seen[eng]
                    for w, fn, incs in lst:
                        for s, v in w:
                            if v > 0 and sv.get(s, 0) < v:
                                e.wait_ge(sems[s], v)
                                sv[s] = v
                        r = fn(e)
                        for s, a in incs:
                            r.then_inc(sems[s], a)
                getattr(blk, eng)(runner)


_CONST_CACHE = {}


def _consts():
    if _CONST_CACHE:
        return _CONST_CACHE
    c = {}
    c["ident"] = np.eye(128).astype(BF)
    c["ones1"] = np.ones((128, 1), dtype=BF)
    c["onesf"] = np.ones((1, 128), dtype=np.float32)
    g8 = np.zeros((4, 8, 128), np.float32)
    for j in range(4):
        g8[j, 2 * j, 0:64] = 1.0
        g8[j, 2 * j + 1, 64:128] = 1.0
    c["g8"] = g8
    e8 = np.zeros((4, 128, 8), np.float32)
    for k in range(4):
        e8[k, 0:64, 2 * k] = 1.0
        e8[k, 64:128, 2 * k + 1] = 1.0
    c["e8"] = e8.astype(BF)
    i_ = np.arange(128)[:, None]
    j_ = np.arange(128)[None, :]
    m_own = np.where(j_ > i_, NEG, 0.0).astype(np.float32)
    m_prev = np.where(j_ < i_, NEG, 0.0).astype(np.float32)
    c["maskownT"] = np.ascontiguousarray(m_own.T).astype(BF)
    c["maskprevT"] = np.ascontiguousarray(m_prev.T).astype(BF)
    R = np.zeros((128, 128), np.float32)
    for d in range(128):
        dl = d % 64
        base = d - dl
        if dl < 32:
            R[d, base + dl + 32] = -1.0
        else:
            R[d, base + dl - 32] = 1.0
    c["rotmT"] = np.ascontiguousarray(R.T).astype(BF)
    half = np.arange(0, DH, 2, dtype=np.float64)
    invf = 1.0 / (10000.0 ** (half / DH))
    t = np.arange(L, dtype=np.float64)
    fr = t[:, None] * invf[None, :]
    freqs = np.concatenate([fr, fr], axis=1)
    base = (half + 0.4 * DH) / (1.4 * DH)
    power = (t - L // 2) / float(WIN // 2)
    scale = base[None, :] ** power[:, None]
    scale = np.concatenate([scale, scale], axis=1)
    cos = np.cos(freqs)
    sin = np.sin(freqs)

    def two(x):
        xt = x.T
        return np.ascontiguousarray(
            np.concatenate([xt, xt], axis=0)).astype(np.float32)
    c["csq0"] = two(cos * scale)
    c["ssq0"] = two(sin * scale)
    c["csk0"] = two(cos / scale)
    c["ssk0"] = two(sin / scale)
    _CONST_CACHE.update(c)
    return c


def _stage_inputs(inputs):
    A = np.ascontiguousarray
    g = lambda k: np.asarray(inputs[k])
    s = {}
    s["inw"] = A(g("L0_attn_in_proj_w").T)
    s["convw"] = A(g("L0_attn_conv_w"))
    s["convb"] = A(g("L0_attn_conv_b"))
    s["xpw"] = A(g("L0_attn_x_proj_w").T)
    s["dtw"] = A(g("L0_attn_dt_proj_w").T)
    s["dtb"] = A(g("L0_attn_dt_proj_b"))
    s["alog"] = A(g("L0_attn_A_log"))
    s["dp"] = A(g("L0_attn_Dp"))
    s["outw"] = A(g("L0_attn_out_proj_w").T)
    s["w1m"] = A(g("L0_attn_w1").T)
    s["b1m"] = A(g("L0_attn_b1"))
    s["w2m"] = A(g("L0_attn_w2").T)
    s["b2m"] = A(g("L0_attn_b2"))
    s["wqkv"] = A(g("L0_ffn_wqkv"))
    s["qs"] = A(g("L0_ffn_q_scale"))
    s["ks"] = A(g("L0_ffn_k_scale"))
    s["wo"] = A(g("L0_ffn_wo"))
    s["lng"] = A(g("L0_ffn_ln_g"))
    s["lnb"] = A(g("L0_ffn_ln_b"))
    w1f = g("L0_ffn_w1")
    pad = HIDP - HID
    s["w1fa"] = A(np.pad(w1f[:, :, :HID], ((0, 0), (0, 0), (0, pad))))
    s["w1fg"] = A(np.pad(w1f[:, :, HID:], ((0, 0), (0, 0), (0, pad))))
    b1f = g("L0_ffn_b1")
    s["b1fa"] = A(np.pad(b1f[:, :HID], ((0, 0), (0, pad))))
    s["b1fg"] = A(np.pad(b1f[:, HID:], ((0, 0), (0, pad))))
    s["w2f"] = A(np.pad(g("L0_ffn_w2"), ((0, 0), (0, pad), (0, 0))))
    s["b2f"] = A(g("L0_ffn_b2"))
    s.update(_consts())
    return s


def _rk(ap, p=128):
    return ap.rearrange("(k p) t -> p k t", p=p)


def _on(name):
    return _PHASES is None or name in _PHASES


def build_nc():
    nc = bass.Bass("TRN2", target_bir_lowering=False, debug=False,
                   detect_race_conditions=False)
    nc.ctx.enter_context(
        nc.allow_non_contiguous_dma(reason="small vector/stat loads"))
    dt_ = nc.dram_tensor

    xT = dt_("xT", [D, L], F32, kind="ExternalInput").ap()
    inw = dt_("inw", [D, HM], F32, kind="ExternalInput").ap()
    convw = dt_("convw", [DIN, 3], F32, kind="ExternalInput").ap()
    convb = dt_("convb", [DIN], F32, kind="ExternalInput").ap()
    xpw = dt_("xpw", [DIN, 96], F32, kind="ExternalInput").ap()
    dtw = dt_("dtw", [DTR, DIN], F32, kind="ExternalInput").ap()
    dtb = dt_("dtb", [DIN], F32, kind="ExternalInput").ap()
    alog = dt_("alog", [DIN, NST], F32, kind="ExternalInput").ap()
    dpv = dt_("dp", [DIN], F32, kind="ExternalInput").ap()
    outw = dt_("outw", [DIN, D], F32, kind="ExternalInput").ap()
    w1m = dt_("w1m", [D, HM], F32, kind="ExternalInput").ap()
    b1m = dt_("b1m", [HM], F32, kind="ExternalInput").ap()
    w2m = dt_("w2m", [HM, D], F32, kind="ExternalInput").ap()
    b2m = dt_("b2m", [D], F32, kind="ExternalInput").ap()
    wqkv = dt_("wqkv", [NL, D, 3 * INNER], F32, kind="ExternalInput").ap()
    qs = dt_("qs", [NL, DH], F32, kind="ExternalInput").ap()
    ks = dt_("ks", [NL, DH], F32, kind="ExternalInput").ap()
    wo = dt_("wo", [NL, INNER, D], F32, kind="ExternalInput").ap()
    lng = dt_("lng", [NL, D], F32, kind="ExternalInput").ap()
    lnbv = dt_("lnb", [NL, D], F32, kind="ExternalInput").ap()
    w1fa = dt_("w1fa", [NL, D, HIDP], F32, kind="ExternalInput").ap()
    w1fg = dt_("w1fg", [NL, D, HIDP], F32, kind="ExternalInput").ap()
    b1fa = dt_("b1fa", [NL, HIDP], F32, kind="ExternalInput").ap()
    b1fg = dt_("b1fg", [NL, HIDP], F32, kind="ExternalInput").ap()
    w2f = dt_("w2f", [NL, HIDP, D], F32, kind="ExternalInput").ap()
    b2f = dt_("b2f", [NL, D], F32, kind="ExternalInput").ap()
    ident = dt_("ident", [128, 128], BF16, kind="ExternalInput").ap()
    ones1 = dt_("ones1", [128, 1], BF16, kind="ExternalInput").ap()
    onesf = dt_("onesf", [1, 128], F32, kind="ExternalInput").ap()
    g8c = dt_("g8", [4, 8, 128], F32, kind="ExternalInput").ap()
    e8 = dt_("e8", [4, 128, 8], BF16, kind="ExternalInput").ap()
    maskownT = dt_("maskownT", [128, 128], BF16, kind="ExternalInput").ap()
    maskprevT = dt_("maskprevT", [128, 128], BF16, kind="ExternalInput").ap()
    rotmT = dt_("rotmT", [128, 128], BF16, kind="ExternalInput").ap()
    csq0 = dt_("csq0", [128, L], F32, kind="ExternalInput").ap()
    ssq0 = dt_("ssq0", [128, L], F32, kind="ExternalInput").ap()
    csk0 = dt_("csk0", [128, L], F32, kind="ExternalInput").ap()
    ssk0 = dt_("ssk0", [128, L], F32, kind="ExternalInput").ap()

    outT = dt_("outT", [D, L], F32, kind="ExternalOutput").ap()

    s_inwb = dt_("s_inwb", [D, HM], BF16).ap()
    s_outwb = dt_("s_outwb", [DIN, D], BF16).ap()
    s_w1mb = dt_("s_w1mb", [D, HM], BF16).ap()
    s_w2mb = dt_("s_w2mb", [HM, D], BF16).ap()
    s_w1fab = dt_("s_w1fab", [NL, D, HIDP], BF16).ap()
    s_w1fgb = dt_("s_w1fgb", [NL, D, HIDP], BF16).ap()
    s_w2fb = dt_("s_w2fb", [NL, HIDP, D], BF16).ap()
    s_ln1 = dt_("s_ln1", [D, L], F32).ap()
    s_ygb = dt_("s_ygb", [DIN, L], BF16).ap()
    s_x2 = dt_("s_x2", [D, L], F32).ap()
    s_xf0 = dt_("s_xf0", [D, L], F32).ap()
    s_xm = [dt_("s_xm0", [D, L], F32).ap(), dt_("s_xm1", [D, L], F32).ap()]
    s_xl = [dt_("s_xl1", [D, L], F32).ap(), dt_("s_xl2", [D, L], F32).ap()]
    s_dbc = dt_("s_dbc", [32, TCM], BF16).ap()

    dbg = None
    if _DBG == "all":
        dbg = dt_("dbg", [7 * D + 3 * DIN, L], F32, kind="ExternalOutput").ap()
    elif _DBG is not None:
        dbg = dt_("dbg", [DIN, L], F32, kind="ExternalOutput").ap()

    p = Prog(nc)
    for s in ("mm", "vv", "ss", "out", "ax", "sp",
              "swiA", "swiB", "swoA", "swoB", "sw1A", "sw1B", "sw2A", "sw2B",
              "swaA", "swaB", "swgA", "swgB", "swfA", "swfB",
              "sbA", "sbB", "gpc", "gbA", "gbB", "gbc", "gxf", "gw"):
        p.sem(s)

    sb = nc.sbuf_tensor
    pst = nc.psum_tensor
    perstack = contextlib.ExitStack()
    E = perstack.enter_context
    identt = E(sb("identt", [128, 128], BF16))
    onest = E(sb("onest", [128, 1], BF16))
    onesft = E(sb("onesft", [1, 128], F32))
    g8t = E(sb("g8t", [8, 4, 128], F32))
    e8t = E(sb("e8t", [128, 4, 8], BF16))
    maskot = E(sb("maskot", [128, 128], BF16))
    maskpt = E(sb("maskpt", [128, 128], BF16))
    rotmt = E(sb("rotmt", [128, 128], BF16))
    cwt = E(sb("cwt", [128, KDI, 3], F32))
    cbt = E(sb("cbt", [128, KDI], F32))
    dtbt = E(sb("dtbt", [128, KDI], F32))
    dpt = E(sb("dpt", [128, KDI], F32))
    negP = E(sb("negP", [128, KDI, NST], F32))
    carry = E(sb("carry", [128, KDI, NST], F32))
    carry2 = E(sb("carry2", [128, KDI, 2], F32))
    b1mv = E(sb("b1mv", [128, KHM], F32))
    b2mv = E(sb("b2mv", [128, KD], F32))
    lngv = E(sb("lngv", [128, NL, KD], F32))
    lnbt = E(sb("lnbt", [128, NL, KD], F32))
    b1fav = E(sb("b1fav", [128, NL, KHID], F32))
    b1fgv = E(sb("b1fgv", [128, NL, KHID], F32))
    b2fv = E(sb("b2fv", [128, NL, KD], F32))
    qs2 = E(sb("qs2", [128, NL], F32))
    qs2r = E(sb("qs2r", [128, NL], F32))
    ks2 = E(sb("ks2", [128, NL], F32))
    ks2r = E(sb("ks2r", [128, NL], F32))
    epsc = E(sb("epsc", [128, 1], F32))
    onec = E(sb("onec", [128, 1], F32))
    xpwb = E(sb("xpwb", [128, KDI, 96], BF16))
    dtwb = E(sb("dtwb", [64, KDI, 128], BF16))

    def ln_stats(src, psrA, psrB, rows, cast16, T, psr_guard,
                 wait_extra=(), shift=None, state=None):
        castg = state.get("cast") if state is not None else None
        rowsg = state.get("rows") if state is not None else None
        p.emit("scalar", lambda e: e.copy(out=cast16[:], in_=src[:]),
               wait=wait_extra + ((castg,) if castg else ()), inc=(("ss", 1),))
        scast = p.cur("ss")

        def rhs(buf, ct):
            if shift is None:
                return buf[:, ct, 0:T]
            return buf[:, ct, 1:T + 1] if ct < KD // 2 else buf[:, ct, 0:T]

        for ct in range(KD):
            p.emit("tensor",
                   lambda e, ct=ct: e.matmul(
                       psrA[0:1, 0:T], onest[:], rhs(cast16, ct),
                       start=(ct == 0), stop=(ct == KD - 1)),
                   wait=((psr_guard[0], ("ss", scast)) if psr_guard[0]
                         else (("ss", scast),)) if ct == 0 else (),
                   inc=((("mm", 1),) if ct == KD - 1 else ()))
        amm = p.cur("mm")
        p.emit("scalar", lambda e: e.activation(cast16[:], src[:], AF.Square),
               wait=(("mm", amm),), inc=(("ss", 1),))
        scast = p.cur("ss")
        for ct in range(KD):
            p.emit("tensor",
                   lambda e, ct=ct: e.matmul(
                       psrB[0:1, 0:T], onest[:], rhs(cast16, ct),
                       start=(ct == 0), stop=(ct == KD - 1)),
                   wait=(("ss", scast),) if ct == 0 else (),
                   inc=((("mm", 1),) if ct == KD - 1 else ()))
        smm = p.cur("mm")
        p.emit("vector", lambda e: e.tensor_scalar(
            rows[:, 0, :], psrA[0:1, 0:T], 1.0 / D, None, ALU.mult),
            wait=(("mm", smm),) + ((rowsg,) if rowsg else ()), inc=(("vv", 1),))
        p.emit("vector", lambda e: e.tensor_scalar(
            rows[:, 1, :], psrB[0:1, 0:T], 1.0 / D, None, ALU.mult),
            inc=(("vv", 1),))
        p.emit("vector", lambda e: e.tensor_tensor(
            rows[:, 2, :], rows[:, 0, :], rows[:, 0, :], ALU.mult),
            inc=(("vv", 1),))
        p.emit("vector", lambda e: e.tensor_tensor(
            rows[:, 1, :], rows[:, 1, :], rows[:, 2, :], ALU.subtract),
            inc=(("vv", 1),))
        svv = p.cur("vv")
        p.emit("scalar", lambda e: e.activation(
            rows[:, 2, :], rows[:, 1, :], AF.Sqrt, bias=epsc[0:1, :]),
            wait=(("vv", svv),), inc=(("ss", 1),))
        p.emit("vector", lambda e: e.reciprocal(rows[:, 3, :], rows[:, 2, :]),
               wait=(("ss", p.cur("ss")), ("vv", None)), inc=(("vv", 1),))
        rvv = p.cur("vv")
        p.emit("tensor", lambda e: e.matmul(
            psrA[:, 0:T], onesft[:], rows[:, 0, :], start=True, stop=True),
            wait=(("vv", rvv),))
        p.emit("tensor", lambda e: e.matmul(
            psrB[:, 0:T], onesft[:], rows[:, 3, :], start=True, stop=True),
            inc=(("mm", 1),))
        repmm = p.cur("mm")
        if state is not None:
            state["cast"] = ("mm", smm)
            state["rows"] = ("mm", repmm)
            state["scast"] = ("ss", scast)
        return repmm

    with contextlib.ExitStack() as st:
        Es = st.enter_context
        alogt = Es(sb("alogt", [128, KDI, NST], F32))
        bb = [Es(sb(f"bb{i}", [128, 4096], BF16)) for i in range(2)]
        def d(q, out, in_):
            p.dma(q, "ax" if q == "sync" else "gpc", out, in_)
        d("sync", identt[:], ident)
        d("sync", onest[:], ones1)
        d("sync", onesft[:], onesf)
        d("sync", g8t[:], g8c.transpose([1, 0, 2]))
        d("sync", e8t[:], e8.transpose([1, 0, 2]))
        d("sync", maskot[:], maskownT)
        d("sync", maskpt[:], maskprevT)
        d("sync", rotmt[:], rotmT)
        d("sync", cwt[:], convw.rearrange("(k p) c -> p k c", p=128))
        d("sync", cbt[:], convb.rearrange("(k p) -> p k", p=128))
        d("sync", dtbt[:], dtb.rearrange("(k p) -> p k", p=128))
        d("sync", dpt[:], dpv.rearrange("(k p) -> p k", p=128))
        d("sync", b1mv[:], b1m.rearrange("(k p) -> p k", p=128))
        d("sync", b2mv[:], b2m.rearrange("(k p) -> p k", p=128))
        d("sync", lngv[:], lng.rearrange("l (k p) -> p l k", p=128))
        d("sync", lnbt[:], lnbv.rearrange("l (k p) -> p l k", p=128))
        d("sync", b1fav[:], b1fa.rearrange("l (k p) -> p l k", p=128))
        d("sync", b1fgv[:], b1fg.rearrange("l (k p) -> p l k", p=128))
        d("sync", b2fv[:], b2f.rearrange("l (k p) -> p l k", p=128))
        for l in range(NL):
            for h in range(2):
                d("sync", qs2[64 * h:64 * h + 64, l:l + 1], qs[l, :].unsqueeze(1))
                d("sync", ks2[64 * h:64 * h + 64, l:l + 1], ks[l, :].unsqueeze(1))
                d("sync", qs2r[64 * h:64 * h + 32, l:l + 1], qs[l, 32:64].unsqueeze(1))
                d("sync", qs2r[64 * h + 32:64 * h + 64, l:l + 1], qs[l, 0:32].unsqueeze(1))
                d("sync", ks2r[64 * h:64 * h + 32, l:l + 1], ks[l, 32:64].unsqueeze(1))
                d("sync", ks2r[64 * h + 32:64 * h + 64, l:l + 1], ks[l, 0:32].unsqueeze(1))
        d("gpsimd", xpwb[:], xpw.rearrange("(k p) o -> p k o", p=128))
        d("gpsimd", dtwb[:], dtw.rearrange("c (k p) -> c k p", p=128))
        d("sync", alogt[:], alog.rearrange("(k p) n -> p k n", p=128))
        p.emit("scalar", lambda e: e.activation(negP[:], alogt[:], AF.Exp),
               wait=(("ax", None), ("gpc", None)), inc=(("ss", 1),))
        p.emit("vector", lambda e: e.tensor_scalar(
            negP[:], negP[:], -1.0, None, ALU.mult),
            wait=(("ss", None),), inc=(("vv", 1),))
        p.emit("vector", lambda e: e.memset(epsc[:], EPS), inc=(("vv", 1),))
        p.emit("vector", lambda e: e.memset(onec[:], 1.0), inc=(("vv", 1),))
        p.emit("vector", lambda e: e.memset(carry[:], 0.0), inc=(("vv", 1),))
        p.emit("vector", lambda e: e.memset(carry2[:], 0.0), inc=(("vv", 1),))

        jobs = []

        def add_rows(src, dst, rows_, cols):
            for r0 in range(0, rows_, 128):
                jobs.append((src[r0:r0 + 128, :], dst[r0:r0 + 128, :], cols))

        add_rows(inw, s_inwb, D, HM)
        add_rows(outw, s_outwb, DIN, D)
        add_rows(w1m, s_w1mb, D, HM)
        add_rows(w2m, s_w2mb, HM, D)
        for l in range(NL):
            add_rows(w1fa[l], s_w1fab[l], D, HIDP)
            add_rows(w1fg[l], s_w1fgb[l], D, HIDP)
            add_rows(w2f[l], s_w2fb[l], HIDP, D)
        for i, (srcap, dstap, cols) in enumerate(jobs):
            b = "AB"[i % 2]
            buf = bb[i % 2]
            p.dma("gpsimd", "gb" + b, buf[:, 0:cols], srcap,
                  wait=(("sb" + b, None),))
            p.dma("sync", "sb" + b, dstap, buf[:, 0:cols],
                  wait=(("gb" + b, None),))
        p.fence()
        p.flush_phase()

    if not _on("m1"):
        ncm_eff = 0
    else:
        ncm_eff = NCM
    with contextlib.ExitStack() as st:
        Es = st.enter_context
        T = TCM
        x0c = Es(sb("x0c", [128, KD, T], F32))
        cast16 = Es(sb("cast16", [128, KD, T], BF16))
        rows = Es(sb("rows", [1, 4, T], F32))
        ln1c = Es(sb("ln1c", [128, KD, T], F32))
        lnb16 = Es(sb("lnb16", [128, KD, T], BF16))
        wbuf = [Es(sb(f"m1wb{i}", [128, KD, 128], BF16)) for i in range(2)]
        xpr = Es(sb("xpr", [128, KDI, T + 2], F32))
        szt = Es(sb("szt", [128, KDI, T], F32))
        dtt = Es(sb("dtt", [128, KDI, T], F32))
        xp2 = Es(sb("xp2", [128, KDI, T], F32))
        xp2b = Es(sb("xp2b", [128, KDI, T], BF16))
        dbcb = Es(sb("dbcb", [32, T], BF16))
        dtinb = Es(sb("dtinb", [64, T], BF16))
        bcrep = Es(sb("bcrep", [128, 32, T], BF16))
        at2 = [Es(sb(f"at{i}", [128, NST, T], F32)) for i in range(2)]
        btl = Es(sb("btl", [128, NST, T], BF16))
        ht = Es(sb("ht", [128, NST, T], F32))
        tmp16w = Es(sb("tmp16w", [128, NST], F32))
        dtx = Es(sb("dtx", [128, T], F32))
        ybuf = Es(sb("ybuf", [128, T], F32))
        ct1 = Es(sb("ct1", [128, T], F32))
        ct2 = Es(sb("ct2", [128, T], F32))
        etmp = Es(sb("etmp", [128, T], F32))
        ygb = Es(sb("ygb", [128, KDI, T], BF16))
        ps = [Es(pst(f"m1ps{i}", [128, 512], F32)) for i in range(4)]
        psrA = Es(pst("m1psrA", [128, 512], F32))
        psrB = Es(pst("m1psrB", [128, 512], F32))
        bankg = [None] * 4
        wguard = [None, None]
        psrg = [None]

        def bw(g):
            return (g,) if g else ()

        brep = bcrep[:, 0:16, :]
        crep = bcrep[:, 16:32, :]
        lnst = {}
        atg = [None, None]
        gx0 = ()
        glnb = None
        ghalo = None
        gsd = None
        for c in range(ncm_eff):
            t0 = c * T
            p.dma("sync", "ax", x0c[:], _rk(xT)[:, :, t0:t0 + T], wait=gx0)
            repmm = ln_stats(x0c, psrA, psrB, rows, cast16, T, psrg,
                             wait_extra=(("ax", None),), state=lnst)
            p.emit("vector", lambda e: e.tensor_tensor(
                ln1c[:], x0c[:],
                psrA[:, 0:T].unsqueeze(1).to_broadcast([128, KD, T]),
                ALU.subtract),
                wait=(("mm", repmm), ("sp", None)), inc=(("vv", 1),))
            p.emit("vector", lambda e: e.tensor_tensor(
                ln1c[:], ln1c[:],
                psrB[:, 0:T].unsqueeze(1).to_broadcast([128, KD, T]),
                ALU.mult),
                inc=(("vv", 1),))
            lnvv = p.cur("vv")
            psrg[0] = ("vv", lnvv)
            gx0 = (lnst["scast"], ("vv", lnvv))
            p.emit("scalar", lambda e: e.copy(out=lnb16[:], in_=ln1c[:]),
                   wait=(("vv", lnvv),) + ((glnb,) if glnb else ()),
                   inc=(("ss", 1),))
            lnss = p.cur("ss")
            p.dma("sync", "sp", _rk(s_ln1)[:, :, t0:t0 + T], ln1c[:],
                  wait=(("vv", lnvv),))
            p.emit("scalar", lambda e: e.copy(out=xpr[:, :, 0:2], in_=carry2[:]),
                   wait=((ghalo,) if ghalo else ()),
                   inc=(("ss", 1),))
            for ot in range(32):
                wb = wbuf[ot % 2]
                wsem = "swi" + "AB"[ot % 2]
                p.dma("sync", wsem, wb[:], _rk(s_inwb)[:, :, ot * 128:(ot + 1) * 128],
                      wait=bw(wguard[ot % 2]))
                ldw = (wsem, p.cur(wsem))
                bk = ot % 4
                for ct in range(KD):
                    p.emit("tensor",
                           lambda e, ct=ct, wb=wb, bk=bk: e.matmul(
                               ps[bk][:, 0:T], wb[:, ct, :],
                               lnb16[:, ct, :],
                               start=(ct == 0), stop=(ct == KD - 1)),
                           wait=(bw(bankg[bk]) + (ldw, ("ss", lnss))) if ct == 0 else (),
                           inc=((("mm", 1),) if ct == KD - 1 else ()))
                mmc = p.cur("mm")
                kk = ot if ot < KDI else ot - KDI
                sdw = ((("vv", gsd[kk]),) if gsd is not None else ())
                if ot < KDI:
                    p.emit("scalar",
                           lambda e, ot=ot, bk=bk: e.copy(
                               out=xpr[:, ot, 2:2 + T], in_=ps[bk][:, 0:T]),
                           wait=(("mm", mmc),) + sdw, inc=(("ss", 1),))
                else:
                    p.emit("scalar",
                           lambda e, ot=ot, bk=bk: e.activation(
                               szt[:, ot - KDI, :], ps[bk][:, 0:T], AF.Silu),
                           wait=(("mm", mmc),) + sdw, inc=(("ss", 1),))
                bankg[bk] = ("ss", p.cur("ss"))
                wguard[ot % 2] = ("mm", p.cur("mm"))
            glnb = ("mm", p.cur("mm"))
            p.emit("scalar", lambda e: e.copy(out=carry2[:], in_=xpr[:, :, T:T + 2]),
                   inc=(("ss", 1),))
            for k in range(KDI):
                p.emit("scalar",
                       lambda e, k=k: e.activation(
                           ct1[:], xpr[:, k, 0:T], AF.Identity,
                           scale=cwt[:, k, 0:1]),
                       inc=(("ss", 1),))
                s1 = p.cur("ss")
                p.emit("vector",
                       lambda e, k=k: e.scalar_tensor_tensor(
                           ct2[:], xpr[:, k, 1:1 + T], cwt[:, k, 1:2], ct1[:],
                           ALU.mult, ALU.add),
                       wait=(("ss", s1),), inc=(("vv", 1),))
                p.emit("vector",
                       lambda e, k=k: e.scalar_tensor_tensor(
                           ct1[:], xpr[:, k, 2:2 + T], cwt[:, k, 2:3], ct2[:],
                           ALU.mult, ALU.add),
                       inc=(("vv", 1),))
                cvv = p.cur("vv")
                p.emit("scalar",
                       lambda e, k=k: e.activation(
                           xp2[:, k, :], ct1[:], AF.Silu, bias=cbt[:, k:k + 1]),
                       wait=(("vv", cvv),), inc=(("ss", 1),))
                p.emit("scalar",
                       lambda e, k=k: e.copy(out=xp2b[:, k, :], in_=xp2[:, k, :]),
                       inc=(("ss", 1),))
            cvss = p.cur("ss")
            ghalo = ("vv", p.cur("vv"))
            for ct in range(KDI):
                p.emit("tensor",
                       lambda e, ct=ct: e.matmul(
                           psrA[0:96, 0:T], xpwb[:, ct, :], xp2b[:, ct, :],
                           start=(ct == 0), stop=(ct == KDI - 1)),
                       wait=(bw(psrg[0]) + (("ss", cvss),)) if ct == 0 else (),
                       inc=((("mm", 1),) if ct == KDI - 1 else ()))
            xpmm = p.cur("mm")
            p.emit("scalar", lambda e: e.copy(out=dbcb[:], in_=psrA[64:96, 0:T]),
                   wait=(("mm", xpmm), ("sp", None)), inc=(("ss", 1),))
            p.emit("scalar", lambda e: e.copy(out=dtinb[:], in_=psrA[0:64, 0:T]),
                   inc=(("ss", 1),))
            psrg[0] = ("ss", p.cur("ss"))
            dbss = p.cur("ss")
            p.dma("sync", "sp", s_dbc, dbcb[:], wait=(("ss", dbss),))
            p.dma("sync", "gbc", bcrep[:],
                  s_dbc.unsqueeze(0).partition_broadcast(128),
                  wait=(("sp", None),) + ((("vv", gsd[KDI - 1]),) if gsd is not None else ()))
            bcdg = p.cur("gbc")
            for ot in range(KDI):
                bk = ot % 4
                p.emit("tensor",
                       lambda e, ot=ot, bk=bk: e.matmul(
                           ps[bk][:, 0:T], dtwb[:, ot, :], dtinb[:],
                           start=True, stop=True),
                       wait=bw(bankg[bk]) + ((("ss", dbss),) if ot == 0 else ()),
                       inc=(("mm", 1),))
                mmc = p.cur("mm")
                p.emit("scalar",
                       lambda e, ot=ot, bk=bk: e.activation(
                           etmp[:], ps[bk][:, 0:T], AF.Exp,
                           bias=dtbt[:, ot:ot + 1]),
                       wait=(("mm", mmc),), inc=(("ss", 1),))
                p.emit("scalar",
                       lambda e, ot=ot: e.activation(
                           dtt[:, ot, :], etmp[:], AF.Ln, bias=onec[:]),
                       inc=(("ss", 1),))
                bankg[bk] = ("ss", p.cur("ss"))
            sd = [0] * KDI
            for k in range(KDI):
                at = at2[k % 2]
                aw = atg[k % 2]
                for n in range(NST):
                    p.emit("scalar",
                           lambda e, k=k, n=n, at=at: e.activation(
                               at[:, n, :], dtt[:, k, :], AF.Exp,
                               scale=negP[:, k, n:n + 1]),
                           wait=(bw(aw) if n == 0 else ()),
                           inc=((("ss", 1),) if n == NST - 1 else ()))
                ass = p.cur("ss")
                p.emit("vector",
                       lambda e, k=k: e.tensor_tensor(
                           dtx[:], dtt[:, k, :], xp2[:, k, :], ALU.mult),
                       wait=(("ss", ass),), inc=(("vv", 1),))
                p.emit("vector",
                       lambda e: e.tensor_tensor(
                           btl[:], dtx[:].unsqueeze(1).to_broadcast([128, NST, T]),
                           brep, ALU.mult),
                       wait=(("gbc", bcdg),), inc=(("vv", 1),))
                p.emit("vector",
                       lambda e, k=k, at=at: e.tensor_tensor(
                           tmp16w[:], at[:, :, 0], carry[:, k, :], ALU.mult),
                       inc=(("vv", 1),))
                p.emit("vector",
                       lambda e: e.tensor_tensor(
                           btl[:, :, 0], btl[:, :, 0], tmp16w[:], ALU.add),
                       inc=(("vv", 1),))
                p.emit("vector", lambda e, at=at: e.memset(at[:, :, 0], 0.0),
                       inc=(("vv", 1),))
                p.emit("vector",
                       lambda e, at=at: e.tensor_tensor_scan(
                           ht[:].rearrange("p n t -> p (n t)"),
                           at[:].rearrange("p n t -> p (n t)"),
                           btl[:].rearrange("p n t -> p (n t)"),
                           0.0, ALU.mult, ALU.add),
                       inc=(("vv", 1),))
                p.emit("vector",
                       lambda e, k=k: e.tensor_copy(carry[:, k, :], ht[:, :, T - 1]),
                       inc=(("vv", 1),))
                p.emit("vector",
                       lambda e: e.tensor_tensor(btl[:], ht[:], crep, ALU.mult),
                       inc=(("vv", 1),))
                atg[k % 2] = ("vv", p.cur("vv"))
                p.emit("vector",
                       lambda e: e.tensor_tensor(
                           btl[:, 0:8, :], btl[:, 0:8, :], btl[:, 8:16, :],
                           ALU.add),
                       inc=(("vv", 1),))
                p.emit("vector",
                       lambda e: e.tensor_tensor(
                           btl[:, 0:4, :], btl[:, 0:4, :], btl[:, 4:8, :],
                           ALU.add),
                       inc=(("vv", 1),))
                p.emit("vector",
                       lambda e: e.tensor_tensor(
                           btl[:, 0:2, :], btl[:, 0:2, :], btl[:, 2:4, :],
                           ALU.add),
                       inc=(("vv", 1),))
                p.emit("vector",
                       lambda e: e.tensor_tensor(
                           ybuf[:], btl[:, 0, :], btl[:, 1, :], ALU.add),
                       inc=(("vv", 1),))
                p.emit("vector",
                       lambda e, k=k: e.scalar_tensor_tensor(
                           ybuf[:], xp2[:, k, :], dpt[:, k:k + 1], ybuf[:],
                           ALU.mult, ALU.add),
                       inc=(("vv", 1),))
                p.emit("vector",
                       lambda e, k=k: e.tensor_tensor(
                           ygb[:, k, :], ybuf[:], szt[:, k, :], ALU.mult),
                       wait=((("sp", None),) if k == 0 else ()),
                       inc=(("vv", 1),))
                sd[k] = p.cur("vv")
            gsd = sd
            ygvv = p.cur("vv")
            p.dma("sync", "sp", _rk(s_ygb)[:, :, t0:t0 + T], ygb[:],
                  wait=(("vv", ygvv),))
            if _DBG == "m1xp2":
                p.dma("sync", "sp", _rk(dbg)[:, 0:KDI, t0:t0 + T], xp2[:],
                      wait=(("ss", cvss),))
            if _DBG == "m1dt":
                p.dma("sync", "sp", _rk(dbg)[:, 0:KDI, t0:t0 + T], dtt[:],
                      wait=(("vv", ygvv),))
            if _DBG == "all":
                b2 = 7 * D + DIN
                p.dma("sync", "sp",
                      dbg[b2:b2 + DIN].rearrange("(k p) t -> p k t", p=128)[
                          :, :, t0:t0 + T],
                      xp2[:], wait=(("ss", cvss),))
                b3 = 7 * D + 2 * DIN
                p.dma("sync", "sp",
                      dbg[b3:b3 + DIN].rearrange("(k p) t -> p k t", p=128)[
                          :, :, t0:t0 + T],
                      dtt[:], wait=(("vv", ygvv),))
        p.fence()
        p.flush_phase()

    ncm2_eff = (L // 512) if _on("m2") else 0
    with contextlib.ExitStack() as st:
        Es = st.enter_context
        T = 512
        ygc = Es(sb("ygc", [128, KDI, T], BF16))
        ln1c = Es(sb("m2ln1", [128, KD, T], F32))
        x0c = Es(sb("m2x0", [128, KD, T], F32))
        x1c = Es(sb("x1c", [128, KD, T], F32))
        x1b = Es(sb("x1b", [128, KD, T], BF16))
        hbt = Es(sb("hbt", [128, KHM, T], BF16))
        aoc = Es(sb("aoc", [128, KD, T], F32))
        x2c = Es(sb("x2c", [128, KD, T], F32))
        cast16 = Es(sb("m2c16", [128, KD, T], BF16))
        rows = Es(sb("m2rows", [1, 4, T], F32))
        owb = [Es(sb(f"owb{i}", [128, KDI, 128], BF16)) for i in range(2)]
        w1b = [Es(sb(f"m2w1b{i}", [128, KD, 512], BF16)) for i in range(2)]
        w2b = [Es(sb(f"m2w2b{i}", [128, KHM, 128], BF16)) for i in range(2)]
        ps = [Es(pst(f"m2ps{i}", [128, 512], F32)) for i in range(4)]
        psrA = Es(pst("m2psrA", [128, 512], F32))
        psrB = Es(pst("m2psrB", [128, 512], F32))
        bankg = [None] * 4
        wguard = {}
        psrg = [None]

        def bw(g):
            return (g,) if g else ()

        lnst = {}
        gld = ()
        gx1c = None
        gx1b = None
        ghbt = None
        gaoc = None
        gx2w = ()
        for c in range(ncm2_eff):
            t0 = c * T
            p.dma("sync", "ax", ygc[:], _rk(s_ygb)[:, :, t0:t0 + T], wait=gld)
            p.dma("sync", "ax", ln1c[:], _rk(s_ln1)[:, :, t0:t0 + T])
            p.dma("sync", "ax", x0c[:], _rk(xT)[:, :, t0:t0 + T])
            ldx = ("ax", p.cur("ax"))
            for ot in range(KD):
                wb = owb[ot % 2]
                wsem = "swo" + "AB"[ot % 2]
                p.dma("sync", wsem, wb[:], _rk(s_outwb)[:, :, ot * 128:(ot + 1) * 128],
                      wait=bw(wguard.get(("ow", ot % 2))))
                ldw = (wsem, p.cur(wsem))
                bk = ot % 4
                for ct in range(KDI):
                    p.emit("tensor",
                           lambda e, ct=ct, wb=wb, bk=bk: e.matmul(
                               ps[bk][:, 0:T], wb[:, ct, :], ygc[:, ct, :],
                               start=(ct == 0), stop=(ct == KDI - 1)),
                           wait=(bw(bankg[bk]) + (ldw, ("ax", None))) if ct == 0 else (),
                           inc=((("mm", 1),) if ct == KDI - 1 else ()))
                mmc = p.cur("mm")
                wguard[("ow", ot % 2)] = ("mm", mmc)
                p.emit("vector",
                       lambda e, ot=ot, bk=bk: e.tensor_tensor(
                           x1c[:, ot, :], ps[bk][:, 0:T], ln1c[:, ot, :], ALU.add),
                       wait=(("mm", mmc),) + (bw(gx1c) if ot == 0 else ()),
                       inc=(("vv", 1),))
                bankg[bk] = ("vv", p.cur("vv"))
            outmm = p.cur("mm")
            x1vv = p.cur("vv")
            p.emit("scalar", lambda e: e.copy(out=x1b[:], in_=x1c[:]),
                   wait=(("vv", x1vv),) + bw(gx1b), inc=(("ss", 1),))
            x1ss = p.cur("ss")
            gx1c = ("ss", x1ss)
            for g in range(KHM // 4):
                wb = w1b[g % 2]
                wsem = "sw1" + "AB"[g % 2]
                p.dma("sync", wsem, wb[:], _rk(s_w1mb)[:, :, g * 512:(g + 1) * 512],
                      wait=bw(wguard.get(("w1", g % 2))))
                ldw = (wsem, p.cur(wsem))
                for otl in range(4):
                    ot = 4 * g + otl
                    for ct in range(KD):
                        p.emit("tensor",
                               lambda e, ct=ct, wb=wb, otl=otl: e.matmul(
                                   ps[otl][:, 0:T],
                                   wb[:, ct, 128 * otl:128 * otl + 128],
                                   x1b[:, ct, :],
                                   start=(ct == 0), stop=(ct == KD - 1)),
                               wait=(bw(bankg[otl]) + (ldw, ("ss", x1ss))) if ct == 0 else (),
                               inc=((("mm", 1),) if ct == KD - 1 else ()))
                    mmc = p.cur("mm")
                    p.emit("scalar",
                           lambda e, ot=ot, otl=otl: e.activation(
                               hbt[:, ot, :], ps[otl][:, 0:T], AF.Gelu,
                               bias=b1mv[:, ot:ot + 1]),
                           wait=(("mm", mmc),) + (bw(ghbt) if ot == 0 else ()),
                           inc=(("ss", 1),))
                    bankg[otl] = ("ss", p.cur("ss"))
                wguard[("w1", g % 2)] = ("mm", p.cur("mm"))
            gx1b = ("mm", p.cur("mm"))
            hbss = p.cur("ss")
            for ot in range(KD):
                wb = w2b[ot % 2]
                wsem = "sw2" + "AB"[ot % 2]
                p.dma("sync", wsem, wb[:], _rk(s_w2mb)[:, :, ot * 128:(ot + 1) * 128],
                      wait=bw(wguard.get(("w2", ot % 2))))
                ldw = (wsem, p.cur(wsem))
                bk = ot % 4
                for ct in range(KHM):
                    p.emit("tensor",
                           lambda e, ct=ct, wb=wb, bk=bk: e.matmul(
                               ps[bk][:, 0:T], wb[:, ct, :], hbt[:, ct, :],
                               start=(ct == 0), stop=(ct == KHM - 1)),
                           wait=(bw(bankg[bk]) + (ldw, ("ss", hbss))) if ct == 0 else (),
                           inc=((("mm", 1),) if ct == KHM - 1 else ()))
                mmc = p.cur("mm")
                wguard[("w2", ot % 2)] = ("mm", mmc)
                p.emit("vector",
                       lambda e, ot=ot, bk=bk: e.scalar_tensor_tensor(
                           aoc[:, ot, :], ps[bk][:, 0:T], b2mv[:, ot:ot + 1],
                           x1c[:, ot, :], ALU.add, ALU.add),
                       wait=(("mm", mmc),) + (bw(gaoc) if ot == 0 else ()),
                       inc=(("vv", 1),))
                bankg[bk] = ("vv", p.cur("vv"))
            ghbt = ("mm", p.cur("mm"))
            aovv = p.cur("vv")
            repmm = ln_stats(aoc, psrA, psrB, rows, cast16, T, psrg,
                             wait_extra=(("vv", aovv),), state=lnst)
            gaoc = lnst["scast"]
            p.emit("vector", lambda e: e.tensor_tensor(
                aoc[:], aoc[:],
                psrA[:, 0:T].unsqueeze(1).to_broadcast([128, KD, T]),
                ALU.subtract),
                wait=(("mm", repmm), ("sp", None)), inc=(("vv", 1),))
            p.emit("vector", lambda e: e.tensor_tensor(
                aoc[:], aoc[:],
                psrB[:, 0:T].unsqueeze(1).to_broadcast([128, KD, T]),
                ALU.mult),
                inc=(("vv", 1),))
            p.emit("vector", lambda e: e.tensor_tensor(
                x2c[:], aoc[:], x0c[:], ALU.add),
                wait=(ldx, ("sp", None)) + gx2w, inc=(("vv", 1),))
            x2vv = p.cur("vv")
            psrg[0] = ("vv", x2vv)
            p.dma("sync", "sp", _rk(s_x2)[:, :, t0:t0 + T], x2c[:],
                  wait=(("vv", x2vv),))
            repmm = ln_stats(x2c, psrA, psrB, rows, cast16, T, psrg,
                             wait_extra=(("vv", x2vv),), state=lnst)
            gx2w = (lnst["scast"],)
            p.emit("vector", lambda e: e.tensor_tensor(
                x2c[:], x2c[:],
                psrA[:, 0:T].unsqueeze(1).to_broadcast([128, KD, T]),
                ALU.subtract),
                wait=(("mm", repmm), ("sp", None)), inc=(("vv", 1),))
            p.emit("vector", lambda e: e.tensor_tensor(
                x2c[:], x2c[:],
                psrB[:, 0:T].unsqueeze(1).to_broadcast([128, KD, T]),
                ALU.mult),
                inc=(("vv", 1),))
            xfvv = p.cur("vv")
            psrg[0] = ("vv", xfvv)
            p.dma("sync", "sp", _rk(s_xf0)[:, :, t0:t0 + T], x2c[:],
                  wait=(("vv", xfvv),))
            gld = (("mm", outmm), ("vv", x2vv))
        p.fence()
        p.flush_phase()

    def fa_phase(l, src, dst):
        T = TCF
        with contextlib.ExitStack() as st:
            Es = st.enter_context
            wqkvb = Es(sb(f"L{l}wqkvb", [128, KD, 3 * INNER], BF16))
            wob = Es(sb(f"L{l}wob", [128, KIN, D], BF16))
            xfb = Es(sb(f"L{l}xfb", [128, KD, T + 1], BF16))
            xff = Es(sb(f"L{l}xff", [128, KD, T], F32))
            qf = Es(sb(f"L{l}qf", [128, KIN, T], F32))
            kf = Es(sb(f"L{l}kf", [128, KIN, T], F32))
            sqb = Es(sb(f"L{l}sqb", [128, T], BF16))
            qrb = Es(sb(f"L{l}qrb", [128, KIN, T], BF16))
            kfull = Es(sb(f"L{l}kfull", [128, KIN, T], BF16))
            kz = [Es(sb(f"L{l}kz{i}", [128, 2, KIN, T], BF16)) for i in range(2)]
            vb = Es(sb(f"L{l}vb", [128, KIN, T], BF16))
            vT = [Es(sb(f"L{l}vT{i}", [128, 4, KIN, 128], BF16)) for i in range(2)]
            rowq = Es(sb(f"L{l}rowq", [8, 4, T], F32))
            csqw = Es(sb(f"L{l}csqw", [128, T], F32))
            ssqw = Es(sb(f"L{l}ssqw", [128, T], F32))
            cskw = Es(sb(f"L{l}cskw", [128, T], F32))
            sskw = Es(sb(f"L{l}sskw", [128, T], F32))
            base4 = Es(sb(f"L{l}base4", [128, 4, T], F32))
            m1t = Es(sb(f"L{l}m1t", [128, T], F32))
            m2t = Es(sb(f"L{l}m2t", [128, T], F32))
            ut = Es(sb(f"L{l}ut", [128, T], F32))
            qb16 = Es(sb(f"L{l}qb16", [128, T], BF16))
            nums = Es(sb(f"L{l}nums", [128, 640], BF16))
            attnT = Es(sb(f"L{l}attnT", [128, 8, T], BF16))
            mxt = Es(sb(f"L{l}mxt", [128, 8], F32))
            o_sb = Es(sb(f"L{l}o_sb", [128, KIN, T], BF16))
            xmo = Es(sb(f"L{l}xmo", [128, KD, T], F32))
            simA = [Es(pst(f"L{l}simA{i}", [128, 512], F32)) for i in range(2)]
            simB = [Es(pst(f"L{l}simB{i}", [128, 512], F32)) for i in range(2)]
            psT = Es(pst(f"L{l}psT", [128, 1024], BF16))
            avp = [Es(pst(f"L{l}avp{i}", [128, 512], F32)) for i in range(2)]
            spare = Es(pst(f"L{l}spare", [128, 512], F32))
            gsim = [None, None]
            gsimB = [None, None]
            gpsT = [None]
            gsp = [None]
            gav = [None, None]
            gnums = [None]
            gattnT = [None]

            def bw(g):
                return (g,) if g else ()

            p.emit("vector", lambda e: e.memset(attnT[:], 0.0), inc=(("vv", 1),))
            p.emit("vector", lambda e: e.memset(kz[0][:], 0.0), inc=(("vv", 1),))
            p.emit("vector", lambda e: e.memset(kz[1][:], 0.0), inc=(("vv", 1),))
            p.dma("gpsimd", "gw", wqkvb[:], wqkv[l].rearrange("(k p) o -> p k o", p=128))
            p.dma("gpsimd", "gw", wob[:], wo[l].rearrange("(k p) o -> p k o", p=128))
            wdg = p.cur("gw")

            for w in range(NCF if _on(f"fa{l}") else 0):
                t0 = w * T
                cw = w % 2
                pw = 1 - cw
                if w == 0:
                    p.emit("vector", lambda e: e.memset(xfb[:, :, 0:1], 0.0),
                           inc=(("vv", 1),))
                    p.dma("gpsimd", "gxf", xfb[:, :, 1:T + 1], _rk(src)[:, :, 0:T])
                else:
                    p.dma("gpsimd", "gxf", xfb[:], _rk(src)[:, :, t0 - 1:t0 + T])
                p.dma("sync", "ax", xff[:], _rk(src)[:, :, t0:t0 + T])
                p.dma("sync", "ax", base4[:, 0, :], csq0[:, t0:t0 + T])
                p.dma("sync", "ax", base4[:, 1, :], ssq0[:, t0:t0 + T])
                p.dma("sync", "ax", base4[:, 2, :], csk0[:, t0:t0 + T])
                p.dma("sync", "ax", base4[:, 3, :], ssk0[:, t0:t0 + T])
                ldx, ldg = p.cur("ax"), p.cur("gxf")
                p.emit("vector", lambda e: e.tensor_scalar(
                    csqw[:], base4[:, 0, :], qs2[:, l:l + 1], 8.0,
                    ALU.mult, ALU.mult),
                    wait=(("ax", ldx),), inc=(("vv", 1),))
                p.emit("vector", lambda e: e.tensor_scalar(
                    ssqw[:], base4[:, 1, :], qs2r[:, l:l + 1], 8.0,
                    ALU.mult, ALU.mult),
                    inc=(("vv", 1),))
                p.emit("vector", lambda e: e.tensor_scalar(
                    cskw[:], base4[:, 2, :], ks2[:, l:l + 1], None, ALU.mult),
                    inc=(("vv", 1),))
                p.emit("vector", lambda e: e.tensor_scalar(
                    sskw[:], base4[:, 3, :], ks2r[:, l:l + 1], None, ALU.mult),
                    inc=(("vv", 1),))
                tabvv = p.cur("vv")

                qkbanks = [simA[0], simA[1], avp[0], avp[1]]
                qkg = [gsim[0], gsim[1], gav[0], gav[1]]
                for part in ("q", "k", "v"):
                    base_ot = {"q": 0, "k": KIN, "v": 2 * KIN}[part]
                    for j in range(KIN):
                        ot = base_ot + j
                        bank = qkbanks[j]
                        for ct in range(KD):
                            ap_rhs = (xfb[:, ct, 1:T + 1] if ct < KD // 2
                                      else xfb[:, ct, 0:T])
                            p.emit("tensor",
                                   lambda e, ct=ct, ot=ot, bank=bank, ap=ap_rhs: e.matmul(
                                       bank[:, 0:T],
                                       wqkvb[:, ct, 128 * ot:128 * ot + 128],
                                       ap, start=(ct == 0), stop=(ct == KD - 1)),
                                   wait=(bw(qkg[j]) +
                                         ((("gxf", ldg), ("gw", wdg),
                                           ("vv", None)) if ct == 0 else ())),
                                   inc=((("mm", 1),) if ct == KD - 1 else ()))
                        mmc = p.cur("mm")
                        if part in ("q", "k"):
                            dest = qf if part == "q" else kf
                            p.emit("scalar",
                                   lambda e, j=j, bank=bank, dest=dest: e.copy(
                                       out=dest[:, j, :], in_=bank[:, 0:T]),
                                   wait=(("mm", mmc),), inc=(("ss", 1),))
                            p.emit("scalar",
                                   lambda e, bank=bank: e.activation(
                                       sqb[:], bank[:, 0:T], AF.Square),
                                   wait=(("mm", None),), inc=(("ss", 1),))
                            sqs = p.cur("ss")
                            p.emit("tensor",
                                   lambda e, j=j: e.matmul(
                                       spare[0:8, 0:T], e8t[:, j, :], sqb[:],
                                       start=(j == 0), stop=(j == KIN - 1)),
                                   wait=(bw(gsp[0]) if j == 0 else ()) + (("ss", sqs),),
                                   inc=(("mm", 1),))
                            qkg[j] = ("ss", sqs)
                            if j == KIN - 1:
                                smm = p.cur("mm")
                                row = 0 if part == "q" else 1
                                p.emit("scalar",
                                       lambda e, row=row: e.activation(
                                           rowq[:, row, :], spare[0:8, 0:T],
                                           AF.Sqrt),
                                       wait=(("mm", smm),), inc=(("ss", 1),))
                                p.emit("vector",
                                       lambda e, row=row: e.tensor_scalar(
                                           rowq[:, row, :], rowq[:, row, :],
                                           1e-12, None, ALU.max),
                                       wait=(("ss", p.cur("ss")),),
                                       inc=(("vv", 1),))
                                p.emit("vector",
                                       lambda e, row=row: e.reciprocal(
                                           rowq[:, 2 + row, :], rowq[:, row, :]),
                                       wait=(("vv", None),), inc=(("vv", 1),))
                                gsp[0] = ("vv", p.cur("vv"))
                        else:
                            p.emit("scalar",
                                   lambda e, j=j, bank=bank: e.copy(
                                       out=vb[:, j, :], in_=bank[:, 0:T]),
                                   wait=(("mm", mmc),), inc=(("ss", 1),))
                            qkg[j] = ("ss", p.cur("ss"))
                gsim[0], gsim[1], gav[0], gav[1] = qkg
                rowvv = p.cur("vv")
                for part in (("q", "k") if _FA_STAGE >= 2 else ()):
                    src_f = qf if part == "q" else kf
                    dst_b = qrb if part == "q" else kfull
                    cst = csqw if part == "q" else cskw
                    sst = ssqw if part == "q" else sskw
                    row = 0 if part == "q" else 1
                    for j in range(KIN):
                        p.emit("tensor",
                               lambda e, j=j, row=row: e.matmul(
                                   spare[:, 0:T], g8t[:, j, :],
                                   rowq[:, 2 + row, :],
                                   start=True, stop=True),
                               wait=bw(gsp[0]) + (("vv", rowvv),),
                               inc=(("mm", 1),))
                        srmm = p.cur("mm")
                        p.emit("vector",
                               lambda e, j=j, src_f=src_f, cst=cst: e.tensor_tensor(
                                   m1t[:], src_f[:, j, :], cst[:], ALU.mult),
                               wait=(("vv", tabvv),), inc=(("vv", 1),))
                        p.emit("scalar",
                               lambda e, j=j, src_f=src_f: e.copy(
                                   out=qb16[:], in_=src_f[:, j, :]),
                               wait=(("mm", None),), inc=(("ss", 1),))
                        qbs = p.cur("ss")
                        p.emit("tensor",
                               lambda e: e.matmul(
                                   simB[0][:, 0:T], rotmt[:], qb16[:],
                                   start=True, stop=True),
                               wait=bw(gsimB[0]) + (("ss", qbs),),
                               inc=(("mm", 1),))
                        rmm = p.cur("mm")
                        p.emit("vector",
                               lambda e, sst=sst: e.tensor_tensor(
                                   m2t[:], simB[0][:, 0:T], sst[:], ALU.mult),
                               wait=(("mm", rmm),), inc=(("vv", 1),))
                        gsimB[0] = ("vv", p.cur("vv"))
                        p.emit("vector",
                               lambda e: e.tensor_tensor(ut[:], m1t[:], m2t[:],
                                                         ALU.add),
                               inc=(("vv", 1),))
                        p.emit("vector",
                               lambda e, j=j, dst_b=dst_b: e.tensor_tensor(
                                   dst_b[:, j, :], ut[:], spare[:, 0:T],
                                   ALU.mult),
                               wait=(("mm", srmm),), inc=(("vv", 1),))
                        gsp[0] = ("vv", p.cur("vv"))
                        if part == "k":
                            kvv = p.cur("vv")
                            p.emit("scalar",
                                   lambda e, j=j, cw=cw: e.copy(
                                       out=kz[cw][0:64, 0, j, :],
                                       in_=kfull[0:64, j, :]),
                                   wait=(("vv", kvv),), inc=(("ss", 1),))
                            p.emit("scalar",
                                   lambda e, j=j, cw=cw: e.copy(
                                       out=kz[cw][64:128, 1, j, :],
                                       in_=kfull[64:128, j, :]),
                                   inc=(("ss", 1),))
                qkvv = p.cur("vv")
                kzss = p.cur("ss")
                vbss = p.cur("ss")
                for half in (range(2) if _FA_STAGE >= 3 else ()):
                    for idx in range(8):
                        gi = half * 8 + idx
                        tb = gi // 4
                        pr = gi % 4
                        p.emit("tensor",
                               lambda e, pr=pr, tb=tb, idx=idx: e.transpose(
                                   psT[:, idx * 128:(idx + 1) * 128],
                                   vb[:, pr, 128 * tb:128 * tb + 128],
                                   identt[:]),
                               wait=(bw(gpsT[0]) + (("ss", vbss),)) if idx == 0 else (),
                               inc=((("mm", 1),) if idx == 7 else ()))
                    tmm = p.cur("mm")
                    p.emit("scalar",
                           lambda e, half=half, cw=cw: e.copy(
                               out=vT[cw][:].rearrange("p a h d -> p (a h d)")
                               [:, half * 1024:(half + 1) * 1024],
                               in_=psT[:]),
                           wait=(("mm", tmm),), inc=(("ss", 1),))
                    gpsT[0] = ("ss", p.cur("ss"))

                def slots(ib):
                    s = []
                    if w > 0:
                        for jt in range(ib, 4):
                            s.append(("prev", jt))
                    for jt in range(0, ib + 1):
                        s.append(("own", jt))
                    return s

                tiles = [(h, ib) for h in range(NH) for ib in range(4)]
                if _FA_NT is not None:
                    tiles = tiles[:_FA_NT]
                simdone = {}

                def emit_sim(h, ib, par):
                    sl = slots(ib)
                    nsl_ = len(sl)
                    for si, (which, jt) in enumerate(sl):
                        ksrc = kz[pw] if which == "prev" else kz[cw]
                        inB = si >= 4
                        tgt = (simB[par][:, 0:128] if inB
                               else simA[par][:, 128 * si:128 * si + 128])
                        mk = None
                        if which == "own" and jt == ib:
                            mk = maskot
                        if which == "prev" and jt == ib:
                            mk = maskpt
                        wt = ()
                        if si == 0:
                            wt = (bw(gsim[par]) + bw(gsimB[par])
                                  + (("vv", qkvv), ("ss", kzss)))
                        last = (si == nsl_ - 1)
                        p.emit("tensor",
                               lambda e, h=h, ib=ib, jt=jt, ksrc=ksrc, tgt=tgt, mk=mk: e.matmul(
                                   tgt,
                                   qrb[:, h // 2, 128 * ib:128 * ib + 128],
                                   ksrc[:, h % 2, h // 2,
                                        128 * jt:128 * jt + 128],
                                   start=True, stop=(mk is None)),
                               wait=wt,
                               inc=((("mm", 1),) if (last and mk is None) else ()))
                        if mk is not None:
                            p.emit("tensor",
                                   lambda e, mk=mk, tgt=tgt: e.matmul(
                                       tgt, mk[:], identt[:],
                                       start=False, stop=True),
                                   inc=((("mm", 1),) if last else ()))

                def soft_tile(h, ib, par):
                    sl = slots(ib)
                    nsl = len(sl)
                    nA = min(nsl, 4)
                    mmsim = simdone[(h, ib)]
                    if _FA_SOFT < 1:
                        gsim[par] = ("mm", mmsim)
                        gsimB[par] = ("mm", mmsim)
                        return
                    p.emit("vector",
                           lambda e, nA=nA, par=par: e.tensor_reduce(
                               mxt[:, 0:1], simA[par][:, 0:128 * nA],
                               AXX, ALU.max),
                           wait=(("mm", mmsim), ("ss", None)),
                           inc=(("vv", 1),))
                    if nsl > 4:
                        p.emit("vector",
                               lambda e, par=par: e.tensor_reduce(
                                   mxt[:, 1:2], simB[par][:, 0:128],
                                   AXX, ALU.max),
                               inc=(("vv", 1),))
                        p.emit("vector",
                               lambda e: e.tensor_tensor(
                                   mxt[:, 0:1], mxt[:, 0:1], mxt[:, 1:2],
                                   ALU.max),
                               inc=(("vv", 1),))
                    p.emit("vector",
                           lambda e: e.tensor_scalar(
                               mxt[:, 2:3], mxt[:, 0:1], -1.0, None, ALU.mult),
                           inc=(("vv", 1),))
                    mxvv = p.cur("vv")
                    if _FA_SOFT < 2:
                        gsim[par] = ("vv", mxvv)
                        gsimB[par] = ("vv", mxvv)
                        return
                    p.emit("scalar",
                           lambda e, nA=nA, par=par: e.activation(
                               nums[:, 0:128 * nA], simA[par][:, 0:128 * nA],
                               AF.Exp, bias=mxt[:, 2:3],
                               accum_out=mxt[:, 3:4]),
                           wait=(("vv", mxvv),) + bw(gnums[0]),
                           inc=(("ss", 1),))
                    if nsl > 4:
                        p.emit("scalar",
                               lambda e, par=par: e.activation(
                                   nums[:, 512:640], simB[par][:, 0:128],
                                   AF.Exp, bias=mxt[:, 2:3],
                                   accum_out=mxt[:, 4:5]),
                               inc=(("ss", 1),))
                    e1ss = p.cur("ss")
                    if _FA_SOFT < 3:
                        gsim[par] = ("ss", e1ss)
                        gsimB[par] = ("ss", e1ss)
                        return
                    if nsl > 4:
                        p.emit("vector",
                               lambda e: e.tensor_tensor(
                                   mxt[:, 5:6], mxt[:, 3:4], mxt[:, 4:5],
                                   ALU.add),
                               wait=(("ss", e1ss),), inc=(("vv", 1),))
                    else:
                        p.emit("vector",
                               lambda e: e.tensor_copy(mxt[:, 5:6], mxt[:, 3:4]),
                               wait=(("ss", e1ss),), inc=(("vv", 1),))
                    p.emit("scalar",
                           lambda e: e.activation(mxt[:, 6:7], mxt[:, 5:6],
                                                  AF.Ln),
                           wait=(("vv", p.cur("vv")),), inc=(("ss", 1),))
                    p.emit("vector",
                           lambda e: e.tensor_tensor(
                               mxt[:, 7:8], mxt[:, 2:3], mxt[:, 6:7],
                               ALU.subtract),
                           wait=(("ss", p.cur("ss")),), inc=(("vv", 1),))
                    b2vv = p.cur("vv")
                    if _FA_SOFT < 4:
                        gsim[par] = ("vv", b2vv)
                        gsimB[par] = ("vv", b2vv)
                        return
                    p.emit("scalar",
                           lambda e, nA=nA, par=par: e.activation(
                               nums[:, 0:128 * nA], simA[par][:, 0:128 * nA],
                               AF.Exp, bias=mxt[:, 7:8]),
                           wait=(("vv", b2vv),), inc=(("ss", 1),))
                    if nsl > 4:
                        p.emit("scalar",
                               lambda e, par=par: e.activation(
                                   nums[:, 512:640], simB[par][:, 0:128],
                                   AF.Exp, bias=mxt[:, 7:8]),
                               inc=(("ss", 1),))
                    ess = p.cur("ss")
                    gsim[par] = ("ss", ess)
                    gsimB[par] = ("ss", ess)
                    if _FA_SOFT < 5:
                        return
                    for si in range(nsl):
                        p.emit("tensor",
                               lambda e, si=si: e.transpose(
                                   psT[:, 128 * si:128 * si + 128],
                                   nums[:, 128 * si:128 * si + 128]
                                   if si < 4 else nums[:, 512:640],
                                   identt[:]),
                               wait=(bw(gpsT[0]) + (("ss", ess),)) if si == 0 else (),
                               inc=((("mm", 1),) if si == nsl - 1 else ()))
                    tmm = p.cur("mm")
                    gnums[0] = ("mm", tmm)
                    if _FA_SOFT < 6:
                        return
                    for si, (which, jt) in enumerate(sl):
                        idx = jt if which == "prev" else 4 + jt
                        p.emit("scalar",
                               lambda e, si=si, idx=idx, ib=ib: e.copy(
                                   out=attnT[:, idx, 128 * ib:128 * ib + 128],
                                   in_=psT[:, 128 * si:128 * si + 128]),
                               wait=((("mm", tmm),) + bw(gattnT[0])) if si == 0 else (),
                               inc=((("ss", 1),) if si == nsl - 1 else ()))
                    gpsT[0] = ("ss", p.cur("ss"))
                    atss = p.cur("ss")
                    if _FA_SOFT < 7:
                        return
                    bank = avp[h % 2]
                    first_av = (ib == 0)
                    last_av = (ib == 3)
                    for si, (which, jt) in enumerate(sl):
                        idx = jt if which == "prev" else 4 + jt
                        vsrc = vT[pw] if which == "prev" else vT[cw]
                        p.emit("tensor",
                               lambda e, h=h, ib=ib, jt=jt, idx=idx, vsrc=vsrc, bank=bank, si=si, nsl=nsl: e.matmul(
                                   bank[0:64, 128 * ib:128 * ib + 128],
                                   vsrc[:, jt, h // 2,
                                        64 * (h % 2):64 * (h % 2) + 64],
                                   attnT[:, idx, 128 * ib:128 * ib + 128],
                                   start=(si == 0),
                                   stop=(si == nsl - 1)),
                               wait=((bw(gav[h % 2]) + (("ss", atss),)) if (first_av and si == 0)
                                     else ((("ss", atss),) if si == 0 else ())),
                               inc=((("mm", 1),) if (last_av and si == nsl - 1) else ()))
                    gattnT[0] = ("mm", p.cur("mm"))
                    if last_av:
                        avmm = p.cur("mm")
                        p.emit("scalar",
                               lambda e, h=h, bank=bank: e.copy(
                                   out=o_sb[64 * (h % 2):64 * (h % 2) + 64,
                                            h // 2, :],
                                   in_=bank[0:64, 0:T]),
                               wait=(("mm", avmm),), inc=(("ss", 1),))
                        gav[h % 2] = ("ss", p.cur("ss"))

                if _FA_STAGE < 4:
                    tiles = []
                if tiles:
                    emit_sim(*tiles[0], 0)
                    simdone[tiles[0]] = p.cur("mm")
                for i, (h, ib) in enumerate(tiles):
                    if i + 1 < len(tiles):
                        nh_, nib = tiles[i + 1]
                        emit_sim(nh_, nib, (i + 1) % 2)
                        simdone[(nh_, nib)] = p.cur("mm")
                    soft_tile(h, ib, i % 2)
                osbss = p.cur("ss")
                for ot in (range(KD) if _FA_STAGE >= 5 else ()):
                    bank = simA[ot % 2]
                    for ct in range(KIN):
                        p.emit("tensor",
                               lambda e, ct=ct, ot=ot, bank=bank: e.matmul(
                                   bank[:, 0:T],
                                   wob[:, ct, 128 * ot:128 * ot + 128],
                                   o_sb[:, ct, :],
                                   start=(ct == 0), stop=(ct == KIN - 1)),
                               wait=(bw(gsim[ot % 2]) + (("ss", osbss),)) if ct == 0 else (),
                               inc=((("mm", 1),) if ct == KIN - 1 else ()))
                    mmc = p.cur("mm")
                    p.emit("vector",
                           lambda e, ot=ot, bank=bank: e.tensor_tensor(
                               xmo[:, ot, :], bank[:, 0:T], xff[:, ot, :],
                               ALU.add),
                           wait=(("mm", mmc), ("ax", ldx)), inc=(("vv", 1),))
                    gsim[ot % 2] = ("vv", p.cur("vv"))
                xmvv = p.cur("vv")
                p.dma("sync", "sp", _rk(dst)[:, :, t0:t0 + T], xmo[:],
                      wait=(("vv", xmvv),))
                p.fence()
            p.flush_phase()

    def fb_phase(l, src, dst):
        T = TCF
        with contextlib.ExitStack() as st:
            Es = st.enter_context
            xmc = Es(sb(f"L{l}xmc", [128, KD, T + 1], F32))
            xmb = Es(sb(f"L{l}xmb", [128, KD, T + 1], BF16))
            rows = Es(sb(f"L{l}fbrows", [1, 4, T], F32))
            tc_ = Es(sb(f"L{l}fbtc", [128, KD, T], F32))
            xrb = Es(sb(f"L{l}xrb", [128, KD, T], BF16))
            h4b = Es(sb(f"L{l}h4b", [128, KHID, T], BF16))
            glt = Es(sb(f"L{l}glt", [128, T], F32))
            xnc = Es(sb(f"L{l}xnc", [128, KD, T], F32))
            w1ab = [Es(sb(f"L{l}w1ab{i}", [128, KD, 256], BF16)) for i in range(2)]
            w1gb = [Es(sb(f"L{l}w1gb{i}", [128, KD, 256], BF16)) for i in range(2)]
            w2b = [Es(sb(f"L{l}fbw2b{i}", [128, KHID, 128], BF16)) for i in range(2)]
            ps = [Es(pst(f"L{l}fbps{i}", [128, 512], F32)) for i in range(4)]
            psrA = Es(pst(f"L{l}fbpsrA", [128, 512], F32))
            psrB = Es(pst(f"L{l}fbpsrB", [128, 512], F32))
            bankg = [None] * 4
            wguard = {}
            psrg = [None]

            def bw(g):
                return (g,) if g else ()

            lnst = {}
            gld = ()
            gxrb = None
            gh4b = None
            for c in range((_NCF_LIM or NCF) if _on(f"fb{l}") else 0):
                t0 = c * T
                if c == 0:
                    p.emit("vector", lambda e: e.memset(xmc[:, :, 0:1], 0.0),
                           inc=(("vv", 1),))
                    p.dma("sync", "ax", xmc[:, :, 1:T + 1], _rk(src)[:, :, 0:T])
                else:
                    p.dma("sync", "ax", xmc[:], _rk(src)[:, :, t0 - 1:t0 + T],
                          wait=gld)
                repmm = ln_stats(xmc, psrA, psrB, rows, xmb, T, psrg,
                                 wait_extra=(("ax", None),), shift=True,
                                 state=lnst)
                for ct in range(KD):
                    src_ap = (xmc[:, ct, 1:T + 1] if ct < KD // 2
                              else xmc[:, ct, 0:T])
                    p.emit("vector",
                           lambda e, ct=ct, src_ap=src_ap: e.tensor_tensor(
                               tc_[:, ct, :], src_ap, psrA[:, 0:T],
                               ALU.subtract),
                           wait=(("mm", repmm),) if ct == 0 else (),
                           inc=(("vv", 1),))
                p.emit("vector", lambda e: e.tensor_tensor(
                    tc_[:], tc_[:],
                    psrB[:, 0:T].unsqueeze(1).to_broadcast([128, KD, T]),
                    ALU.mult),
                    inc=(("vv", 1),))
                lnvv = p.cur("vv")
                psrg[0] = ("vv", lnvv)
                for ct in range(KD):
                    p.emit("scalar",
                           lambda e, ct=ct: e.activation(
                               xrb[:, ct, :], tc_[:, ct, :], AF.Identity,
                               scale=lngv[:, l, ct:ct + 1],
                               bias=lnbt[:, l, ct:ct + 1]),
                           wait=((("vv", lnvv),) + bw(gxrb)) if ct == 0 else (),
                           inc=(("ss", 1),))
                xrss = p.cur("ss")
                for g in range(KHID // 2):
                    pts = [2 * g, 2 * g + 1]
                    wa = w1ab[g % 2]
                    wg = w1gb[g % 2]
                    wsa = "swa" + "AB"[g % 2]
                    wsg = "swg" + "AB"[g % 2]
                    p.dma("sync", wsa, wa[:],
                          _rk(s_w1fab[l])[:, :, 256 * g:256 * (g + 1)],
                          wait=bw(wguard.get(("wa", g % 2))))
                    p.dma("sync", wsg, wg[:],
                          _rk(s_w1fgb[l])[:, :, 256 * g:256 * (g + 1)],
                          wait=bw(wguard.get(("wg", g % 2))))
                    ldw = ((wsa, p.cur(wsa)), (wsg, p.cur(wsg)))
                    for ii, pt in enumerate(pts):
                        bka = (2 * ii) % 4
                        bkg_ = (2 * ii + 1) % 4
                        for ct in range(KD):
                            p.emit("tensor",
                                   lambda e, ct=ct, wa=wa, ii=ii, bka=bka: e.matmul(
                                       ps[bka][:, 0:T],
                                       wa[:, ct, 128 * ii:128 * ii + 128],
                                       xrb[:, ct, :],
                                       start=(ct == 0), stop=(ct == KD - 1)),
                                   wait=(bw(bankg[bka]) +
                                         (ldw + (("ss", xrss),) if ct == 0 else ())),
                                   inc=((("mm", 1),) if ct == KD - 1 else ()))
                        for ct in range(KD):
                            p.emit("tensor",
                                   lambda e, ct=ct, wg=wg, ii=ii, bkg_=bkg_: e.matmul(
                                       ps[bkg_][:, 0:T],
                                       wg[:, ct, 128 * ii:128 * ii + 128],
                                       xrb[:, ct, :],
                                       start=(ct == 0), stop=(ct == KD - 1)),
                                   wait=(bw(bankg[bkg_]) if ct == 0 else ()),
                                   inc=((("mm", 1),) if ct == KD - 1 else ()))
                        mmc = p.cur("mm")
                        p.emit("scalar",
                               lambda e, pt=pt, bkg_=bkg_: e.activation(
                                   glt[:], ps[bkg_][:, 0:T], AF.Gelu,
                                   bias=b1fgv[:, l, pt:pt + 1]),
                               wait=(("mm", mmc), ("vv", None)), inc=(("ss", 1),))
                        gss = p.cur("ss")
                        p.emit("vector",
                               lambda e, pt=pt, bka=bka: e.scalar_tensor_tensor(
                                   h4b[:, pt, :], ps[bka][:, 0:T],
                                   b1fav[:, l, pt:pt + 1], glt[:],
                                   ALU.add, ALU.mult),
                               wait=(("ss", gss),) + (bw(gh4b) if pt == 0 else ()),
                               inc=(("vv", 1),))
                        bankg[bka] = ("vv", p.cur("vv"))
                        bankg[bkg_] = ("vv", p.cur("vv"))
                    wguard[("wa", g % 2)] = ("mm", p.cur("mm"))
                    wguard[("wg", g % 2)] = ("mm", p.cur("mm"))
                gxrb = ("mm", p.cur("mm"))
                h4vv = p.cur("vv")
                for ot in range(KD):
                    wb = w2b[ot % 2]
                    wsem = "swf" + "AB"[ot % 2]
                    p.dma("sync", wsem, wb[:],
                          _rk(s_w2fb[l])[:, :, 128 * ot:128 * (ot + 1)],
                          wait=bw(wguard.get(("w2", ot % 2))))
                    ldw = (wsem, p.cur(wsem))
                    bk = ot % 4
                    for ct in range(KHID):
                        p.emit("tensor",
                               lambda e, ct=ct, wb=wb, bk=bk: e.matmul(
                                   ps[bk][:, 0:T], wb[:, ct, :], h4b[:, ct, :],
                                   start=(ct == 0), stop=(ct == KHID - 1)),
                               wait=(bw(bankg[bk]) +
                                     ((ldw, ("vv", h4vv)) if ct == 0 else ())),
                               inc=((("mm", 1),) if ct == KHID - 1 else ()))
                    mmc = p.cur("mm")
                    wguard[("w2", ot % 2)] = ("mm", mmc)
                    p.emit("vector",
                           lambda e, ot=ot, bk=bk: e.scalar_tensor_tensor(
                               xnc[:, ot, :], ps[bk][:, 0:T],
                               b2fv[:, l, ot:ot + 1], xmc[:, ot, 1:T + 1],
                               ALU.add, ALU.add),
                           wait=(("mm", mmc),) + ((("sp", None),) if ot == 0 else ()),
                           inc=(("vv", 1),))
                    bankg[bk] = ("vv", p.cur("vv"))
                gh4b = ("mm", p.cur("mm"))
                xnvv = p.cur("vv")
                p.dma("sync", "sp", _rk(dst)[:, :, t0:t0 + T], xnc[:],
                      wait=(("vv", xnvv),))
                gld = (("vv", xnvv), lnst["scast"])
            p.fence()
            p.flush_phase()

    fa_phase(0, s_xf0, s_xm[0])
    fb_phase(0, s_xm[0], s_xl[0])
    fa_phase(1, s_xl[0], s_xm[1])
    fb_phase(1, s_xm[1], s_xl[1])

    with contextlib.ExitStack() as st:
        Es = st.enter_context
        T = TCF
        fc = Es(sb("fc", [128, KD, T], F32))
        x2c = Es(sb("finx2", [128, KD, T], F32))
        cast16 = Es(sb("fic16", [128, KD, T], BF16))
        rows = Es(sb("firows", [1, 4, T], F32))
        oc = Es(sb("oc", [128, KD, T], F32))
        psrA = Es(pst("fipsrA", [128, 512], F32))
        psrB = Es(pst("fipsrB", [128, 512], F32))
        psrg = [None]
        for c in range(NCF if _on("fin") else 0):
            t0 = c * T
            p.dma("sync", "ax", fc[:], _rk(s_xl[1])[:, :, t0:t0 + T])
            p.dma("sync", "ax", x2c[:], _rk(s_x2)[:, :, t0:t0 + T])
            repmm = ln_stats(fc, psrA, psrB, rows, cast16, T, psrg,
                             wait_extra=(("ax", None),))
            p.emit("vector", lambda e: e.tensor_tensor(
                oc[:], fc[:],
                psrA[:, 0:T].unsqueeze(1).to_broadcast([128, KD, T]),
                ALU.subtract),
                wait=(("mm", repmm),), inc=(("vv", 1),))
            p.emit("vector", lambda e: e.tensor_tensor(
                oc[:], oc[:],
                psrB[:, 0:T].unsqueeze(1).to_broadcast([128, KD, T]),
                ALU.mult),
                inc=(("vv", 1),))
            p.emit("vector", lambda e: e.tensor_tensor(
                oc[:], oc[:], x2c[:], ALU.add),
                wait=(("ax", None),), inc=(("vv", 1),))
            ovv = p.cur("vv")
            psrg[0] = ("vv", ovv)
            p.emit("sync", lambda e, t0=t0: e.dma_start(
                out=_rk(outT)[:, :, t0:t0 + T], in_=oc[:]),
                wait=(("vv", ovv),), inc=(("out", 16),))
            p.fence()
        if _DBG is not None:
            srcmap = {"ln1": s_ln1, "x2": s_x2, "xf0": s_xf0,
                      "xm0": s_xm[0], "xl1": s_xl[0], "xm1": s_xm[1],
                      "xl2": s_xl[1]}
            if _DBG == "all":
                order = ["ln1", "x2", "xf0", "xm0", "xl1", "xm1", "xl2"]
                for i, nm in enumerate(order):
                    p.emit("sync", lambda e, i=i, s=srcmap[nm]: e.dma_start(
                        out=dbg[i * D:(i + 1) * D, :], in_=s[:]),
                        wait=(("sp", None),), inc=(("out", 16),))
                p.emit("gpsimd", lambda e: e.dma_start(
                    out=dbg[7 * D:7 * D + DIN, :], in_=s_ygb[:]),
                    wait=(("sp", None),), inc=(("out", 16),))
            elif _DBG in srcmap:
                p.emit("sync", lambda e: e.dma_start(
                    out=_rk(dbg)[:, 0:KD, :], in_=_rk(srcmap[_DBG])[:]),
                    wait=(("sp", None),), inc=(("out", 16),))
            elif _DBG == "ygb":
                p.emit("gpsimd", lambda e: e.dma_start(
                    out=_rk(dbg)[:, 0:KDI, :], in_=_rk(s_ygb)[:]),
                    wait=(("sp", None),), inc=(("out", 16),))
        p.emit("sync", lambda e: e.nop(),
               wait=(("out", p.cur("out")),))
        p.flush_phase()

    perstack.close()
    return nc


_NC_CACHE = {}


def kernel(**inputs):
    staged = _stage_inputs(inputs)
    x = np.asarray(inputs["L0_attn_x"])
    in_maps = []
    for ccore in range(8):
        m = dict(staged)
        m["xT"] = np.ascontiguousarray(x[ccore].T)
        in_maps.append(m)
    if "nc" not in _NC_CACHE:
        _NC_CACHE["nc"] = build_nc()
    nc = _NC_CACHE["nc"]
    res = run_bass_kernel_spmd(nc, in_maps, core_ids=list(range(8)))
    out = np.stack([np.asarray(r["outT"]).T for r in res.results], axis=0)
    if _DBG is not None:
        kernel._dbg = [np.asarray(r.get("dbg")) for r in res.results]
    return out
